# version 25
# baseline (speedup 1.0000x reference)
import contextlib
import sys
import ml_dtypes
import numpy as np
import concourse.bass as bass
import concourse.mybir as mybir
from concourse.bass_utils import run_bass_kernel_spmd

F32 = mybir.dt.float32
BF16 = mybir.dt.bfloat16
ALU = mybir.AluOpType
AF = mybir.ActivationFunctionType
AX = mybir.AxisListType

T, C, H, KD, VBD, VD, DH = 2048, 1024, 16, 64, 2048, 128, 4096
NCC = C // 128
NCH = DH // 128
NT = T // 128
EPS_LN = 1e-5
EPS_RMS = float(np.finfo(np.float32).eps)

ENGINES = ("tensor", "vector", "scalar", "gpsimd", "sync")
NQCLS = 8
_QUEUES = ("sync", "scalar", "gpsimd")
SEMS = ("TEN", "VEC", "SCA", "GPS") + tuple(
    f"{q[:2].upper()}{i}" for q in _QUEUES for i in range(NQCLS))


class Sched:

    def __init__(self):
        self.items = {e: [] for e in ENGINES}
        self.count = {s: 0 for s in SEMS}
        self.pending_bar = {e: None for e in ENGINES}

    def barrier(self):
        snap = [(s, v) for s, v in self.count.items() if v > 0]
        for e in ENGINES:
            self.pending_bar[e] = snap

    _SELF = {"vector": "VEC", "scalar": "SCA", "gpsimd": "GPS"}

    def emit(self, eng, fn, waits=(), inc=None, inc_by=1):
        w = dict()
        if self.pending_bar[eng]:
            for s, v in self.pending_bar[eng]:
                w[s] = max(w.get(s, 0), v)
            self.pending_bar[eng] = None
        for s, v in waits:
            w[s] = max(w.get(s, 0), v)
        selfsem = self._SELF.get(eng)
        if selfsem is not None and inc == selfsem and self.count[selfsem] > 0:
            w[selfsem] = max(w.get(selfsem, 0), self.count[selfsem])
        self.items[eng].append((fn, sorted(w.items()), inc, inc_by))
        if inc is not None:
            self.count[inc] += inc_by
            return self.count[inc]
        return None

    def install(self, block, sems):
        for engname in ENGINES:
            items = self.items[engname]
            if not items:
                continue

            def body(e, items=items):
                for fn, waits, inc, inc_by in items:
                    for s, v in waits:
                        e.wait_ge(sems[s], v)
                    ins = fn(e)
                    if inc is not None:
                        ins.then_inc(sems[inc], inc_by)

            getattr(block, engname)(body)



def _decay_tables():
    h = np.arange(H, dtype=np.float64)
    gamma = np.log(1.0 - 2.0 ** (-5.0 - h))
    t = np.arange(T, dtype=np.float64)
    eg = np.exp(gamma)[:, None]
    ssum = (1.0 - eg ** (t[None, :] + 1.0)) / (1.0 - eg)
    inv_scale = 1.0 / np.sqrt(ssum)
    et = np.exp(gamma[:, None] * t[None, :]) * inv_scale
    es = np.exp(-gamma[:, None] * t[None, :])
    et64 = np.repeat(et, KD, axis=0)
    es64 = np.repeat(es, KD, axis=0)
    etw = et64.astype(np.float32)
    esw = es64.astype(np.float32)
    etwl = (et64 - etw.astype(np.float64)).astype(ml_dtypes.bfloat16)
    eswl = (es64 - esw.astype(np.float64)).astype(ml_dtypes.bfloat16)
    return etw, esw, etwl, eswl


def _stage_coeffs(wstages):
    n_st = wstages.shape[0]
    f = np.arange(DH)
    a = np.empty((n_st, DH), np.float32)
    bP = np.empty((n_st, DH), np.float32)
    bT = np.empty((n_st, DH), np.float32)
    for k in range(n_st):
        s = 1 << k
        pr = (f // (2 * s)) * s + (f % s)
        bit = (f >> k) & 1
        a[k] = wstages[k, pr, bit, bit]
        bP[k] = wstages[k, pr, bit, 1 - bit]
        bT[k] = bP[k][f ^ s]
    return a, bP, bT


def _f_of_l2():
    c2 = np.arange(32)[:, None]
    p2 = np.arange(128)[None, :]
    return 128 * (p2 % 32) + 4 * c2 + (p2 // 32)


def _chain_specs(inputs):
    ua, ubP, ubT = _stage_coeffs(np.asarray(inputs["L0_ffn_up_weights"], np.float32))
    da, dbP, dbT = _stage_coeffs(np.asarray(inputs["L0_ffn_down_weights"], np.float32))
    l1 = lambda v: v.reshape(32, 128)
    l2 = lambda v: v[_f_of_l2()]
    spec = []
    spec.append((l1(ua[6]), l1(ubT[6]), 64))
    for k in range(5, -1, -1):
        spec.append((l1(ua[k]), l1(ubT[k]), 1 << k))
    spec.append((l2(ua[11]), l2(ubT[11]), 16))
    for k in range(10, 6, -1):
        spec.append((l2(ua[k]), l2(ubT[k]), 1 << (k - 7)))
    spec.append((l2(da[7]), l2(dbP[7]), 1))
    for k in range(8, 12):
        spec.append((l2(da[k]), l2(dbP[k]), 1 << (k - 7)))
    spec.append((l1(da[0]), l1(dbP[0]), 1))
    for k in range(1, 7):
        spec.append((l1(da[k]), l1(dbP[k]), 1 << k))
    alpha = np.stack([s[0] for s in spec]).astype(np.float32)
    beta = np.stack([s[1] for s in spec]).astype(np.float32)
    xors = [s[2] for s in spec]
    return alpha, beta, xors


CHAIN_LAYOUT = (("A_up", 7), ("B_up", 5), ("B_dnT", 5), ("A_dnT", 7))


def build_host_constants(inputs):
    import ml_dtypes
    alpha, beta, xors = _chain_specs(inputs)
    etw, esw, etwl, eswl = _decay_tables()
    xs_list = (1, 2, 4, 8, 16, 32, 64)
    xpat = np.zeros((7, 128, 128), np.float32)
    i = np.arange(128)
    for xi, s in enumerate(xs_list):
        xpat[xi, i, i ^ s] = 1.0
    causal = (np.arange(128)[None, :] >= np.arange(128)[:, None]).astype(np.float32)
    ind4 = np.zeros((128, 16), np.float32)
    for b in range(4):
        ind4[:, 4 * b + b] = 1.0
    ind8d = np.zeros((128, 32), np.float32)
    ind8m = np.zeros((128, 4 * 36), np.float32)
    for b in range(4):
        ind8d[:, 8 * b + b] = 1.0
        ind8m[:, 36 * b + 32 + b] = 1.0
    indT4 = np.zeros((4, 4 * 128), np.float32)
    for b in range(4):
        indT4[b, 128 * b:128 * (b + 1)] = 1.0
    b_qk = np.asarray(inputs["L0_attn_b_qk"], np.float32)
    consts = {
        "alpha": np.ascontiguousarray(alpha.transpose(2, 0, 1).reshape(128, -1)),
        "beta": np.ascontiguousarray(beta.transpose(2, 0, 1).reshape(128, -1)),
        "etw": etw, "esw": esw, "etwl": etwl, "eswl": eswl,
        "xpat": xpat, "causal": causal,
        "ident": np.eye(128, dtype=np.float32),
        "onesr": np.ones((1, 128), np.float32),
        "ind4f": ind4,
        "ind4b": ind4.astype(ml_dtypes.bfloat16),
        "ind8m": ind8m,
        "indT4": indT4.T.copy(),
        "bqkq": b_qk[:C].reshape(8, 128).T.copy(),
        "bqkk": b_qk[C:].reshape(8, 128).T.copy(),
        "bg2": np.asarray(inputs["L0_attn_b_g"], np.float32).reshape(16, 128).T.copy(),
        "upb_l2": np.asarray(inputs["L0_ffn_up_bias"], np.float32)[_f_of_l2()].T.copy(),
        "bv": np.asarray(inputs["L0_attn_b_v"], np.float32).reshape(1, VBD),
        "bo": np.asarray(inputs["L0_attn_b_o"], np.float32).reshape(1, C),
        "bexp": np.asarray(inputs["L0_ffn_b_exp"], np.float32).reshape(1, DH),
        "bcon": np.asarray(inputs["L0_ffn_b_con"], np.float32).reshape(1, C),
        "dbias": np.asarray(inputs["L0_ffn_down_bias"], np.float32).reshape(1, C),
    }
    ind_t = np.zeros((4, 512), np.float32)
    for b in range(4):
        ind_t[b, 128 * b:128 * (b + 1)] = 1.0
    consts["indT4"] = ind_t
    consts["ind8d"] = ind8d.astype(ml_dtypes.bfloat16)
    xpi = {s: i for i, s in enumerate(xs_list)}
    meta = {"xors": xors, "xpi": xpi}
    return consts, meta



MM_LINES = []


def build_program(meta, taps=(), stop_after=None, p2_groups=8, p2_heads=2, p2_wo=True, p2_parts="sadvrgp"):
    nc = bass.Bass("TRN2", target_bir_lowering=False,
                   detect_race_conditions=False)
    S = Sched()
    cnt = S.count

    def din(name, shape, dt=F32):
        return nc.dram_tensor(name, list(shape), dt, kind="ExternalInput").ap()

    d = {}
    d["x"] = din("x", (T, C))
    d["Wqk"] = din("Wqk", (C, 2 * C))
    d["Wv"] = din("Wv", (C, VBD))
    d["Wg"] = din("Wg", (C, VBD))
    d["Wo"] = din("Wo", (VBD, C))
    d["Wexp"] = din("Wexp", (C, DH))
    d["Wcon"] = din("Wcon", (DH, C))
    for name, shape in (
        ("alpha", (128, 24 * 32)), ("beta", (128, 24 * 32)),
        ("etw", (C, T)), ("esw", (C, T)),
        ("xpat", (7, 128, 128)), ("causal", (128, 128)), ("ident", (128, 128)),
        ("onesr", (1, 128)), ("ind4f", (128, 16)), ("indT4", (4, 512)),
        ("bqkq", (128, 8)), ("bqkk", (128, 8)), ("bg2", (128, 16)),
        ("upb_l2", (128, 32)), ("bv", (1, VBD)), ("bo", (1, C)),
        ("bexp", (1, DH)), ("bcon", (1, C)), ("dbias", (1, C)),
    ):
        d[name] = din(name, shape)
    d["etwl"] = din("etwl", (C, T), BF16)
    d["eswl"] = din("eswl", (C, T), BF16)
    d["ind4b"] = din("ind4b", (128, 16), BF16)
    d["ind8d"] = din("ind8d", (128, 32), BF16)
    d["ind8m"] = din("ind8m", (128, 4 * 36))
    y = nc.dram_tensor("y", [T, C], F32, kind="ExternalOutput").ap()

    scrA = nc.dram_tensor("scrA", [DH, C], F32, kind="Internal").ap()
    scrB = nc.dram_tensor("scrB", [DH, C + 1], F32, kind="Internal").ap()
    scrWE = nc.dram_tensor("scrWE", [NCH, 2, 128, NCC, 128], BF16,
                           kind="Internal").ap()
    scrV = nc.dram_tensor("scrV", [DH, C], F32, kind="Internal").ap()
    scrWC = nc.dram_tensor("scrWC", [DH, 2, C], BF16,
                           kind="Internal").ap()
    scrATT = nc.dram_tensor("scrATT", [T, C], F32, kind="Internal").ap()

    tap_d = {}
    for tname, tshape in dict(
        bprime=(128, 32),
        q0=(128, T), k0=(128, T), v0=(128, NT * 256), s0=(128, T),
        g0=(128, T), retn0=(128, T),
        den0=(4, 512), phi0=(4, 512), ret0=(128, T), gat0=(128, T),
        attn=(T, C), h3=(128, NCH * 512),
    ).items():
        if tname in taps:
            tap_d[tname] = nc.dram_tensor("tap_" + tname, list(tshape), F32,
                                          kind="ExternalOutput").ap()

    root = contextlib.ExitStack()

    def mk_sb(st):
        return (lambda name, shape, dt=F32:
                st.enter_context(nc.sbuf_tensor(name, list(shape), dt)))

    sb = mk_sb(root)
    psum = lambda name, shape: root.enter_context(
        nc.psum_tensor(name, list(shape), F32))

    xnh_p = [sb(f"xnh{c}", (128, T), BF16) for c in range(NCC)]
    xnl_p = [sb(f"xnl{c}", (128, T), BF16) for c in range(NCC)]
    k_ident = sb("k_ident", (128, 128))
    k_causal = sb("k_causal", (128, 128))
    k_onesr = sb("k_onesr", (1, 128))
    k_ind4f = sb("k_ind4f", (128, 16))
    k_ind4b = sb("k_ind4b", (128, 16), BF16)
    k_ind8d = sb("k_ind8d", (128, 32), BF16)
    k_ind8m = sb("k_ind8m", (128, 4 * 36))
    k_indT4 = sb("k_indT4", (4, 512))
    k_bqkq = sb("k_bqkq", (128, 8))
    k_bqkk = sb("k_bqkk", (128, 8))
    k_bg2 = sb("k_bg2", (128, 16))
    k_upb = sb("k_upb", (128, 32))
    k_bv = sb("k_bv", (1, VBD))
    k_bo = sb("k_bo", (1, C))
    k_bfin = sb("k_bfin", (1, C))
    bprime = sb("bprime", (128, 32))

    ps_a = psum("ps_a", (128, 512))
    ps_b = psum("ps_b", (128, 512))
    ps_ret = [psum(f"ps_ret{i}", (128, 512)) for i in range(4)]
    ps_c = psum("ps_c", (128, 512))
    ps_d = psum("ps_d", (128, 512))
    ps_den = ps_c
    CH_PS = [(ps_a, "cps_a"), (ps_b, "cps_b"), (ps_c, "cps_c"),
             (ps_d, "cps_d"), (ps_ret[0], "ret0"), (ps_ret[1], "ret1"),
             (ps_ret[2], "ret2"), (ps_ret[3], "ret3")]
    ps_ms = ps_d

    sems = {s: root.enter_context(nc.semaphore(s)) for s in SEMS}

    def mm(ps_ap, lhsT, rhs, start, stop, waits=(), inc=None):
        MM_LINES.append((sys._getframe(1).f_lineno,
                         lhsT.dtype == F32 and rhs.dtype == F32))
        return S.emit("tensor",
                      lambda e, o=ps_ap, a=lhsT, b=rhs, st=start, sp=stop:
                      e.matmul(o, a, b, start=st, stop=sp),
                      waits=waits, inc=inc)

    def tp(ps_ap, in_ap, id_ap, waits=(), inc="TEN"):
        MM_LINES.append((-sys._getframe(1).f_lineno, False))
        return S.emit("tensor",
                      lambda e, o=ps_ap, i=in_ap, idm=id_ap:
                      e.transpose(o, i, idm),
                      waits=waits, inc=inc)

    qseq = {q: 0 for q in _QUEUES}

    def dma(queue, out, in_, waits=(), accum=False):
        if accum:
            fn = lambda e, o=out, i=in_: e.dma_start(out=o, in_=i,
                                                     accum_op=ALU.add)
        else:
            fn = lambda e, o=out, i=in_: e.dma_start(out=o, in_=i)
        n = qseq[queue]
        qseq[queue] += 1
        sem = f"{queue[:2].upper()}{n % NQCLS}"
        prev = 16 * (n // NQCLS)
        S.emit(queue, fn, waits=list(waits) + [(sem, prev)],
               inc=sem, inc_by=16)
        return (sem, prev + 16)

    def qall(queue):
        return [(f"{queue[:2].upper()}{i}",
                 cnt[f"{queue[:2].upper()}{i}"]) for i in range(NQCLS)]

    def vop(fn, waits=()):
        return S.emit("vector", fn, waits=waits, inc="VEC")

    def sop(fn, waits=()):
        return S.emit("scalar", fn, waits=waits, inc="SCA")

    def allw():
        return [(s, v) for s, v in cnt.items() if v > 0]

    pfree = {}
    bnote = {}

    def _note(dic, name, sem, val):
        d_ = dic.setdefault(name, {})
        d_[sem] = max(d_.get(sem, 0), val)

    def _w(dic, name):
        return list(dic.get(name, {}).items())

    pnote = lambda name, sem, val: _note(pfree, name, sem, val)
    bmark = lambda name, sem, val: _note(bnote, name, sem, val)
    pw = lambda name: _w(pfree, name)
    bw = lambda name: _w(bnote, name)

    def bclr(name):
        bnote.pop(name, None)

    def finish():
        S.emit("gpsimd", lambda e: e.nop(),
               waits=qall("gpsimd") + qall("sync") + qall("scalar"))
        with nc.Block() as block:
            S.install(block, sems)
        root.close()
        return nc

    for tile_ap, src in (
        (k_ident[:, :], d["ident"][:, :]), (k_causal[:, :], d["causal"][:, :]),
        (k_ind8d[:, :], d["ind8d"][:, :]), (k_ind8m[:, :], d["ind8m"][:, :]),
        (k_onesr[:, :], d["onesr"][:, :]), (k_ind4f[:, :], d["ind4f"][:, :]),
        (k_ind4b[:, :], d["ind4b"][:, :]), (k_indT4[:, :], d["indT4"][:, :]),
        (k_bqkq[:, :], d["bqkq"][:, :]), (k_bqkk[:, :], d["bqkk"][:, :]),
        (k_bg2[:, :], d["bg2"][:, :]), (k_upb[:, :], d["upb_l2"][:, :]),
        (k_bv[:, :], d["bv"][:, :]), (k_bo[:, :], d["bo"][:, :]),
        (k_bfin[:, :], d["bcon"][:, :]),
    ):
        dma("sync", tile_ap, src)
    CONSTW = qall("sync")

    prep = contextlib.ExitStack()
    psb = mk_sb(prep)
    wrow = psb("wrow", (128, DH))
    wrow2 = psb("wrow2", (128, DH))
    wtile = psb("wtile", (128, C + 1))
    wtile2 = psb("wtile2", (128, C + 1))
    wtile3 = psb("wtile3", (128, C + 1))
    evt3 = psb("evt3", (128, 1025))
    chainL = psb("chainL", (128, 32 * 128))
    stile8 = psb("stile8", (128, 1024))
    xpat_t = [psb(f"xpat{i}", (128, 128)) for i in range(7)]
    alpha_t = psb("alpha_t", (128, 24 * 32))
    beta_t = psb("beta_t", (128, 24 * 32))
    evt = psb("evt", (128, 1025))
    evt2 = psb("evt2", (128, 1025))
    whl0 = psb("whl0", (128, 2048), BF16)
    whl1 = psb("whl1", (128, 2048), BF16)
    k_brow = psb("k_brow", (1, DH))
    x_tok = psb("x_tok", (128, C))
    x_tok2 = wtile3[:, 0:C]
    x_sq = evt3[:, 0:1024]
    xn_st = evt2[:, 0:1024]
    ln_s = psb("ln_s", (128, 16))

    dma("sync", k_brow[:, :], d["bexp"][:, :])
    for i in range(7):
        dma("sync", xpat_t[i][:, :], d["xpat"][i, :, :])
    dma("sync", alpha_t[:, :], d["alpha"][:, :])
    dma("sync", beta_t[:, :], d["beta"][:, :])
    dma("sync", evt[0:1, 0:C], d["dbias"][:, :])
    CONSTW2 = qall("sync")
    BFIN_V = vop(lambda e: e.tensor_add(k_bfin[0:1, :], k_bfin[0:1, :],
                                        evt[0:1, 0:C]),
                 waits=[*CONSTW2])

    p0st = {"tp": 0}
    xcons = {0: (0, 0), 1: (0, 0)}

    def emit_p0_tile(ti):
        xt = x_tok if ti % 2 == 0 else x_tok2
        o = 8 * (ti % 2)
        pv, psc = xcons[ti % 2]
        ld = dma("sync" if ti % 2 == 0 else "scalar", xt[:, :],
                 d["x"][128 * ti:128 * (ti + 1), :],
                 waits=[("VEC", pv), ("SCA", psc)])
        SQ = sop(lambda e: e.activation(x_sq[:, :], xt[:, :], AF.Square,
                                        accum_out=ln_s[:, o + 1:o + 2]),
                 waits=[ld])
        vop(lambda e: e.tensor_reduce(ln_s[:, o:o + 1], xt[:, :], AX.X,
                                      ALU.add),
            waits=[ld])
        vop(lambda e: e.tensor_scalar(ln_s[:, o + 2:o + 3], ln_s[:, o:o + 1],
                                      1.0 / C, None, ALU.mult))
        vop(lambda e: e.tensor_mul(ln_s[:, o + 3:o + 4], ln_s[:, o + 2:o + 3],
                                   ln_s[:, o + 2:o + 3]))
        vop(lambda e: e.scalar_tensor_tensor(ln_s[:, o + 4:o + 5],
                                             ln_s[:, o + 1:o + 2],
                                             1.0 / C, ln_s[:, o + 3:o + 4],
                                             ALU.mult, ALU.subtract),
            waits=[("SCA", SQ)])
        vop(lambda e: e.tensor_scalar(ln_s[:, o + 4:o + 5],
                                      ln_s[:, o + 4:o + 5], EPS_LN,
                                      None, ALU.add))
        VE = cnt["VEC"]
        SQ2 = sop(lambda e: e.activation(ln_s[:, o + 5:o + 6],
                                         ln_s[:, o + 4:o + 5], AF.Sqrt),
                  waits=[("VEC", VE)])
        vop(lambda e: e.reciprocal(ln_s[:, o + 6:o + 7],
                                   ln_s[:, o + 5:o + 6]),
            waits=[("SCA", SQ2)])
        vop(lambda e: e.tensor_mul(ln_s[:, o + 7:o + 8], ln_s[:, o + 6:o + 7],
                                   ln_s[:, o + 6:o + 7]))
        vop(lambda e: e.tensor_mul(ln_s[:, o + 7:o + 8], ln_s[:, o + 7:o + 8],
                                   ln_s[:, o + 4:o + 5]))
        vop(lambda e: e.tensor_scalar(ln_s[:, o + 7:o + 8],
                                      ln_s[:, o + 7:o + 8], -0.5, 1.5,
                                      ALU.mult, ALU.add))
        vop(lambda e: e.tensor_mul(ln_s[:, o + 7:o + 8], ln_s[:, o + 7:o + 8],
                                   ln_s[:, o + 6:o + 7]))
        XN_V = vop(lambda e: e.tensor_scalar(xn_st[:, 0:C], xt[:, :],
                                             ln_s[:, o + 2:o + 3],
                                             ln_s[:, o + 7:o + 8],
                                             ALU.subtract, ALU.mult),
                   waits=[("TEN", p0st["tp"])])
        for c in range(NCC):
            pst, pn = CH_PS[c % 8]
            t_mm = tp(pst[:, 0:128], xn_st[:, 128 * c:128 * (c + 1)],
                      k_ident[:, :],
                      waits=[("VEC", XN_V)] + pw(pn) +
                      ([*CONSTW] if ti == 0 and c == 0 else []))
            p0st["tp"] = t_mm
            vop(lambda e, c=c, ti=ti, pst=pst: e.tensor_copy(
                xnh_p[c][:, 128 * ti:128 * (ti + 1)], pst[:, 0:128]),
                waits=[("TEN", t_mm)])
            lo_ = vop(lambda e, c=c, ti=ti, pst=pst: e.scalar_tensor_tensor(
                xnl_p[c][:, 128 * ti:128 * (ti + 1)],
                xnh_p[c][:, 128 * ti:128 * (ti + 1)], -1.0,
                pst[:, 0:128], ALU.mult, ALU.add))
            pnote(pn, "VEC", lo_)
        xcons[ti % 2] = (XN_V, SQ)

    w1_last = {(bf, ch): ("VEC", 0) for bf in range(2) for ch in range(4)}

    def emit_w1_rc(rc):
        wr_ = wrow if rc % 2 == 0 else wrow2
        ld_ch = []
        for ch in range(4):
            ld_ch.append(dma("sync" if ch % 2 == 0 else "scalar",
                             wr_[:, 1024 * ch:1024 * (ch + 1)],
                             d["Wexp"][128 * rc:128 * (rc + 1),
                                       1024 * ch:1024 * (ch + 1)],
                             waits=[w1_last[(rc % 2, ch)]]))
        for cc in range(NCH):
            pst, pn = CH_PS[cc % 8]
            t_mm = tp(pst[:, 0:128], wr_[:, 128 * cc:128 * (cc + 1)],
                      k_ident[:, :],
                      waits=[ld_ch[cc // 8]] + pw(pn))
            w_extra = qall("gpsimd") if cc % 8 == 0 else []
            ev_w = vop(lambda e, cc=cc, pst=pst: e.tensor_copy(
                evt[:, 128 * (cc % 8):128 * (cc % 8 + 1)], pst[:, 0:128]),
                waits=[("TEN", t_mm)] + w_extra)
            pnote(pn, "VEC", ev_w)
            if cc % 8 == 7:
                base = cc - 7
                w1_last[(rc % 2, cc // 8)] = ("TEN", t_mm)
                dst = scrA[:, 128 * rc:128 * (rc + 1)] \
                    .rearrange("(a p) q -> p a q", p=128)[:, base:cc + 1, :]
                dma("gpsimd", dst,
                    evt[:, 0:1024].rearrange("p (a q) -> p a q", q=128),
                    waits=[("VEC", cnt["VEC"])])

    for ti in range(NT):
        emit_p0_tile(ti)
        if ti % 2 == 1:
            emit_w1_rc(ti // 2)
    P0_V = cnt["VEC"]
    bmark("wt2", "VEC", P0_V)
    bmark("wt2", "SCA", cnt["SCA"])

    if stop_after == "p0":
        prep.close()
        return finish()

    W1_DGP = qall("gpsimd")

    xors = meta["xors"]
    xpi = meta["xpi"]
    n_base = {}
    acc = 0
    for nm, st_n in CHAIN_LAYOUT:
        n_base[nm] = acc
        acc += st_n

    chst = {"upd": [0] * 32, "t": [0] * 8, "rd": [0] * 32}

    def build_chain(name):
        base, steps = n_base[name], dict(CHAIN_LAYOUT)[name]
        for si in range(steps):
            for c in range(32):
                n = base + si
                xp = xpat_t[xpi[xors[n]]]
                al = alpha_t[:, 32 * n + c:32 * n + c + 1]
                be = beta_t[:, 32 * n + c:32 * n + c + 1]
                cl = chainL[:, 128 * c:128 * (c + 1)]
                tsl = stile8[:, 128 * (c % 8):128 * (c % 8 + 1)]
                prev_upd = chst["upd"][c]
                if si == 0:
                    s1 = sop(lambda e, al=al, cl=cl: e.activation(
                        cl, k_ident[:, :], AF.Copy, bias=0.0, scale=al),
                        waits=[*CONSTW2, ("VEC", prev_upd),
                               ("TEN", chst["rd"][c])])
                    v1 = vop(lambda e, be=be, xp=xp, cl=cl:
                             e.scalar_tensor_tensor(cl, xp[:, :], be, cl,
                                                    ALU.mult, ALU.add),
                             waits=[("SCA", s1)])
                    chst["upd"][c] = v1
                else:
                    pst, pn = CH_PS[c % 8]
                    t_mm = mm(pst[:, 0:128], xp[:, :], cl, True, True,
                              waits=[("VEC", prev_upd)] + pw(pn), inc="TEN")
                    s1 = sop(lambda e, al=al, cl=cl, tsl=tsl: e.activation(
                        tsl, cl, AF.Copy, bias=0.0, scale=al),
                        waits=[("VEC", prev_upd),
                               ("VEC", chst["t"][c % 8])])
                    v1 = vop(lambda e, be=be, cl=cl, tsl=tsl, pst=pst:
                             e.scalar_tensor_tensor(cl, pst[:, 0:128], be,
                                                    tsl, ALU.mult,
                                                    ALU.add),
                             waits=[("TEN", t_mm), ("SCA", s1)])
                    chst["upd"][c] = v1
                    pnote(pn, "VEC", v1)
                    chst["t"][c % 8] = v1

    build_chain("A_up")
    for c in range(32):
        wt = [wtile, wtile2, wtile3][c % 3]
        ev_ = [evt, evt2, evt3][c % 3]
        wtn = f"wt{c % 3}"
        evn = f"ev{c % 3}"
        ldw = dma("scalar" if c % 2 == 0 else "sync",
                  wt[:, 0:C], scrA[128 * c:128 * (c + 1), :],
                  waits=W1_DGP + bw(wtn))
        bclr(wtn)
        t_bt = tp(ps_c[:, 0:1], k_brow[0:1, 128 * c:128 * (c + 1)],
                  k_onesr[0:1, 0:1],
                  waits=[*CONSTW] + pw("cps_c"))
        bv_ = vop(lambda e, wt=wt: e.tensor_copy(wt[:, C:C + 1], ps_c[:, 0:1]),
                  waits=[("TEN", t_bt), ldw])
        pnote("cps_c", "VEC", bv_)
        lhs = chainL[:, 128 * c:128 * (c + 1)]
        m1 = mm(ps_a[:, 0:512], lhs, wt[:, 0:512], True, True,
                waits=[("VEC", bv_)] + pw("cps_a"), inc="TEN")
        m2 = mm(ps_b[:, 0:512], lhs, wt[:, 512:1024], True, True,
                waits=[("VEC", bv_)] + pw("cps_b"), inc="TEN")
        m3 = mm(ps_d[:, 0:1], lhs, wt[:, C:C + 1], True, True,
                waits=[("VEC", bv_)] + pw("cps_d"), inc="TEN")
        chst["rd"][c] = m3
        bmark(wtn, "TEN", m3)
        e1 = vop(lambda e, ev_=ev_: e.tensor_copy(ev_[:, 0:512],
                                                  ps_a[:, 0:512]),
                 waits=[("TEN", m1)] + bw(evn))
        pnote("cps_a", "VEC", e1)
        e2 = vop(lambda e, ev_=ev_: e.tensor_copy(ev_[:, 512:1024],
                                                  ps_b[:, 0:512]),
                 waits=[("TEN", m2)])
        pnote("cps_b", "VEC", e2)
        e3 = vop(lambda e, ev_=ev_: e.tensor_copy(ev_[:, 1024:1025],
                                                  ps_d[:, 0:1]),
                 waits=[("TEN", m3)])
        pnote("cps_d", "VEC", e3)
        bclr(evn)
        od = dma("gpsimd", scrB[128 * c:128 * (c + 1), 0:C + 1],
                 ev_[:, 0:1025], waits=[("VEC", e3)])
        bmark(evn, od[0], od[1])
    F1_DGP = qall("gpsimd")

    build_chain("B_up")
    for c2 in range(32):
        wt = [wtile, wtile2, wtile3][c2 % 3]
        ev_ = [evt, evt2, evt3][c2 % 3]
        wtn = f"wt{c2 % 3}"
        evn = f"ev{c2 % 3}"
        src = scrB.rearrange("(hi gg) q -> hi gg q", gg=128)
        src = src[:, 4 * c2:4 * c2 + 4, :].rearrange("hi g q -> g hi q")
        ldw = dma("scalar" if c2 % 2 == 0 else "sync",
                  wt[:, 0:C + 1], src,
                  waits=F1_DGP + bw(wtn))
        bclr(wtn)
        lhs = chainL[:, 128 * c2:128 * (c2 + 1)]
        m1 = mm(ps_a[:, 0:512], lhs, wt[:, 0:512], True, True,
                waits=[ldw] + pw("cps_a"), inc="TEN")
        m2 = mm(ps_b[:, 0:512], lhs, wt[:, 512:1024], True, True,
                waits=[ldw] + pw("cps_b"), inc="TEN")
        m3 = mm(ps_d[:, 0:1], lhs, wt[:, C:C + 1], True, True,
                waits=[ldw] + pw("cps_d"), inc="TEN")
        chst["rd"][c2] = m3
        e1 = vop(lambda e, ev_=ev_: e.tensor_copy(ev_[:, 0:512],
                                                  ps_a[:, 0:512]),
                 waits=[("TEN", m1)] + bw(evn))
        bclr(evn)
        pnote("cps_a", "VEC", e1)
        e2 = vop(lambda e, ev_=ev_: e.tensor_copy(ev_[:, 512:1024],
                                                  ps_b[:, 0:512]),
                 waits=[("TEN", m2)])
        pnote("cps_b", "VEC", e2)
        e3 = vop(lambda e, c2=c2: e.tensor_copy(bprime[:, c2:c2 + 1],
                                                ps_d[:, 0:1]),
                 waits=[("TEN", m3)])
        pnote("cps_d", "VEC", e3)
        tv = None
        whl = whl0 if c2 % 2 == 0 else whl1
        for rc in range(NCC):
            pst, pn = CH_PS[(rc + 4 * (c2 % 2)) % 8]
            t_mm = tp(pst[:, 0:128], ev_[:, 128 * rc:128 * (rc + 1)],
                      k_ident[:, :],
                      waits=[("VEC", e2)] + pw(pn))
            bmark(evn, "TEN", t_mm)
            vop(lambda e, rc=rc, whl=whl, pst=pst: e.tensor_copy(
                whl[:, 128 * rc:128 * (rc + 1)], pst[:, 0:128]),
                waits=[("TEN", t_mm)] +
                (bw(f"whl{c2 % 2}") if rc == 0 else []))
            tv = vop(lambda e, rc=rc, whl=whl, pst=pst:
                     e.scalar_tensor_tensor(
                         whl[:, 1024 + 128 * rc:1024 + 128 * (rc + 1)],
                         whl[:, 128 * rc:128 * (rc + 1)], -1.0,
                         pst[:, 0:128], ALU.mult, ALU.add))
            pnote(pn, "VEC", tv)
        bclr(f"whl{c2 % 2}")
        od = dma("gpsimd",
                 scrWE[c2].rearrange("hl p a q -> p hl a q"),
                 whl[:, 0:2048].rearrange("p (hl a q) -> p hl a q",
                                          hl=2, a=NCC),
                 waits=[("VEC", tv)])
        bmark(f"whl{c2 % 2}", od[0], od[1])
    BPRIME_V = vop(lambda e: e.tensor_add(bprime[:, :], bprime[:, :],
                                          k_upb[:, :]),
                   waits=[*CONSTW])
    WE_DGP = qall("gpsimd")
    if "bprime" in tap_d:
        dma("gpsimd", tap_d["bprime"][:, :], bprime[:, :],
            waits=[("VEC", BPRIME_V)])

    build_chain("B_dnT")

    def down_fold(src_of, dst_of, split):
        for c in range(32):
            wt = [wtile, wtile2, wtile3][c % 3]
            ev_ = [evt, evt2, evt3][c % 3]
            whl = whl0 if c % 2 == 0 else whl1
            wtn = f"wt{c % 3}"
            evn = f"ev{c % 3}"
            ldw = dma("scalar" if c % 2 == 0 else "sync",
                      wt[:, 0:C], src_of(c), waits=bw(wtn))
            bclr(wtn)
            lhs = chainL[:, 128 * c:128 * (c + 1)]
            m1 = mm(ps_a[:, 0:512], lhs, wt[:, 0:512], True, True,
                    waits=[ldw] + pw("cps_a"), inc="TEN")
            m2 = mm(ps_b[:, 0:512], lhs, wt[:, 512:1024], True, True,
                    waits=[ldw] + pw("cps_b"), inc="TEN")
            chst["rd"][c] = m2
            bmark(wtn, "TEN", m2)
            if not split:
                e1 = vop(lambda e, ev_=ev_: e.tensor_copy(ev_[:, 0:512],
                                                          ps_a[:, 0:512]),
                         waits=[("TEN", m1)] + bw(evn))
                bclr(evn)
                pnote("cps_a", "VEC", e1)
                e2 = vop(lambda e, ev_=ev_: e.tensor_copy(ev_[:, 512:1024],
                                                          ps_b[:, 0:512]),
                         waits=[("TEN", m2)])
                pnote("cps_b", "VEC", e2)
                od = dma("gpsimd", dst_of(c), ev_[:, 0:1024],
                         waits=[("VEC", e2)])
                bmark(evn, od[0], od[1])
            else:
                vop(lambda e, whl=whl: e.tensor_copy(whl[:, 0:512],
                                                     ps_a[:, 0:512]),
                    waits=[("TEN", m1)] + bw(f"whl{c % 2}"))
                bclr(f"whl{c % 2}")
                v1 = vop(lambda e, whl=whl: e.scalar_tensor_tensor(
                    whl[:, 1024:1536], whl[:, 0:512], -1.0,
                    ps_a[:, 0:512], ALU.mult, ALU.add))
                pnote("cps_a", "VEC", v1)
                vop(lambda e, whl=whl: e.tensor_copy(whl[:, 512:1024],
                                                     ps_b[:, 0:512]),
                    waits=[("TEN", m2)])
                v2 = vop(lambda e, whl=whl: e.scalar_tensor_tensor(
                    whl[:, 1536:2048], whl[:, 512:1024], -1.0,
                    ps_b[:, 0:512], ALU.mult, ALU.add))
                pnote("cps_b", "VEC", v2)
                od = dma("gpsimd", dst_of(c),
                         whl[:, 0:2048].rearrange("p (hl q) -> p hl q", hl=2),
                         waits=[("VEC", v2)])
                bmark(f"whl{c % 2}", od[0], od[1])

    def l2_rows(t, c):
        s_ = t.rearrange("(hi gg) q -> hi gg q", gg=128)
        return s_[:, 4 * c:4 * c + 4, :].rearrange("hi g q -> g hi q")

    down_fold(lambda c: l2_rows(d["Wcon"], c), lambda c: l2_rows(scrV, c),
              False)
    V1_DGP = qall("gpsimd")

    build_chain("A_dnT")
    for s_, v_ in V1_DGP:
        _note(bnote, "wt0", s_, v_)
        _note(bnote, "wt1", s_, v_)
        _note(bnote, "wt2", s_, v_)
    down_fold(lambda c: scrV[128 * c:128 * (c + 1), :],
              lambda c: scrWC[128 * c:128 * (c + 1), :, :], True)
    WC_DGP = qall("gpsimd")
    S.barrier()
    prep.close()
    if stop_after == "p1":
        return finish()

    attn = contextlib.ExitStack()
    asb = mk_sb(attn)
    qh_sb = asb("qh_sb", (128, T), BF16)
    ql_sb = asb("ql_sb", (128, T), BF16)
    kh_sb = asb("kh_sb", (128, T), BF16)
    kl_sb = asb("kl_sb", (128, T), BF16)
    v_sb = asb("v_sb", (128, NT * 256))
    dec_hi = asb("dec_hi", (128, 4 * 512))
    dec_lo = asb("dec_lo", (128, 4 * 512), BF16)
    wst = asb("wst", (128, NCC * 128))
    wqh = asb("wqh", (128, NCC * 128), BF16)
    wql = asb("wql", (128, NCC * 128), BF16)
    wkh = asb("wkh", (128, NCC * 128), BF16)
    wkl = asb("wkl", (128, NCC * 128), BF16)
    wvh = asb("wvh", (128, NCC * 256), BF16)
    wvl = asb("wvl", (128, NCC * 256), BF16)
    woh, wol = wvh, wvl
    s_sb = [asb(f"s_sb{i}", (128, T)) for i in range(2)]
    sabs = asb("sabs", (128, T), BF16)
    ret_sb = asb("ret_sb", (128, T))
    g_sb = asb("g_sb", (128, T))
    gath = asb("gath", (128, 2 * T), BF16)
    gatl = asb("gatl", (128, 2 * T), BF16)

    P2_PS = [(ps_a, "ps_a"), (ps_b, "ps_b"), (ps_den, "den"),
             (ps_d, "ps_d"), (ps_ret[0], "ret0"), (ps_ret[1], "ret1"),
             (ps_ret[2], "ret2"), (ps_ret[3], "ret3")]
    att_prev_dgp = []
    for grp in range(p2_groups):
        h0 = 2 * grp

        dec_ld = {}

        def load_dec(which, tb):
            wi = 0 if which == "q" else 1
            s4 = 2 * wi + tb % 2
            hname, lname = ("etw", "etwl") if wi == 0 else ("esw", "eswl")
            w_ = bw(f"dec{s4}")
            bclr(f"dec{s4}")
            dec_ld[(which, tb)] = [
                dma("sync" if wi == 0 else "scalar",
                    dec_hi[:, 512 * s4:512 * (s4 + 1)],
                    d[hname][128 * grp:128 * (grp + 1),
                             512 * tb:512 * (tb + 1)], waits=w_),
                dma("scalar" if wi == 0 else "sync",
                    dec_lo[:, 512 * s4:512 * (s4 + 1)],
                    d[lname][128 * grp:128 * (grp + 1),
                             512 * tb:512 * (tb + 1)], waits=w_)]

        for tb0 in range(2):
            load_dec("q", tb0)
            load_dec("k", tb0)
        spl = {}
        for nm_, hib, lob, dsrc, cw in (
            ("wq", wqh, wql, d["Wqk"][:, 128 * grp:128 * (grp + 1)], 128),
            ("wk", wkh, wkl,
             d["Wqk"][:, C + 128 * grp:C + 128 * (grp + 1)], 128),
            ("wv", wvh, wvl, d["Wv"][:, 256 * grp:256 * (grp + 1)], 256),
        ):
            kc_half = 1024 // cw
            for half in range((cw * NCC) // 1024):
                off = 1024 * half
                wl_ = []
                lw = bw("wst")
                bclr("wst")
                for kc in range(kc_half):
                    kk = half * kc_half + kc
                    wl_.append(dma("sync" if kc % 2 == 0 else "scalar",
                                   wst[:, cw * kc:cw * (kc + 1)],
                                   dsrc[128 * kk:128 * (kk + 1), :],
                                   waits=lw))
                sh = sop(lambda e, h_=hib, off=off: e.copy(
                    h_[:, off:off + 1024], wst[:, 0:1024]),
                    waits=list(wl_) + bw(nm_ + "h"))
                sv = vop(lambda e, h_=hib, l_=lob, off=off:
                         e.scalar_tensor_tensor(
                             l_[:, off:off + 1024], h_[:, off:off + 1024],
                             -1.0, wst[:, 0:1024], ALU.mult, ALU.add),
                         waits=[("SCA", sh)])
                bmark("wst", "VEC", sv)
            bclr(nm_ + "h")
            spl[nm_] = sv

        qk_i = 0
        for tb in range(4):
            for which in ("q", "k"):
                whi = wqh if which == "q" else wkh
                wlo = wql if which == "q" else wkl
                pst, pn = P2_PS[qk_i % 8]
                qk_i += 1
                dh_ = qh_sb if which == "q" else kh_sb
                dl_ = ql_sb if which == "q" else kl_sb
                bias = k_bqkq[:, grp:grp + 1] if which == "q" \
                    else k_bqkk[:, grp:grp + 1]
                wi = 0 if which == "q" else 1
                s4 = 2 * wi + tb % 2
                first = True
                for hw, hx in ((0, 0), (0, 1), (1, 0)):
                    wt_ = whi if hw == 0 else wlo
                    xs_ = xnh_p if hx == 0 else xnl_p
                    for kc in range(NCC):
                        mm(pst[:, 0:512], wt_[:, 128 * kc:128 * (kc + 1)],
                           xs_[kc][:, 512 * tb:512 * (tb + 1)],
                           first, (hw, hx, kc) == (1, 0, NCC - 1),
                           waits=[("VEC", spl[
                               "wq" if which == "q" else "wk"])] +
                           (pw(pn) if first else []),
                           inc="TEN" if (hw, hx, kc) == (1, 0, NCC - 1)
                           else None)
                        first = False
                tmm = cnt["TEN"]
                
                bmark("wqh" if which == "q" else "wkh", "TEN", tmm)
                qf_st = s_sb[1][:, 1536:2048]
                sc2 = s_sb[1][:, 1024:1536]
                hi_ap = dec_hi[:, 512 * s4:512 * (s4 + 1)]
                lo_ap = dec_lo[:, 512 * s4:512 * (s4 + 1)]
                e0 = vop(lambda e, bias=bias, pst=pst, qf_st=qf_st:
                         e.tensor_scalar(qf_st[:, 0:512], pst[:, 0:512],
                                         bias, None, ALU.add),
                         waits=[("TEN", tmm)] + dec_ld[(which, tb)])
                pnote(pn, "VEC", e0)
                vop(lambda e, qf_st=qf_st, sc2=sc2, lo_ap=lo_ap:
                    e.tensor_mul(sc2[:, 0:512], qf_st[:, 0:512], lo_ap))
                vop(lambda e, qf_st=qf_st, hi_ap=hi_ap:
                    e.tensor_mul(qf_st[:, 0:512], qf_st[:, 0:512], hi_ap))
                e0d = vop(lambda e, qf_st=qf_st, sc2=sc2:
                          e.tensor_add(qf_st[:, 0:512], qf_st[:, 0:512],
                                       sc2[:, 0:512]))
                bmark(f"dec{s4}", "VEC", e0d)
                vop(lambda e, dh_=dh_, tb=tb, qf_st=qf_st: e.tensor_copy(
                    dh_[:, 512 * tb:512 * (tb + 1)], qf_st[:, 0:512]))
                vop(lambda e, dh_=dh_, dl_=dl_, tb=tb, qf_st=qf_st:
                    e.scalar_tensor_tensor(
                        dl_[:, 512 * tb:512 * (tb + 1)],
                        dh_[:, 512 * tb:512 * (tb + 1)], -1.0,
                        qf_st[:, 0:512], ALU.mult, ALU.add))
                if tb + 2 < 4:
                    load_dec(which, tb + 2)
        for sl in range(NT):
            pst, pn = P2_PS[sl % 8]
            first = True
            for hx, hw in ((0, 0), (0, 1), (1, 0)):
                xs_ = xnh_p if hx == 0 else xnl_p
                wt_ = wvh if hw == 0 else wvl
                for kc in range(NCC):
                    mm(pst[:, 0:256], xs_[kc][:, 128 * sl:128 * (sl + 1)],
                       wt_[:, 256 * kc:256 * (kc + 1)],
                       first, False,
                       waits=[("VEC", spl["wv"])] +
                       (pw(pn) if first else []))
                    first = False
            bmm = mm(pst[:, 0:256], k_onesr[0:1, :],
                     k_bv[0:1, 256 * grp:256 * (grp + 1)], False, True,
                     waits=[*CONSTW], inc="TEN")
            bmark("wvh", "TEN", bmm)
            ev = vop(lambda e, sl=sl, pst=pst: e.tensor_copy(
                v_sb[:, 256 * sl:256 * (sl + 1)], pst[:, 0:256]),
                waits=[("TEN", bmm)])
            pnote(pn, "VEC", ev)
        QKV_V = cnt["VEC"]

        gspl = []
        for hl2 in range(2):
            h2 = h0 + hl2
            g_hi = wqh if hl2 == 0 else wkh
            g_lo = wql if hl2 == 0 else wkl
            g_rn = "wqh" if hl2 == 0 else "wkh"
            wg_l = []
            wgw = bw("wst")
            bclr("wst")
            for kc in range(NCC):
                wg_l.append(dma("sync" if kc % 2 == 0 else "scalar",
                                wst[:, 128 * kc:128 * (kc + 1)],
                                d["Wg"][128 * kc:128 * (kc + 1),
                                        128 * h2:128 * (h2 + 1)],
                                waits=wgw))
            gsh = sop(lambda e, g_hi=g_hi: e.copy(g_hi[:, 0:1024],
                                                  wst[:, 0:1024]),
                      waits=list(wg_l) + bw(g_rn))
            gsv = vop(lambda e, g_hi=g_hi, g_lo=g_lo:
                      e.scalar_tensor_tensor(
                          g_lo[:, 0:1024], g_hi[:, 0:1024], -1.0,
                          wst[:, 0:1024], ALU.mult, ALU.add),
                      waits=[("SCA", gsh)])
            bclr(g_rn)
            bmark("wst", "VEC", gsv)
            gspl.append((g_hi, g_lo, g_rn, gsv))

        for hl2 in range(2):
            wow = bw("wst")
            bclr("wst")
            wo_ld = [dma("sync", wst[:, 0:512],
                         d["Wo"][128 * (h0 + hl2):128 * (h0 + hl2 + 1),
                                 0:512], waits=wow),
                     dma("scalar", wst[:, 512:1024],
                         d["Wo"][128 * (h0 + hl2):128 * (h0 + hl2 + 1),
                                 512:1024], waits=wow)]
            osh = sop(lambda e, hl2=hl2: e.copy(
                woh[:, C * hl2:C * (hl2 + 1)], wst[:, 0:1024]),
                waits=wo_ld + (bw("wvh") if hl2 == 0 else []))
            osv = vop(lambda e, hl2=hl2: e.scalar_tensor_tensor(
                wol[:, C * hl2:C * (hl2 + 1)], woh[:, C * hl2:C * (hl2 + 1)],
                -1.0, wst[:, 0:1024], ALU.mult, ALU.add),
                waits=[("SCA", osh)])
            bmark("wst", "VEC", osv)
        bclr("wvh")
        gate_done = [[None] * 4, [None] * 4]

        for hl in range(p2_heads):
            h = h0 + hl
            qof = 64 * hl
            den_q = []
            den_done = {}
            n_den = 0

            def flush_den(j_end):
                nonlocal n_den, last_den
                while den_q and den_q[0][0] < j_end:
                    jq, b, lo, hi, ab = den_q.pop(0)
                    last_den = mm(ps_den[0:8, lo - 512 * b:512],
                                  k_ind8d[:, 8 * b:8 * b + 8], sabs[:, lo:hi],
                                  n_den == 0, (jq, b) == (NT - 1, 3),
                                  waits=[("SCA", ab)] +
                                  (pw("den") if n_den == 0 else []),
                                  inc="TEN")
                    den_done[b] = last_den
                    n_den += 1

            sv_q = []
            sv_done = {}

            def flush_sv(j_end):
                nonlocal last_sv
                while sv_q and sv_q[0][0] < j_end:
                    jq, b, lo, hi, ev1e_, ev1, buf_ = sv_q.pop(0)
                    last_sv = mm(ps_ret[b][:, lo - 512 * b:512],
                                 v_sb[:, 256 * jq + 128 * hl:
                                      256 * jq + 128 * (hl + 1)],
                                 buf_[:, lo:hi], jq == 0, jq == 4 * b + 3,
                                 waits=[(ev1e_, ev1)] +
                                 (pw(f"ret{b}") if jq == 0 else []),
                                 inc="TEN")
                    sv_done[b] = last_sv

            last_den = None
            last_sv = None
            sc_i = 0
            for j in range(NT):
                flush_den(j)
                flush_sv(j)
                t0 = 128 * j
                buf = s_sb[j % 2]
                for b in range(j // 4, 4):
                    lo = max(t0, 512 * b)
                    hi = 512 * (b + 1)
                    pst, pn = [(ps_a, "ps_a"), (ps_b, "ps_b"),
                               (ps_d, "ps_d")][sc_i % 3]
                    sc_i += 1
                    smm = None
                    for sp, (kb, qb) in enumerate(
                            ((kh_sb, qh_sb), (kh_sb, ql_sb), (kl_sb, qh_sb))):
                        smm = mm(pst[:, 0:hi - lo],
                                 kb[qof:qof + 64, t0:t0 + 128],
                                 qb[qof:qof + 64, lo:hi], sp == 0, sp == 2,
                                 waits=[("VEC", QKV_V)] +
                                 (pw(pn) if sp == 0 else []),
                                 inc="TEN" if sp == 2 else None)
                    if lo == t0:
                        ev1e = "VEC"
                        ev1 = vop(lambda e, buf=buf, lo=lo, pst=pst:
                                  e.tensor_mul(buf[:, lo:lo + 128],
                                               pst[:, 0:128], k_causal[:, :]),
                                  waits=[("TEN", smm)])
                        if hi > lo + 128:
                            ev1 = vop(lambda e, buf=buf, lo=lo, hi=hi, pst=pst:
                                      e.tensor_copy(buf[:, lo + 128:hi],
                                                    pst[:, 128:hi - lo]))
                    else:
                        ev1e = "SCA"
                        ev1 = sop(lambda e, buf=buf, lo=lo, hi=hi, pst=pst:
                                  e.copy(buf[:, lo:hi], pst[:, 0:hi - lo]),
                                  waits=[("TEN", smm)])
                    pnote(pn, ev1e, ev1)
                    ab = sop(lambda e, buf=buf, lo=lo, hi=hi: e.activation(
                        sabs[:, lo:hi], buf[:, lo:hi], AF.Abs),
                        waits=[(ev1e, ev1),
                               ("TEN", den_done.get(b, 0))])
                    den_q.append((j, b, lo, hi, ab))
                    sv_q.append((j, b, lo, hi, ev1e, ev1, buf))
            flush_sv(NT)
            flush_den(NT)
            A_ = s_sb[1][0:4, 0:512]
            B_ = s_sb[1][0:4, 512:1024]
            Cs = s_sb[1][0:4, 1024:1536]
            phi = A_
            dv = vop(lambda e: e.tensor_copy(A_, ps_den[0:4, 0:512]),
                     waits=[("TEN", last_den)])
            pnote("den", "VEC", dv)
            ret2_sb = s_sb[0]
            for b in range(4):
                rev = vop(lambda e, b=b: e.tensor_copy(
                    ret_sb[:, 512 * b:512 * (b + 1)], ps_ret[b][:, 0:512]),
                    waits=[("TEN", sv_done[b])] +
                    (bw("ret_sb") if hl == 0 and b == 0 else []))
                if hl == 0 and b == 0:
                    bclr("ret_sb")
                pnote(f"ret{b}", "VEC", rev)
                sq = sop(lambda e, b=b, ret2_sb=ret2_sb: e.activation(
                    ret2_sb[:, 512 * b:512 * (b + 1)],
                    ret_sb[:, 512 * b:512 * (b + 1)], AF.Square),
                    waits=[("VEC", rev)] +
                    ([("TEN", last_sv)] if b == 0 else []))
                last_ms = mm(ps_den[0:36, 0:512],
                             k_ind8m[:, 36 * b:36 * b + 36],
                             ret2_sb[:, 512 * b:512 * (b + 1)],
                             b == 0, b == 3,
                             waits=[("SCA", sq)] +
                             ([("VEC", dv)] if b == 0 else []),
                             inc="TEN" if b == 3 else None)
            mv = vop(lambda e: e.tensor_copy(B_, ps_den[32:36, 0:512]),
                     waits=[("TEN", last_ms)])
            pnote("den", "VEC", mv)
            vop(lambda e: e.tensor_scalar(A_, A_, 1.0, 50000.0, ALU.max,
                                          ALU.min))
            vop(lambda e: e.tensor_mul(Cs, A_, A_))
            vop(lambda e: e.tensor_scalar(B_, B_, 1.0 / VD, None, ALU.mult))
            VE2 = vop(lambda e: e.scalar_tensor_tensor(
                Cs, Cs, EPS_RMS, B_, ALU.mult, ALU.add))
            SQ3 = sop(lambda e: e.activation(A_, Cs, AF.Sqrt),
                      waits=[("VEC", VE2)])
            vop(lambda e: e.reciprocal(B_, A_), waits=[("SCA", SQ3)])
            vop(lambda e: e.tensor_mul(A_, B_, B_))
            vop(lambda e: e.tensor_mul(A_, A_, Cs))
            vop(lambda e: e.tensor_scalar(A_, A_, -0.5, 1.5, ALU.mult,
                                          ALU.add))
            PHI_V = vop(lambda e: e.tensor_mul(A_, A_, B_))
            if h == 0:
                for nm, ap_ in (("phi0", phi), ("ret0", ret_sb[:, :])):
                    if nm in tap_d:
                        dma("gpsimd", tap_d[nm][:, :], ap_,
                            waits=[("VEC", PHI_V)])
            g_hi, g_lo, g_rn, gsv = gspl[hl]
            GP_PS = [(ps_d, "ps_d"), (ps_a, "ps_a"), (ps_b, "ps_b"),
                     (ps_den, "den")]
            for tb in range(4):
                gpst, gpn = GP_PS[tb]
                first = True
                for hw, hx in ((0, 0), (0, 1), (1, 0)):
                    wt_ = g_hi if hw == 0 else g_lo
                    xs_ = xnh_p if hx == 0 else xnl_p
                    for kc in range(NCC):
                        mm(gpst[:, 0:512], wt_[:, 128 * kc:128 * (kc + 1)],
                           xs_[kc][:, 512 * tb:512 * (tb + 1)],
                           first, (hw, hx, kc) == (1, 0, NCC - 1),
                           waits=[("VEC", gsv)] +
                           (pw(gpn) if first else []),
                           inc="TEN" if (hw, hx, kc) == (1, 0, NCC - 1)
                           else None)
                        first = False
                gmm = cnt["TEN"]
                bmark(g_rn, "TEN", gmm)
                ev = vop(lambda e, tb=tb, h=h, gpst=gpst: e.tensor_scalar(
                    g_sb[:, 512 * tb:512 * (tb + 1)], gpst[:, 0:512],
                    k_bg2[:, h:h + 1], None, ALU.add),
                    waits=[("TEN", gmm)] +
                    (bw("g_sb") if hl == 0 and tb == 0 else []))
                if hl == 0 and tb == 0:
                    bclr("g_sb")
                pnote(gpn, "VEC", ev)
            for b in range(4):
                gpst, gpn = GP_PS[b]
                pb = mm(gpst[:, 0:512], k_indT4[0:4, 128 * b:128 * (b + 1)],
                        phi, True, True,
                        waits=pw(gpn) + [("VEC", PHI_V), *CONSTW],
                        inc="TEN")
                gv = vop(lambda e, b=b, gpst=gpst: e.tensor_mul(
                    ret_sb[:, 512 * b:512 * (b + 1)],
                    ret_sb[:, 512 * b:512 * (b + 1)], gpst[:, 0:512]),
                    waits=[("TEN", pb)])
                pnote(gpn, "VEC", gv)
                vop(lambda e, b=b: e.tensor_mul(
                    ret_sb[:, 512 * b:512 * (b + 1)],
                    ret_sb[:, 512 * b:512 * (b + 1)],
                    g_sb[:, 512 * b:512 * (b + 1)]))
                vop(lambda e, hl=hl, b=b: e.tensor_copy(
                    gath[:, T * hl + 512 * b:T * hl + 512 * (b + 1)],
                    ret_sb[:, 512 * b:512 * (b + 1)]),
                    waits=bw("gat" + str(hl)) if b == 0 else [])
                vg = vop(lambda e, hl=hl, b=b: e.scalar_tensor_tensor(
                    gatl[:, T * hl + 512 * b:T * hl + 512 * (b + 1)],
                    gath[:, T * hl + 512 * b:T * hl + 512 * (b + 1)],
                    -1.0, ret_sb[:, 512 * b:512 * (b + 1)],
                    ALU.mult, ALU.add))
                gate_done[hl][b] = vg
            bclr("gat" + str(hl))
            if h == 0 and "gat0" in tap_d:
                dma("gpsimd", tap_d["gat0"][:, :], gath[:, 0:T],
                    waits=[("VEC", vg)])

        for ts in range(NT):
            for cb in range(2):
                pst = ps_a if cb == 0 else ps_b
                pn = "ps_a" if cb == 0 else "ps_b"
                first = True
                wmm = None
                gw = max(osv, gate_done[0][ts // 4], gate_done[1][ts // 4])
                for hg, hw in ((0, 0), (0, 1), (1, 0)):
                    gsrc = gath if hg == 0 else gatl
                    wsrc = woh if hw == 0 else wol
                    for hl in range(2):
                        last = (hg, hw, hl) == (1, 0, 1) and grp > 0
                        wmm = mm(
                            pst[:, 0:512],
                            gsrc[:, T * hl + 128 * ts:T * hl + 128 * (ts + 1)],
                            wsrc[:, C * hl + 512 * cb:C * hl + 512 * (cb + 1)],
                            first, last,
                            waits=[("VEC", gw)] + (pw(pn) if first else []),
                            inc="TEN" if last else None)
                        first = False
                if grp == 0:
                    wmm = mm(pst[:, 0:512], k_onesr[0:1, :],
                             k_bo[0:1, 512 * cb:512 * (cb + 1)], False, True,
                             waits=[*CONSTW], inc="TEN")
                bmark("wvh", "TEN", wmm)
                bmark("gat0", "TEN", wmm)
                bmark("gat1", "TEN", wmm)
                sl4 = ts % 4
                wbuf = ret_sb if sl4 < 2 else g_sb
                wrn = "ret_sb" if sl4 < 2 else "g_sb"
                off = 1024 * (sl4 % 2)
                ev = vop(lambda e, cb=cb, pst=pst, wbuf=wbuf, off=off:
                         e.tensor_copy(
                             wbuf[:, off + 512 * cb:off + 512 * (cb + 1)],
                             pst[:, 0:512]),
                         waits=[("TEN", wmm)] + bw(f"wo_dma{sl4}"))
                pnote(pn, "VEC", ev)
            bclr(f"wo_dma{sl4}")
            wod = dma("gpsimd", scrATT[128 * ts:128 * (ts + 1), :],
                      wbuf[:, off:off + C],
                      waits=[("VEC", cnt["VEC"])] + att_prev_dgp,
                      accum=(grp > 0))
            bmark(f"wo_dma{sl4}", wod[0], wod[1])
            bmark(wrn, wod[0], wod[1])
        att_prev_dgp = qall("gpsimd")
    ATT_DGP = qall("gpsimd")
    if "attn" in tap_d:
        for ts in range(NT):
            ldw = dma("scalar", ret_sb[:, 0:C],
                      scrATT[128 * ts:128 * (ts + 1), :],
                      waits=qall("gpsimd") + [("VEC", cnt["VEC"])])
            dma("gpsimd", tap_d["attn"][128 * ts:128 * (ts + 1), :],
                ret_sb[:, 0:C], waits=[ldw])

    S.barrier()
    attn.close()
    if stop_after == "p2":
        return finish()

    ffn = contextlib.ExitStack()
    fsb = mk_sb(ffn)
    h3h = fsb("h3h", (128, NCH * 512), BF16)
    h3l = fsb("h3l", (128, NCH * 512), BF16)
    h3f = fsb("h3f", (128, 2 * 512))
    we_t = fsb("we_t", (128, 2 * 2048), BF16)
    wc_t = fsb("wc_t", (128, 2 * 1024), BF16)
    x_res = fsb("x_res", (128, 2 * C))
    a_res = fsb("a_res", (128, 2 * C))
    out_st = fsb("out_st", (128, 2 * C))

    cb1_ps = [ps_a, ps_b, ps_c, ps_d]
    cb1_pn = ["ps_a", "ps_b", "ps_c", "ps_d"]
    for nm in ("we0", "we1", "wc0", "wc1"):
        bclr(nm)
    for tile in range(4):
        toff = 512 * tile
        for oc in range(NCH):
            slot = 2048 * (oc % 2)
            wl = dma("sync", we_t[:, slot:slot + 2048],
                     scrWE[oc].rearrange("hl p a q -> p hl a q"),
                     waits=WE_DGP + bw(f"we{oc % 2}"))
            bclr(f"we{oc % 2}")
            pst = ps_a if oc % 2 == 0 else ps_b
            pn = "ps_a" if oc % 2 == 0 else "ps_b"
            first = True
            for hl_w, hl_x in ((0, 0), (0, 1), (1, 0)):
                xsrc = xnh_p if hl_x == 0 else xnl_p
                for kc in range(NCC):
                    mm(pst[:, 0:512],
                       we_t[:, slot + 1024 * hl_w + 128 * kc:
                            slot + 1024 * hl_w + 128 * (kc + 1)],
                       xsrc[kc][:, toff:toff + 512],
                       first, (hl_w, hl_x, kc) == (1, 0, NCC - 1),
                       waits=[wl] + (pw(pn) if first else []),
                       inc="TEN" if (hl_w, hl_x, kc) == (1, 0, NCC - 1)
                       else None)
                    first = False
            emm = cnt["TEN"]
            bmark(f"we{oc % 2}", "TEN", emm)
            gl = sop(lambda e, oc=oc, pst=pst: e.activation(
                h3f[:, 512 * (oc % 2):512 * (oc % 2 + 1)], pst[:, 0:512],
                AF.Gelu, bias=bprime[:, oc:oc + 1], scale=1.0),
                waits=[("TEN", emm), ("VEC", BPRIME_V)] + bw(f"h3f{oc % 2}"))
            bclr(f"h3f{oc % 2}")
            pnote(pn, "SCA", gl)
            vop(lambda e, oc=oc: e.tensor_copy(
                h3h[:, 512 * oc:512 * (oc + 1)],
                h3f[:, 512 * (oc % 2):512 * (oc % 2 + 1)]),
                waits=[("SCA", gl)])
            hv = vop(lambda e, oc=oc: e.scalar_tensor_tensor(
                h3l[:, 512 * oc:512 * (oc + 1)],
                h3h[:, 512 * oc:512 * (oc + 1)], -1.0,
                h3f[:, 512 * (oc % 2):512 * (oc % 2 + 1)],
                ALU.mult, ALU.add))
            bmark(f"h3f{oc % 2}", "VEC", hv)
        H3_S = cnt["VEC"]
        if tile == 0 and "h3" in tap_d:
            pass
        def load_res(ts):
            rs = C * (ts % 2)
            dma("scalar", x_res[:, rs:rs + C],
                d["x"][toff + 128 * ts:toff + 128 * (ts + 1), :],
                waits=bw(f"res{ts % 2}"))
            al = dma("scalar", a_res[:, rs:rs + C],
                     scrATT[toff + 128 * ts:toff + 128 * (ts + 1), :],
                     waits=ATT_DGP + bw(f"res{ts % 2}"))
            bclr(f"res{ts % 2}")
            return al

        al_l = [load_res(0), load_res(1)]
        bmms = {}
        for cb in range(2):
            for c2 in range(NCH):
                slot = 1024 * (c2 % 2)
                srcb = scrWC.rearrange("(hi gg) hl q -> hi gg hl q", gg=128)
                srcb = srcb[:, 4 * c2:4 * c2 + 4, :, 512 * cb:512 * (cb + 1)]
                srcb = srcb.rearrange("hi g hl q -> g hi hl q")
                wlw = WC_DGP + bw(f"wc{c2 % 2}")
                wl0 = dma("sync", wc_t[:, slot:slot + 512], srcb[:, :, 0, :],
                          waits=wlw)
                wl = dma("sync", wc_t[:, slot + 512:slot + 1024],
                         srcb[:, :, 1, :], waits=wlw)
                bclr(f"wc{c2 % 2}")
                wm = None
                for hl_h, hl_w in ((0, 0), (0, 1), (1, 0)):
                    hsrc = h3h if hl_h == 0 else h3l
                    for ts in range(4):
                        pst = ps_ret[ts] if cb == 0 else cb1_ps[ts]
                        pn = f"ret{ts}" if cb == 0 else cb1_pn[ts]
                        wm = mm(pst[:, 0:512],
                                hsrc[:, 512 * c2 + 128 * ts:
                                     512 * c2 + 128 * (ts + 1)],
                                wc_t[:, slot + 512 * hl_w:
                                     slot + 512 * (hl_w + 1)],
                                c2 == 0 and (hl_h, hl_w) == (0, 0), False,
                                waits=[wl0, wl, ("VEC", H3_S)] +
                                (pw(pn) if c2 == 0 and
                                 (hl_h, hl_w) == (0, 0) else []),
                                inc="TEN" if (hl_h, hl_w, ts) == (1, 0, 3)
                                else None)
                bmark(f"wc{c2 % 2}", "TEN", wm)
            for ts in range(4):
                pst = ps_ret[ts] if cb == 0 else cb1_ps[ts]
                bmms[(cb, ts)] = mm(
                    pst[:, 0:512], k_onesr[0:1, :],
                    k_bfin[0:1, 512 * cb:512 * (cb + 1)], False, True,
                    waits=[("VEC", BFIN_V)], inc="TEN")
        for ts in range(4):
            slot = C * (ts % 2)
            for cb in range(2):
                pst = ps_ret[ts] if cb == 0 else cb1_ps[ts]
                pn = f"ret{ts}" if cb == 0 else cb1_pn[ts]
                rs = C * (ts % 2)
                ev = vop(lambda e, pst=pst, rs=rs, cb=cb, slot=slot:
                         e.scalar_tensor_tensor(
                             out_st[:, slot + 512 * cb:slot + 512 * (cb + 1)],
                             pst[:, 0:512], 1.0,
                             x_res[:, rs + 512 * cb:rs + 512 * (cb + 1)],
                             ALU.mult, ALU.add),
                         waits=[("TEN", bmms[(cb, ts)]), al_l[ts]]
                         + (bw(f"y{ts % 2}") if cb == 0 else []))
                pnote(pn, "VEC", ev)
                ev2 = vop(lambda e, rs=rs, cb=cb, slot=slot: e.tensor_add(
                    out_st[:, slot + 512 * cb:slot + 512 * (cb + 1)],
                    out_st[:, slot + 512 * cb:slot + 512 * (cb + 1)],
                    a_res[:, rs + 512 * cb:rs + 512 * (cb + 1)]))
                bmark(f"res{ts % 2}", "VEC", ev2)
            bclr(f"y{ts % 2}")
            yd = dma("gpsimd",
                     y[toff + 128 * ts:toff + 128 * (ts + 1), :],
                     out_st[:, slot:slot + C],
                     waits=[("VEC", cnt["VEC"])])
            bmark(f"y{ts % 2}", yd[0], yd[1])
            if ts + 2 < 4:
                al_l.append(load_res(ts + 2))

    ffn.close()
    return finish()



def kernel(**inputs):
    x_full = np.ascontiguousarray(np.asarray(inputs["L0_attn_x"], np.float32))
    B = x_full.shape[0]
    consts, meta = build_host_constants(inputs)
    nc = build_program(meta)
    base = {
        "Wqk": np.ascontiguousarray(np.asarray(inputs["L0_attn_W_qk"], np.float32)),
        "Wv": np.ascontiguousarray(np.asarray(inputs["L0_attn_W_v"], np.float32)),
        "Wg": np.ascontiguousarray(np.asarray(inputs["L0_attn_W_g"], np.float32)),
        "Wo": np.ascontiguousarray(np.asarray(inputs["L0_attn_W_o"], np.float32)),
        "Wexp": np.ascontiguousarray(np.asarray(inputs["L0_ffn_W_exp"], np.float32)),
        "Wcon": np.ascontiguousarray(np.asarray(inputs["L0_ffn_W_con"], np.float32)),
    }
    base.update({k: np.ascontiguousarray(v) for k, v in consts.items()})
    in_maps = []
    for b in range(B):
        m = dict(base)
        m["x"] = x_full[b]
        in_maps.append(m)
    res = run_bass_kernel_spmd(nc, in_maps, core_ids=list(range(B)))
    out = np.stack([np.asarray(r["y"]) for r in res.results], axis=0)
    return out.astype(np.float32)


# version 26
# speedup vs baseline: 1.0884x; 1.0884x over previous
import contextlib
import sys
import ml_dtypes
import numpy as np
import concourse.bass as bass
import concourse.mybir as mybir
from concourse.bass_utils import run_bass_kernel_spmd

F32 = mybir.dt.float32
BF16 = mybir.dt.bfloat16
ALU = mybir.AluOpType
AF = mybir.ActivationFunctionType
AX = mybir.AxisListType

T, C, H, KD, VBD, VD, DH = 2048, 1024, 16, 64, 2048, 128, 4096
NCC = C // 128
NCH = DH // 128
NT = T // 128
EPS_LN = 1e-5
EPS_RMS = float(np.finfo(np.float32).eps)

ENGINES = ("tensor", "vector", "scalar", "gpsimd", "sync")
NQCLS = 8
_QUEUES = ("sync", "scalar", "gpsimd")
SEMS = ("TEN", "VEC", "SCA", "GPS") + tuple(
    f"{q[:2].upper()}{i}" for q in _QUEUES for i in range(NQCLS))


class Sched:

    def __init__(self):
        self.items = {e: [] for e in ENGINES}
        self.count = {s: 0 for s in SEMS}
        self.pending_bar = {e: None for e in ENGINES}

    def barrier(self):
        snap = [(s, v) for s, v in self.count.items() if v > 0]
        for e in ENGINES:
            self.pending_bar[e] = snap

    _SELF = {"vector": "VEC", "scalar": "SCA", "gpsimd": "GPS"}

    def emit(self, eng, fn, waits=(), inc=None, inc_by=1):
        w = dict()
        if self.pending_bar[eng]:
            for s, v in self.pending_bar[eng]:
                w[s] = max(w.get(s, 0), v)
            self.pending_bar[eng] = None
        for s, v in waits:
            w[s] = max(w.get(s, 0), v)
        selfsem = self._SELF.get(eng)
        if selfsem is not None and inc == selfsem and self.count[selfsem] > 0:
            w[selfsem] = max(w.get(selfsem, 0), self.count[selfsem])
        self.items[eng].append((fn, sorted(w.items()), inc, inc_by))
        if inc is not None:
            self.count[inc] += inc_by
            return self.count[inc]
        return None

    def install(self, block, sems):
        for engname in ENGINES:
            items = self.items[engname]
            if not items:
                continue

            def body(e, items=items):
                for fn, waits, inc, inc_by in items:
                    for s, v in waits:
                        e.wait_ge(sems[s], v)
                    ins = fn(e)
                    if inc is not None:
                        ins.then_inc(sems[inc], inc_by)

            getattr(block, engname)(body)



def _decay_tables():
    h = np.arange(H, dtype=np.float64)
    gamma = np.log(1.0 - 2.0 ** (-5.0 - h))
    t = np.arange(T, dtype=np.float64)
    eg = np.exp(gamma)[:, None]
    ssum = (1.0 - eg ** (t[None, :] + 1.0)) / (1.0 - eg)
    inv_scale = 1.0 / np.sqrt(ssum)
    et = np.exp(gamma[:, None] * t[None, :]) * inv_scale
    es = np.exp(-gamma[:, None] * t[None, :])
    et64 = np.repeat(et, KD, axis=0)
    es64 = np.repeat(es, KD, axis=0)
    etw = et64.astype(np.float32)
    esw = es64.astype(np.float32)
    etwl = (et64 - etw.astype(np.float64)).astype(ml_dtypes.bfloat16)
    eswl = (es64 - esw.astype(np.float64)).astype(ml_dtypes.bfloat16)
    return etw, esw, etwl, eswl


def _stage_coeffs(wstages):
    n_st = wstages.shape[0]
    f = np.arange(DH)
    a = np.empty((n_st, DH), np.float32)
    bP = np.empty((n_st, DH), np.float32)
    bT = np.empty((n_st, DH), np.float32)
    for k in range(n_st):
        s = 1 << k
        pr = (f // (2 * s)) * s + (f % s)
        bit = (f >> k) & 1
        a[k] = wstages[k, pr, bit, bit]
        bP[k] = wstages[k, pr, bit, 1 - bit]
        bT[k] = bP[k][f ^ s]
    return a, bP, bT


def _f_of_l2():
    c2 = np.arange(32)[:, None]
    p2 = np.arange(128)[None, :]
    return 128 * (p2 % 32) + 4 * c2 + (p2 // 32)


def _chain_specs(inputs):
    ua, ubP, ubT = _stage_coeffs(np.asarray(inputs["L0_ffn_up_weights"], np.float32))
    da, dbP, dbT = _stage_coeffs(np.asarray(inputs["L0_ffn_down_weights"], np.float32))
    l1 = lambda v: v.reshape(32, 128)
    l2 = lambda v: v[_f_of_l2()]
    spec = []
    spec.append((l1(ua[6]), l1(ubT[6]), 64))
    for k in range(5, -1, -1):
        spec.append((l1(ua[k]), l1(ubT[k]), 1 << k))
    spec.append((l2(ua[11]), l2(ubT[11]), 16))
    for k in range(10, 6, -1):
        spec.append((l2(ua[k]), l2(ubT[k]), 1 << (k - 7)))
    spec.append((l2(da[7]), l2(dbP[7]), 1))
    for k in range(8, 12):
        spec.append((l2(da[k]), l2(dbP[k]), 1 << (k - 7)))
    spec.append((l1(da[0]), l1(dbP[0]), 1))
    for k in range(1, 7):
        spec.append((l1(da[k]), l1(dbP[k]), 1 << k))
    alpha = np.stack([s[0] for s in spec]).astype(np.float32)
    beta = np.stack([s[1] for s in spec]).astype(np.float32)
    xors = [s[2] for s in spec]
    return alpha, beta, xors


CHAIN_LAYOUT = (("A_up", 7), ("B_up", 5), ("B_dnT", 5), ("A_dnT", 7))


def build_host_constants(inputs):
    import ml_dtypes
    alpha, beta, xors = _chain_specs(inputs)
    etw, esw, etwl, eswl = _decay_tables()
    xs_list = (1, 2, 4, 8, 16, 32, 64)
    xpat = np.zeros((7, 128, 128), np.float32)
    i = np.arange(128)
    for xi, s in enumerate(xs_list):
        xpat[xi, i, i ^ s] = 1.0
    causal = (np.arange(128)[None, :] >= np.arange(128)[:, None]).astype(np.float32)
    ind4 = np.zeros((128, 16), np.float32)
    for b in range(4):
        ind4[:, 4 * b + b] = 1.0
    ind8d = np.zeros((128, 32), np.float32)
    ind8m = np.zeros((128, 4 * 36), np.float32)
    for b in range(4):
        ind8d[:, 8 * b + b] = 1.0
        ind8m[:, 36 * b + 32 + b] = 1.0
    indT4 = np.zeros((4, 4 * 128), np.float32)
    for b in range(4):
        indT4[b, 128 * b:128 * (b + 1)] = 1.0
    b_qk = np.asarray(inputs["L0_attn_b_qk"], np.float32)
    consts = {
        "alpha": np.ascontiguousarray(alpha.transpose(2, 0, 1).reshape(128, -1)),
        "beta": np.ascontiguousarray(beta.transpose(2, 0, 1).reshape(128, -1)),
        "etw": etw, "esw": esw, "etwl": etwl, "eswl": eswl,
        "xpat": xpat, "causal": causal,
        "ident": np.eye(128, dtype=np.float32),
        "onesr": np.ones((1, 128), np.float32),
        "ind4f": ind4,
        "ind4b": ind4.astype(ml_dtypes.bfloat16),
        "ind8m": ind8m,
        "indT4": indT4.T.copy(),
        "bqkq": b_qk[:C].reshape(8, 128).T.copy(),
        "bqkk": b_qk[C:].reshape(8, 128).T.copy(),
        "bg2": np.asarray(inputs["L0_attn_b_g"], np.float32).reshape(16, 128).T.copy(),
        "upb_l2": np.asarray(inputs["L0_ffn_up_bias"], np.float32)[_f_of_l2()].T.copy(),
        "bv": np.asarray(inputs["L0_attn_b_v"], np.float32).reshape(1, VBD),
        "bo": np.asarray(inputs["L0_attn_b_o"], np.float32).reshape(1, C),
        "bexp": np.asarray(inputs["L0_ffn_b_exp"], np.float32).reshape(1, DH),
        "bcon": np.asarray(inputs["L0_ffn_b_con"], np.float32).reshape(1, C),
        "dbias": np.asarray(inputs["L0_ffn_down_bias"], np.float32).reshape(1, C),
    }
    ind_t = np.zeros((4, 512), np.float32)
    for b in range(4):
        ind_t[b, 128 * b:128 * (b + 1)] = 1.0
    consts["indT4"] = ind_t
    consts["ind8d"] = ind8d.astype(ml_dtypes.bfloat16)
    xpi = {s: i for i, s in enumerate(xs_list)}
    meta = {"xors": xors, "xpi": xpi}
    return consts, meta



MM_LINES = []


def build_program(meta, taps=(), stop_after=None, p2_groups=8, p2_heads=2, p2_wo=True, p2_parts="sadvrgp"):
    nc = bass.Bass("TRN2", target_bir_lowering=False,
                   detect_race_conditions=False)
    S = Sched()
    cnt = S.count

    def din(name, shape, dt=F32):
        return nc.dram_tensor(name, list(shape), dt, kind="ExternalInput").ap()

    d = {}
    d["x"] = din("x", (T, C))
    d["Wqk"] = din("Wqk", (C, 2 * C))
    d["Wv"] = din("Wv", (C, VBD))
    d["Wg"] = din("Wg", (C, VBD))
    d["Wo"] = din("Wo", (VBD, C))
    d["Wexp"] = din("Wexp", (C, DH))
    d["Wcon"] = din("Wcon", (DH, C))
    for name, shape in (
        ("alpha", (128, 24 * 32)), ("beta", (128, 24 * 32)),
        ("etw", (C, T)), ("esw", (C, T)),
        ("xpat", (7, 128, 128)), ("causal", (128, 128)), ("ident", (128, 128)),
        ("onesr", (1, 128)), ("ind4f", (128, 16)), ("indT4", (4, 512)),
        ("bqkq", (128, 8)), ("bqkk", (128, 8)), ("bg2", (128, 16)),
        ("upb_l2", (128, 32)), ("bv", (1, VBD)), ("bo", (1, C)),
        ("bexp", (1, DH)), ("bcon", (1, C)), ("dbias", (1, C)),
    ):
        d[name] = din(name, shape)
    d["etwl"] = din("etwl", (C, T), BF16)
    d["eswl"] = din("eswl", (C, T), BF16)
    d["ind4b"] = din("ind4b", (128, 16), BF16)
    d["ind8d"] = din("ind8d", (128, 32), BF16)
    d["ind8m"] = din("ind8m", (128, 4 * 36))
    y = nc.dram_tensor("y", [T, C], F32, kind="ExternalOutput").ap()

    scrA = nc.dram_tensor("scrA", [DH, C], F32, kind="Internal").ap()
    scrB = nc.dram_tensor("scrB", [DH, C + 1], F32, kind="Internal").ap()
    scrWE = nc.dram_tensor("scrWE", [NCH, 2, 128, NCC, 128], BF16,
                           kind="Internal").ap()
    scrV = nc.dram_tensor("scrV", [DH, C], F32, kind="Internal").ap()
    scrWC = nc.dram_tensor("scrWC", [DH, 2, C], BF16,
                           kind="Internal").ap()
    scrATT = nc.dram_tensor("scrATT", [T, C], F32, kind="Internal").ap()

    tap_d = {}
    for tname, tshape in dict(
        bprime=(128, 32),
        q0=(128, T), k0=(128, T), v0=(128, NT * 256), s0=(128, T),
        g0=(128, T), retn0=(128, T),
        den0=(4, 512), phi0=(4, 512), ret0=(128, T), gat0=(128, T),
        attn=(T, C), h3=(128, NCH * 512),
    ).items():
        if tname in taps:
            tap_d[tname] = nc.dram_tensor("tap_" + tname, list(tshape), F32,
                                          kind="ExternalOutput").ap()

    root = contextlib.ExitStack()

    def mk_sb(st):
        return (lambda name, shape, dt=F32:
                st.enter_context(nc.sbuf_tensor(name, list(shape), dt)))

    sb = mk_sb(root)
    psum = lambda name, shape: root.enter_context(
        nc.psum_tensor(name, list(shape), F32))

    xnh_p = [sb(f"xnh{c}", (128, T), BF16) for c in range(NCC)]
    xnl_p = [sb(f"xnl{c}", (128, T), BF16) for c in range(NCC)]
    k_ident = sb("k_ident", (128, 128))
    k_causal = sb("k_causal", (128, 128))
    k_onesr = sb("k_onesr", (1, 128))
    k_ind4f = sb("k_ind4f", (128, 16))
    k_ind4b = sb("k_ind4b", (128, 16), BF16)
    k_ind8d = sb("k_ind8d", (128, 32), BF16)
    k_ind8m = sb("k_ind8m", (128, 4 * 36))
    k_indT4 = sb("k_indT4", (4, 512))
    k_bqkq = sb("k_bqkq", (128, 8))
    k_bqkk = sb("k_bqkk", (128, 8))
    k_bg2 = sb("k_bg2", (128, 16))
    k_upb = sb("k_upb", (128, 32))
    k_bv = sb("k_bv", (1, VBD))
    k_bo = sb("k_bo", (1, C))
    k_bfin = sb("k_bfin", (1, C))
    bprime = sb("bprime", (128, 32))

    ps_a = psum("ps_a", (128, 512))
    ps_b = psum("ps_b", (128, 512))
    ps_ret = [psum(f"ps_ret{i}", (128, 512)) for i in range(4)]
    ps_c = psum("ps_c", (128, 512))
    ps_d = psum("ps_d", (128, 512))
    ps_den = ps_c
    CH_PS = [(ps_a, "cps_a"), (ps_b, "cps_b"), (ps_c, "cps_c"),
             (ps_d, "cps_d"), (ps_ret[0], "ret0"), (ps_ret[1], "ret1"),
             (ps_ret[2], "ret2"), (ps_ret[3], "ret3")]
    ps_ms = ps_d

    sems = {s: root.enter_context(nc.semaphore(s)) for s in SEMS}

    def mm(ps_ap, lhsT, rhs, start, stop, waits=(), inc=None):
        MM_LINES.append((sys._getframe(1).f_lineno,
                         lhsT.dtype == F32 and rhs.dtype == F32))
        return S.emit("tensor",
                      lambda e, o=ps_ap, a=lhsT, b=rhs, st=start, sp=stop:
                      e.matmul(o, a, b, start=st, stop=sp),
                      waits=waits, inc=inc)

    def tp(ps_ap, in_ap, id_ap, waits=(), inc="TEN"):
        MM_LINES.append((-sys._getframe(1).f_lineno, False))
        return S.emit("tensor",
                      lambda e, o=ps_ap, i=in_ap, idm=id_ap:
                      e.transpose(o, i, idm),
                      waits=waits, inc=inc)

    qseq = {q: 0 for q in _QUEUES}

    def dma(queue, out, in_, waits=(), accum=False):
        if accum:
            fn = lambda e, o=out, i=in_: e.dma_start(out=o, in_=i,
                                                     accum_op=ALU.add)
        else:
            fn = lambda e, o=out, i=in_: e.dma_start(out=o, in_=i)
        n = qseq[queue]
        qseq[queue] += 1
        sem = f"{queue[:2].upper()}{n % NQCLS}"
        prev = 16 * (n // NQCLS)
        S.emit(queue, fn, waits=list(waits) + [(sem, prev)],
               inc=sem, inc_by=16)
        return (sem, prev + 16)

    def qall(queue):
        return [(f"{queue[:2].upper()}{i}",
                 cnt[f"{queue[:2].upper()}{i}"]) for i in range(NQCLS)]

    def vop(fn, waits=()):
        return S.emit("vector", fn, waits=waits, inc="VEC")

    def sop(fn, waits=()):
        return S.emit("scalar", fn, waits=waits, inc="SCA")

    def allw():
        return [(s, v) for s, v in cnt.items() if v > 0]

    pfree = {}
    bnote = {}

    def _note(dic, name, sem, val):
        d_ = dic.setdefault(name, {})
        d_[sem] = max(d_.get(sem, 0), val)

    def _w(dic, name):
        return list(dic.get(name, {}).items())

    pnote = lambda name, sem, val: _note(pfree, name, sem, val)
    bmark = lambda name, sem, val: _note(bnote, name, sem, val)
    pw = lambda name: _w(pfree, name)
    bw = lambda name: _w(bnote, name)

    def bclr(name):
        bnote.pop(name, None)

    def finish():
        S.emit("gpsimd", lambda e: e.nop(),
               waits=qall("gpsimd") + qall("sync") + qall("scalar"))
        with nc.Block() as block:
            S.install(block, sems)
        root.close()
        return nc

    for tile_ap, src in (
        (k_ident[:, :], d["ident"][:, :]), (k_causal[:, :], d["causal"][:, :]),
        (k_ind8d[:, :], d["ind8d"][:, :]), (k_ind8m[:, :], d["ind8m"][:, :]),
        (k_onesr[:, :], d["onesr"][:, :]), (k_ind4f[:, :], d["ind4f"][:, :]),
        (k_ind4b[:, :], d["ind4b"][:, :]), (k_indT4[:, :], d["indT4"][:, :]),
        (k_bqkq[:, :], d["bqkq"][:, :]), (k_bqkk[:, :], d["bqkk"][:, :]),
        (k_bg2[:, :], d["bg2"][:, :]), (k_upb[:, :], d["upb_l2"][:, :]),
        (k_bv[:, :], d["bv"][:, :]), (k_bo[:, :], d["bo"][:, :]),
        (k_bfin[:, :], d["bcon"][:, :]),
    ):
        dma("sync", tile_ap, src)
    CONSTW = qall("sync")

    prep = contextlib.ExitStack()
    psb = mk_sb(prep)
    wrow = psb("wrow", (128, DH))
    wrow2 = psb("wrow2", (128, DH))
    wtile = psb("wtile", (128, C + 1))
    wtile2 = psb("wtile2", (128, C + 1))
    wtile3 = psb("wtile3", (128, C + 1))
    evt3 = psb("evt3", (128, 1025))
    chainL = psb("chainL", (128, 32 * 128))
    stile8 = psb("stile8", (128, 1024))
    xpat_t = [psb(f"xpat{i}", (128, 128)) for i in range(7)]
    alpha_t = psb("alpha_t", (128, 24 * 32))
    beta_t = psb("beta_t", (128, 24 * 32))
    evt = psb("evt", (128, 1025))
    evt2 = psb("evt2", (128, 1025))
    whl0 = psb("whl0", (128, 2048), BF16)
    whl1 = psb("whl1", (128, 2048), BF16)
    k_brow = psb("k_brow", (1, DH))
    x_tok = psb("x_tok", (128, C))
    x_tok2 = wtile3[:, 0:C]
    x_sq = evt3[:, 0:1024]
    xn_st = evt2[:, 0:1024]
    ln_s = psb("ln_s", (128, 16))

    dma("sync", k_brow[:, :], d["bexp"][:, :])
    for i in range(7):
        dma("sync", xpat_t[i][:, :], d["xpat"][i, :, :])
    dma("sync", alpha_t[:, :], d["alpha"][:, :])
    dma("sync", beta_t[:, :], d["beta"][:, :])
    dma("sync", evt[0:1, 0:C], d["dbias"][:, :])
    CONSTW2 = qall("sync")
    BFIN_V = vop(lambda e: e.tensor_add(k_bfin[0:1, :], k_bfin[0:1, :],
                                        evt[0:1, 0:C]),
                 waits=[*CONSTW2])

    p0st = {"tp": 0}
    xcons = {0: (0, 0), 1: (0, 0)}

    def emit_p0_tile(ti):
        xt = x_tok if ti % 2 == 0 else x_tok2
        o = 8 * (ti % 2)
        pv, psc = xcons[ti % 2]
        ld = dma("sync" if ti % 2 == 0 else "scalar", xt[:, :],
                 d["x"][128 * ti:128 * (ti + 1), :],
                 waits=[("VEC", pv), ("SCA", psc)])
        SQ = sop(lambda e: e.activation(x_sq[:, :], xt[:, :], AF.Square,
                                        accum_out=ln_s[:, o + 1:o + 2]),
                 waits=[ld])
        vop(lambda e: e.tensor_reduce(ln_s[:, o:o + 1], xt[:, :], AX.X,
                                      ALU.add),
            waits=[ld])
        vop(lambda e: e.tensor_scalar(ln_s[:, o + 2:o + 3], ln_s[:, o:o + 1],
                                      1.0 / C, None, ALU.mult))
        vop(lambda e: e.tensor_mul(ln_s[:, o + 3:o + 4], ln_s[:, o + 2:o + 3],
                                   ln_s[:, o + 2:o + 3]))
        vop(lambda e: e.scalar_tensor_tensor(ln_s[:, o + 4:o + 5],
                                             ln_s[:, o + 1:o + 2],
                                             1.0 / C, ln_s[:, o + 3:o + 4],
                                             ALU.mult, ALU.subtract),
            waits=[("SCA", SQ)])
        vop(lambda e: e.tensor_scalar(ln_s[:, o + 4:o + 5],
                                      ln_s[:, o + 4:o + 5], EPS_LN,
                                      None, ALU.add))
        VE = cnt["VEC"]
        SQ2 = sop(lambda e: e.activation(ln_s[:, o + 5:o + 6],
                                         ln_s[:, o + 4:o + 5], AF.Sqrt),
                  waits=[("VEC", VE)])
        vop(lambda e: e.reciprocal(ln_s[:, o + 6:o + 7],
                                   ln_s[:, o + 5:o + 6]),
            waits=[("SCA", SQ2)])
        vop(lambda e: e.tensor_mul(ln_s[:, o + 7:o + 8], ln_s[:, o + 6:o + 7],
                                   ln_s[:, o + 6:o + 7]))
        vop(lambda e: e.tensor_mul(ln_s[:, o + 7:o + 8], ln_s[:, o + 7:o + 8],
                                   ln_s[:, o + 4:o + 5]))
        vop(lambda e: e.tensor_scalar(ln_s[:, o + 7:o + 8],
                                      ln_s[:, o + 7:o + 8], -0.5, 1.5,
                                      ALU.mult, ALU.add))
        vop(lambda e: e.tensor_mul(ln_s[:, o + 7:o + 8], ln_s[:, o + 7:o + 8],
                                   ln_s[:, o + 6:o + 7]))
        XN_V = vop(lambda e: e.tensor_scalar(xn_st[:, 0:C], xt[:, :],
                                             ln_s[:, o + 2:o + 3],
                                             ln_s[:, o + 7:o + 8],
                                             ALU.subtract, ALU.mult),
                   waits=[("TEN", p0st["tp"])])
        for c in range(NCC):
            pst, pn = CH_PS[c % 8]
            t_mm = tp(pst[:, 0:128], xn_st[:, 128 * c:128 * (c + 1)],
                      k_ident[:, :],
                      waits=[("VEC", XN_V)] + pw(pn) +
                      ([*CONSTW] if ti == 0 and c == 0 else []))
            p0st["tp"] = t_mm
            vop(lambda e, c=c, ti=ti, pst=pst: e.tensor_copy(
                xnh_p[c][:, 128 * ti:128 * (ti + 1)], pst[:, 0:128]),
                waits=[("TEN", t_mm)])
            lo_ = vop(lambda e, c=c, ti=ti, pst=pst: e.scalar_tensor_tensor(
                xnl_p[c][:, 128 * ti:128 * (ti + 1)],
                xnh_p[c][:, 128 * ti:128 * (ti + 1)], -1.0,
                pst[:, 0:128], ALU.mult, ALU.add))
            pnote(pn, "VEC", lo_)
        xcons[ti % 2] = (XN_V, SQ)

    w1_last = {(bf, ch): ("VEC", 0) for bf in range(2) for ch in range(4)}

    def emit_w1_rc(rc):
        wr_ = wrow if rc % 2 == 0 else wrow2
        ld_ch = []
        for ch in range(4):
            ld_ch.append(dma("sync" if ch % 2 == 0 else "scalar",
                             wr_[:, 1024 * ch:1024 * (ch + 1)],
                             d["Wexp"][128 * rc:128 * (rc + 1),
                                       1024 * ch:1024 * (ch + 1)],
                             waits=[w1_last[(rc % 2, ch)]]))
        for cc in range(NCH):
            pst, pn = CH_PS[cc % 8]
            t_mm = tp(pst[:, 0:128], wr_[:, 128 * cc:128 * (cc + 1)],
                      k_ident[:, :],
                      waits=[ld_ch[cc // 8]] + pw(pn))
            w_extra = qall("gpsimd") if cc % 8 == 0 else []
            ev_w = vop(lambda e, cc=cc, pst=pst: e.tensor_copy(
                evt[:, 128 * (cc % 8):128 * (cc % 8 + 1)], pst[:, 0:128]),
                waits=[("TEN", t_mm)] + w_extra)
            pnote(pn, "VEC", ev_w)
            if cc % 8 == 7:
                base = cc - 7
                w1_last[(rc % 2, cc // 8)] = ("TEN", t_mm)
                dst = scrA[:, 128 * rc:128 * (rc + 1)] \
                    .rearrange("(a p) q -> p a q", p=128)[:, base:cc + 1, :]
                dma("gpsimd", dst,
                    evt[:, 0:1024].rearrange("p (a q) -> p a q", q=128),
                    waits=[("VEC", cnt["VEC"])])

    for ti in range(NT):
        emit_p0_tile(ti)
        if ti % 2 == 1:
            emit_w1_rc(ti // 2)
    P0_V = cnt["VEC"]
    bmark("wt2", "VEC", P0_V)
    bmark("wt2", "SCA", cnt["SCA"])

    if stop_after == "p0":
        prep.close()
        return finish()

    W1_DGP = qall("gpsimd")

    xors = meta["xors"]
    xpi = meta["xpi"]
    n_base = {}
    acc = 0
    for nm, st_n in CHAIN_LAYOUT:
        n_base[nm] = acc
        acc += st_n

    chst = {"upd": [0] * 32, "t": [0] * 8, "rd": [0] * 32}

    def build_chain(name):
        base, steps = n_base[name], dict(CHAIN_LAYOUT)[name]
        for si in range(steps):
            for c in range(32):
                n = base + si
                xp = xpat_t[xpi[xors[n]]]
                al = alpha_t[:, 32 * n + c:32 * n + c + 1]
                be = beta_t[:, 32 * n + c:32 * n + c + 1]
                cl = chainL[:, 128 * c:128 * (c + 1)]
                tsl = stile8[:, 128 * (c % 8):128 * (c % 8 + 1)]
                prev_upd = chst["upd"][c]
                if si == 0:
                    s1 = sop(lambda e, al=al, cl=cl: e.activation(
                        cl, k_ident[:, :], AF.Copy, bias=0.0, scale=al),
                        waits=[*CONSTW2, ("VEC", prev_upd),
                               ("TEN", chst["rd"][c])])
                    v1 = vop(lambda e, be=be, xp=xp, cl=cl:
                             e.scalar_tensor_tensor(cl, xp[:, :], be, cl,
                                                    ALU.mult, ALU.add),
                             waits=[("SCA", s1)])
                    chst["upd"][c] = v1
                else:
                    pst, pn = CH_PS[c % 8]
                    t_mm = mm(pst[:, 0:128], xp[:, :], cl, True, True,
                              waits=[("VEC", prev_upd)] + pw(pn), inc="TEN")
                    s1 = sop(lambda e, al=al, cl=cl, tsl=tsl: e.activation(
                        tsl, cl, AF.Copy, bias=0.0, scale=al),
                        waits=[("VEC", prev_upd),
                               ("VEC", chst["t"][c % 8])])
                    v1 = vop(lambda e, be=be, cl=cl, tsl=tsl, pst=pst:
                             e.scalar_tensor_tensor(cl, pst[:, 0:128], be,
                                                    tsl, ALU.mult,
                                                    ALU.add),
                             waits=[("TEN", t_mm), ("SCA", s1)])
                    chst["upd"][c] = v1
                    pnote(pn, "VEC", v1)
                    chst["t"][c % 8] = v1

    build_chain("A_up")
    for c in range(32):
        wt = [wtile, wtile2, wtile3][c % 3]
        ev_ = [evt, evt2, evt3][c % 3]
        wtn = f"wt{c % 3}"
        evn = f"ev{c % 3}"
        ldw = dma("scalar" if c % 2 == 0 else "sync",
                  wt[:, 0:C], scrA[128 * c:128 * (c + 1), :],
                  waits=W1_DGP + bw(wtn))
        bclr(wtn)
        t_bt = tp(ps_c[:, 0:1], k_brow[0:1, 128 * c:128 * (c + 1)],
                  k_onesr[0:1, 0:1],
                  waits=[*CONSTW] + pw("cps_c"))
        bv_ = vop(lambda e, wt=wt: e.tensor_copy(wt[:, C:C + 1], ps_c[:, 0:1]),
                  waits=[("TEN", t_bt), ldw])
        pnote("cps_c", "VEC", bv_)
        lhs = chainL[:, 128 * c:128 * (c + 1)]
        m1 = mm(ps_a[:, 0:512], lhs, wt[:, 0:512], True, True,
                waits=[("VEC", bv_)] + pw("cps_a"), inc="TEN")
        m2 = mm(ps_b[:, 0:512], lhs, wt[:, 512:1024], True, True,
                waits=[("VEC", bv_)] + pw("cps_b"), inc="TEN")
        m3 = mm(ps_d[:, 0:1], lhs, wt[:, C:C + 1], True, True,
                waits=[("VEC", bv_)] + pw("cps_d"), inc="TEN")
        chst["rd"][c] = m3
        bmark(wtn, "TEN", m3)
        e1 = vop(lambda e, ev_=ev_: e.tensor_copy(ev_[:, 0:512],
                                                  ps_a[:, 0:512]),
                 waits=[("TEN", m1)] + bw(evn))
        pnote("cps_a", "VEC", e1)
        e2 = vop(lambda e, ev_=ev_: e.tensor_copy(ev_[:, 512:1024],
                                                  ps_b[:, 0:512]),
                 waits=[("TEN", m2)])
        pnote("cps_b", "VEC", e2)
        e3 = vop(lambda e, ev_=ev_: e.tensor_copy(ev_[:, 1024:1025],
                                                  ps_d[:, 0:1]),
                 waits=[("TEN", m3)])
        pnote("cps_d", "VEC", e3)
        bclr(evn)
        od = dma("gpsimd", scrB[128 * c:128 * (c + 1), 0:C + 1],
                 ev_[:, 0:1025], waits=[("VEC", e3)])
        bmark(evn, od[0], od[1])
    F1_DGP = qall("gpsimd")

    build_chain("B_up")
    for c2 in range(32):
        wt = [wtile, wtile2, wtile3][c2 % 3]
        ev_ = [evt, evt2, evt3][c2 % 3]
        wtn = f"wt{c2 % 3}"
        evn = f"ev{c2 % 3}"
        src = scrB.rearrange("(hi gg) q -> hi gg q", gg=128)
        src = src[:, 4 * c2:4 * c2 + 4, :].rearrange("hi g q -> g hi q")
        ldw = dma("scalar" if c2 % 2 == 0 else "sync",
                  wt[:, 0:C + 1], src,
                  waits=F1_DGP + bw(wtn))
        bclr(wtn)
        lhs = chainL[:, 128 * c2:128 * (c2 + 1)]
        m1 = mm(ps_a[:, 0:512], lhs, wt[:, 0:512], True, True,
                waits=[ldw] + pw("cps_a"), inc="TEN")
        m2 = mm(ps_b[:, 0:512], lhs, wt[:, 512:1024], True, True,
                waits=[ldw] + pw("cps_b"), inc="TEN")
        m3 = mm(ps_d[:, 0:1], lhs, wt[:, C:C + 1], True, True,
                waits=[ldw] + pw("cps_d"), inc="TEN")
        chst["rd"][c2] = m3
        e1 = vop(lambda e, ev_=ev_: e.tensor_copy(ev_[:, 0:512],
                                                  ps_a[:, 0:512]),
                 waits=[("TEN", m1)] + bw(evn))
        bclr(evn)
        pnote("cps_a", "VEC", e1)
        e2 = vop(lambda e, ev_=ev_: e.tensor_copy(ev_[:, 512:1024],
                                                  ps_b[:, 0:512]),
                 waits=[("TEN", m2)])
        pnote("cps_b", "VEC", e2)
        e3 = vop(lambda e, c2=c2: e.tensor_copy(bprime[:, c2:c2 + 1],
                                                ps_d[:, 0:1]),
                 waits=[("TEN", m3)])
        pnote("cps_d", "VEC", e3)
        tv = None
        whl = whl0 if c2 % 2 == 0 else whl1
        for rc in range(NCC):
            pst, pn = CH_PS[(rc + 4 * (c2 % 2)) % 8]
            t_mm = tp(pst[:, 0:128], ev_[:, 128 * rc:128 * (rc + 1)],
                      k_ident[:, :],
                      waits=[("VEC", e2)] + pw(pn))
            bmark(evn, "TEN", t_mm)
            vop(lambda e, rc=rc, whl=whl, pst=pst: e.tensor_copy(
                whl[:, 128 * rc:128 * (rc + 1)], pst[:, 0:128]),
                waits=[("TEN", t_mm)] +
                (bw(f"whl{c2 % 2}") if rc == 0 else []))
            tv = vop(lambda e, rc=rc, whl=whl, pst=pst:
                     e.scalar_tensor_tensor(
                         whl[:, 1024 + 128 * rc:1024 + 128 * (rc + 1)],
                         whl[:, 128 * rc:128 * (rc + 1)], -1.0,
                         pst[:, 0:128], ALU.mult, ALU.add))
            pnote(pn, "VEC", tv)
        bclr(f"whl{c2 % 2}")
        od = dma("gpsimd",
                 scrWE[c2].rearrange("hl p a q -> p hl a q"),
                 whl[:, 0:2048].rearrange("p (hl a q) -> p hl a q",
                                          hl=2, a=NCC),
                 waits=[("VEC", tv)])
        bmark(f"whl{c2 % 2}", od[0], od[1])
    BPRIME_V = vop(lambda e: e.tensor_add(bprime[:, :], bprime[:, :],
                                          k_upb[:, :]),
                   waits=[*CONSTW])
    WE_DGP = qall("gpsimd")
    if "bprime" in tap_d:
        dma("gpsimd", tap_d["bprime"][:, :], bprime[:, :],
            waits=[("VEC", BPRIME_V)])

    build_chain("B_dnT")

    def down_fold(src_of, dst_of, split):
        for c in range(32):
            wt = [wtile, wtile2, wtile3][c % 3]
            ev_ = [evt, evt2, evt3][c % 3]
            whl = whl0 if c % 2 == 0 else whl1
            wtn = f"wt{c % 3}"
            evn = f"ev{c % 3}"
            ldw = dma("scalar" if c % 2 == 0 else "sync",
                      wt[:, 0:C], src_of(c), waits=bw(wtn))
            bclr(wtn)
            lhs = chainL[:, 128 * c:128 * (c + 1)]
            m1 = mm(ps_a[:, 0:512], lhs, wt[:, 0:512], True, True,
                    waits=[ldw] + pw("cps_a"), inc="TEN")
            m2 = mm(ps_b[:, 0:512], lhs, wt[:, 512:1024], True, True,
                    waits=[ldw] + pw("cps_b"), inc="TEN")
            chst["rd"][c] = m2
            bmark(wtn, "TEN", m2)
            if not split:
                e1 = vop(lambda e, ev_=ev_: e.tensor_copy(ev_[:, 0:512],
                                                          ps_a[:, 0:512]),
                         waits=[("TEN", m1)] + bw(evn))
                bclr(evn)
                pnote("cps_a", "VEC", e1)
                e2 = vop(lambda e, ev_=ev_: e.tensor_copy(ev_[:, 512:1024],
                                                          ps_b[:, 0:512]),
                         waits=[("TEN", m2)])
                pnote("cps_b", "VEC", e2)
                od = dma("gpsimd", dst_of(c), ev_[:, 0:1024],
                         waits=[("VEC", e2)])
                bmark(evn, od[0], od[1])
            else:
                vop(lambda e, whl=whl: e.tensor_copy(whl[:, 0:512],
                                                     ps_a[:, 0:512]),
                    waits=[("TEN", m1)] + bw(f"whl{c % 2}"))
                bclr(f"whl{c % 2}")
                v1 = vop(lambda e, whl=whl: e.scalar_tensor_tensor(
                    whl[:, 1024:1536], whl[:, 0:512], -1.0,
                    ps_a[:, 0:512], ALU.mult, ALU.add))
                pnote("cps_a", "VEC", v1)
                vop(lambda e, whl=whl: e.tensor_copy(whl[:, 512:1024],
                                                     ps_b[:, 0:512]),
                    waits=[("TEN", m2)])
                v2 = vop(lambda e, whl=whl: e.scalar_tensor_tensor(
                    whl[:, 1536:2048], whl[:, 512:1024], -1.0,
                    ps_b[:, 0:512], ALU.mult, ALU.add))
                pnote("cps_b", "VEC", v2)
                od = dma("gpsimd", dst_of(c),
                         whl[:, 0:2048].rearrange("p (hl q) -> p hl q", hl=2),
                         waits=[("VEC", v2)])
                bmark(f"whl{c % 2}", od[0], od[1])

    def l2_rows(t, c):
        s_ = t.rearrange("(hi gg) q -> hi gg q", gg=128)
        return s_[:, 4 * c:4 * c + 4, :].rearrange("hi g q -> g hi q")

    down_fold(lambda c: l2_rows(d["Wcon"], c), lambda c: l2_rows(scrV, c),
              False)
    V1_DGP = qall("gpsimd")

    build_chain("A_dnT")
    for s_, v_ in V1_DGP:
        _note(bnote, "wt0", s_, v_)
        _note(bnote, "wt1", s_, v_)
        _note(bnote, "wt2", s_, v_)
    down_fold(lambda c: scrV[128 * c:128 * (c + 1), :],
              lambda c: scrWC[128 * c:128 * (c + 1), :, :], True)
    WC_DGP = qall("gpsimd")
    S.barrier()
    prep.close()
    if stop_after == "p1":
        return finish()

    attn = contextlib.ExitStack()
    asb = mk_sb(attn)
    qh_sb = asb("qh_sb", (128, T), BF16)
    ql_sb = asb("ql_sb", (128, T), BF16)
    kh_sb = asb("kh_sb", (128, T), BF16)
    kl_sb = asb("kl_sb", (128, T), BF16)
    v_sb = asb("v_sb", (128, NT * 256))
    dec_hi = asb("dec_hi", (128, 4 * 512))
    dec_lo = asb("dec_lo", (128, 4 * 512), BF16)
    wst = asb("wst", (128, NCC * 128))
    wqh = asb("wqh", (128, NCC * 128), BF16)
    wql = asb("wql", (128, NCC * 128), BF16)
    wkh = asb("wkh", (128, NCC * 128), BF16)
    wkl = asb("wkl", (128, NCC * 128), BF16)
    wvh = asb("wvh", (128, NCC * 256), BF16)
    wvl = asb("wvl", (128, NCC * 256), BF16)
    woh, wol = wvh, wvl
    s_sb = [asb(f"s_sb{i}", (128, T)) for i in range(2)]
    sabs = asb("sabs", (128, T), BF16)
    ret_sb = asb("ret_sb", (128, T))
    g_sb = asb("g_sb", (128, T))
    gath = asb("gath", (128, 2 * T), BF16)
    gatl = asb("gatl", (128, 2 * T), BF16)

    P2_PS = [(ps_a, "ps_a"), (ps_b, "ps_b"), (ps_den, "den"),
             (ps_d, "ps_d"), (ps_ret[0], "ret0"), (ps_ret[1], "ret1"),
             (ps_ret[2], "ret2"), (ps_ret[3], "ret3")]
    att_prev_dgp = []
    for grp in range(p2_groups):
        h0 = 2 * grp

        dec_ld = {}

        def load_dec(which, tb):
            wi = 0 if which == "q" else 1
            s4 = 2 * wi + tb % 2
            hname, lname = ("etw", "etwl") if wi == 0 else ("esw", "eswl")
            w_ = bw(f"dec{s4}")
            bclr(f"dec{s4}")
            dec_ld[(which, tb)] = [
                dma("sync" if wi == 0 else "scalar",
                    dec_hi[:, 512 * s4:512 * (s4 + 1)],
                    d[hname][128 * grp:128 * (grp + 1),
                             512 * tb:512 * (tb + 1)], waits=w_),
                dma("scalar" if wi == 0 else "sync",
                    dec_lo[:, 512 * s4:512 * (s4 + 1)],
                    d[lname][128 * grp:128 * (grp + 1),
                             512 * tb:512 * (tb + 1)], waits=w_)]

        for tb0 in range(2):
            load_dec("q", tb0)
            load_dec("k", tb0)
        spl = {}
        for nm_, hib, lob, dsrc, cw in (
            ("wq", wqh, wql, d["Wqk"][:, 128 * grp:128 * (grp + 1)], 128),
            ("wk", wkh, wkl,
             d["Wqk"][:, C + 128 * grp:C + 128 * (grp + 1)], 128),
            ("wv", wvh, wvl, d["Wv"][:, 256 * grp:256 * (grp + 1)], 256),
        ):
            kc_half = 1024 // cw
            for half in range((cw * NCC) // 1024):
                off = 1024 * half
                wl_ = []
                lw = bw("wst")
                bclr("wst")
                for kc in range(kc_half):
                    kk = half * kc_half + kc
                    wl_.append(dma("sync" if kc % 2 == 0 else "scalar",
                                   wst[:, cw * kc:cw * (kc + 1)],
                                   dsrc[128 * kk:128 * (kk + 1), :],
                                   waits=lw))
                sh = sop(lambda e, h_=hib, off=off: e.copy(
                    h_[:, off:off + 1024], wst[:, 0:1024]),
                    waits=list(wl_) + bw(nm_ + "h"))
                sv = vop(lambda e, h_=hib, l_=lob, off=off:
                         e.scalar_tensor_tensor(
                             l_[:, off:off + 1024], h_[:, off:off + 1024],
                             -1.0, wst[:, 0:1024], ALU.mult, ALU.add),
                         waits=[("SCA", sh)])
                bmark("wst", "VEC", sv)
            bclr(nm_ + "h")
            spl[nm_] = sv

        qk_i = 0
        for tb in range(4):
            for which in ("q", "k"):
                whi = wqh if which == "q" else wkh
                wlo = wql if which == "q" else wkl
                pst, pn = P2_PS[qk_i % 8]
                qk_i += 1
                dh_ = qh_sb if which == "q" else kh_sb
                dl_ = ql_sb if which == "q" else kl_sb
                bias = k_bqkq[:, grp:grp + 1] if which == "q" \
                    else k_bqkk[:, grp:grp + 1]
                wi = 0 if which == "q" else 1
                s4 = 2 * wi + tb % 2
                first = True
                for hw, hx in ((0, 0), (0, 1), (1, 0)):
                    wt_ = whi if hw == 0 else wlo
                    xs_ = xnh_p if hx == 0 else xnl_p
                    for kc in range(NCC):
                        mm(pst[:, 0:512], wt_[:, 128 * kc:128 * (kc + 1)],
                           xs_[kc][:, 512 * tb:512 * (tb + 1)],
                           first, (hw, hx, kc) == (1, 0, NCC - 1),
                           waits=[("VEC", spl[
                               "wq" if which == "q" else "wk"])] +
                           (pw(pn) if first else []),
                           inc="TEN" if (hw, hx, kc) == (1, 0, NCC - 1)
                           else None)
                        first = False
                tmm = cnt["TEN"]
                
                bmark("wqh" if which == "q" else "wkh", "TEN", tmm)
                qf_st = s_sb[1][:, 1536:2048]
                sc2 = s_sb[1][:, 1024:1536]
                hi_ap = dec_hi[:, 512 * s4:512 * (s4 + 1)]
                lo_ap = dec_lo[:, 512 * s4:512 * (s4 + 1)]
                e0 = vop(lambda e, bias=bias, pst=pst, qf_st=qf_st:
                         e.tensor_scalar(qf_st[:, 0:512], pst[:, 0:512],
                                         bias, None, ALU.add),
                         waits=[("TEN", tmm)] + dec_ld[(which, tb)])
                pnote(pn, "VEC", e0)
                vop(lambda e, qf_st=qf_st, sc2=sc2, lo_ap=lo_ap:
                    e.tensor_mul(sc2[:, 0:512], qf_st[:, 0:512], lo_ap))
                vop(lambda e, qf_st=qf_st, hi_ap=hi_ap:
                    e.tensor_mul(qf_st[:, 0:512], qf_st[:, 0:512], hi_ap))
                e0d = vop(lambda e, qf_st=qf_st, sc2=sc2:
                          e.tensor_add(qf_st[:, 0:512], qf_st[:, 0:512],
                                       sc2[:, 0:512]))
                bmark(f"dec{s4}", "VEC", e0d)
                vop(lambda e, dh_=dh_, tb=tb, qf_st=qf_st: e.tensor_copy(
                    dh_[:, 512 * tb:512 * (tb + 1)], qf_st[:, 0:512]))
                vop(lambda e, dh_=dh_, dl_=dl_, tb=tb, qf_st=qf_st:
                    e.scalar_tensor_tensor(
                        dl_[:, 512 * tb:512 * (tb + 1)],
                        dh_[:, 512 * tb:512 * (tb + 1)], -1.0,
                        qf_st[:, 0:512], ALU.mult, ALU.add))
                if tb + 2 < 4:
                    load_dec(which, tb + 2)
        for sl in range(NT):
            pst, pn = P2_PS[sl % 8]
            first = True
            for hx, hw in ((0, 0), (0, 1), (1, 0)):
                xs_ = xnh_p if hx == 0 else xnl_p
                wt_ = wvh if hw == 0 else wvl
                for kc in range(NCC):
                    mm(pst[:, 0:256], xs_[kc][:, 128 * sl:128 * (sl + 1)],
                       wt_[:, 256 * kc:256 * (kc + 1)],
                       first, False,
                       waits=[("VEC", spl["wv"])] +
                       (pw(pn) if first else []))
                    first = False
            bmm = mm(pst[:, 0:256], k_onesr[0:1, :],
                     k_bv[0:1, 256 * grp:256 * (grp + 1)], False, True,
                     waits=[*CONSTW], inc="TEN")
            bmark("wvh", "TEN", bmm)
            ev = vop(lambda e, sl=sl, pst=pst: e.tensor_copy(
                v_sb[:, 256 * sl:256 * (sl + 1)], pst[:, 0:256]),
                waits=[("TEN", bmm)])
            pnote(pn, "VEC", ev)
        QKV_V = cnt["VEC"]

        gspl = []
        for hl2 in range(2):
            h2 = h0 + hl2
            g_hi = wqh if hl2 == 0 else wkh
            g_lo = wql if hl2 == 0 else wkl
            g_rn = "wqh" if hl2 == 0 else "wkh"
            wg_l = []
            wgw = bw("wst")
            bclr("wst")
            for kc in range(NCC):
                wg_l.append(dma("sync" if kc % 2 == 0 else "scalar",
                                wst[:, 128 * kc:128 * (kc + 1)],
                                d["Wg"][128 * kc:128 * (kc + 1),
                                        128 * h2:128 * (h2 + 1)],
                                waits=wgw))
            gsh = sop(lambda e, g_hi=g_hi: e.copy(g_hi[:, 0:1024],
                                                  wst[:, 0:1024]),
                      waits=list(wg_l) + bw(g_rn))
            gsv = vop(lambda e, g_hi=g_hi, g_lo=g_lo:
                      e.scalar_tensor_tensor(
                          g_lo[:, 0:1024], g_hi[:, 0:1024], -1.0,
                          wst[:, 0:1024], ALU.mult, ALU.add),
                      waits=[("SCA", gsh)])
            bclr(g_rn)
            bmark("wst", "VEC", gsv)
            gspl.append((g_hi, g_lo, g_rn, gsv))

        for hl2 in range(2):
            wow = bw("wst")
            bclr("wst")
            wo_ld = [dma("sync", wst[:, 0:512],
                         d["Wo"][128 * (h0 + hl2):128 * (h0 + hl2 + 1),
                                 0:512], waits=wow),
                     dma("scalar", wst[:, 512:1024],
                         d["Wo"][128 * (h0 + hl2):128 * (h0 + hl2 + 1),
                                 512:1024], waits=wow)]
            osh = sop(lambda e, hl2=hl2: e.copy(
                woh[:, C * hl2:C * (hl2 + 1)], wst[:, 0:1024]),
                waits=wo_ld + (bw("wvh") if hl2 == 0 else []))
            osv = vop(lambda e, hl2=hl2: e.scalar_tensor_tensor(
                wol[:, C * hl2:C * (hl2 + 1)], woh[:, C * hl2:C * (hl2 + 1)],
                -1.0, wst[:, 0:1024], ALU.mult, ALU.add),
                waits=[("SCA", osh)])
            bmark("wst", "VEC", osv)
        bclr("wvh")
        gate_done = [[None] * 4, [None] * 4]

        for hl in range(p2_heads):
            h = h0 + hl
            qof = 64 * hl
            den_q = []
            den_done = {}
            n_den = 0

            def flush_den(j_end):
                nonlocal n_den, last_den
                while den_q and den_q[0][0] < j_end:
                    jq, b, lo, hi, ab = den_q.pop(0)
                    last_den = mm(ps_den[0:8, lo - 512 * b:512],
                                  k_ind8d[:, 8 * b:8 * b + 8], sabs[:, lo:hi],
                                  n_den == 0, (jq, b) == (NT - 1, 3),
                                  waits=[("SCA", ab)] +
                                  (pw("den") if n_den == 0 else []),
                                  inc="TEN")
                    den_done[b] = last_den
                    n_den += 1

            sv_q = []
            sv_done = {}

            def flush_sv(j_end):
                nonlocal last_sv
                while sv_q and sv_q[0][0] < j_end:
                    jq, b, lo, hi, ev1, buf_ = sv_q.pop(0)
                    last_sv = mm(ps_ret[b][:, lo - 512 * b:512],
                                 v_sb[:, 256 * jq + 128 * hl:
                                      256 * jq + 128 * (hl + 1)],
                                 buf_[:, lo:hi], jq == 0, jq == 4 * b + 3,
                                 waits=[("VEC", ev1)] +
                                 (pw(f"ret{b}") if jq == 0 else []),
                                 inc="TEN")
                    sv_done[b] = last_sv

            last_den = None
            last_sv = None
            sc_i = 0
            for j in range(NT):
                flush_den(j)
                flush_sv(j)
                t0 = 128 * j
                buf = s_sb[j % 2]
                for b in range(j // 4, 4):
                    lo = max(t0, 512 * b)
                    hi = 512 * (b + 1)
                    pst, pn = [(ps_a, "ps_a"), (ps_b, "ps_b"),
                               (ps_d, "ps_d")][sc_i % 3]
                    sc_i += 1
                    smm = None
                    for sp, (kb, qb) in enumerate(
                            ((kh_sb, qh_sb), (kh_sb, ql_sb), (kl_sb, qh_sb))):
                        smm = mm(pst[:, 0:hi - lo],
                                 kb[qof:qof + 64, t0:t0 + 128],
                                 qb[qof:qof + 64, lo:hi], sp == 0, sp == 2,
                                 waits=[("VEC", QKV_V)] +
                                 (pw(pn) if sp == 0 else []),
                                 inc="TEN" if sp == 2 else None)
                    if lo == t0:
                        ev1 = vop(lambda e, buf=buf, lo=lo, pst=pst:
                                  e.tensor_mul(buf[:, lo:lo + 128],
                                               pst[:, 0:128], k_causal[:, :]),
                                  waits=[("TEN", smm)])
                        if hi > lo + 128:
                            ev1 = vop(lambda e, buf=buf, lo=lo, hi=hi, pst=pst:
                                      e.tensor_copy(buf[:, lo + 128:hi],
                                                    pst[:, 128:hi - lo]))
                    else:
                        ev1 = vop(lambda e, buf=buf, lo=lo, hi=hi, pst=pst:
                                  e.tensor_copy(buf[:, lo:hi],
                                                pst[:, 0:hi - lo]),
                                  waits=[("TEN", smm)])
                    pnote(pn, "VEC", ev1)
                    ab = sop(lambda e, buf=buf, lo=lo, hi=hi: e.activation(
                        sabs[:, lo:hi], buf[:, lo:hi], AF.Abs),
                        waits=[("VEC", ev1),
                               ("TEN", den_done.get(b, 0))])
                    den_q.append((j, b, lo, hi, ab))
                    sv_q.append((j, b, lo, hi, ev1, buf))
            flush_sv(NT)
            flush_den(NT)
            A_ = s_sb[1][0:4, 0:512]
            B_ = s_sb[1][0:4, 512:1024]
            Cs = s_sb[1][0:4, 1024:1536]
            phi = A_
            dv = vop(lambda e: e.tensor_copy(A_, ps_den[0:4, 0:512]),
                     waits=[("TEN", last_den)])
            pnote("den", "VEC", dv)
            ret2_sb = s_sb[0]
            for b in range(4):
                rev = vop(lambda e, b=b: e.tensor_copy(
                    ret_sb[:, 512 * b:512 * (b + 1)], ps_ret[b][:, 0:512]),
                    waits=[("TEN", sv_done[b])] +
                    (bw("ret_sb") if hl == 0 and b == 0 else []))
                if hl == 0 and b == 0:
                    bclr("ret_sb")
                pnote(f"ret{b}", "VEC", rev)
                sq = sop(lambda e, b=b, ret2_sb=ret2_sb: e.activation(
                    ret2_sb[:, 512 * b:512 * (b + 1)],
                    ret_sb[:, 512 * b:512 * (b + 1)], AF.Square),
                    waits=[("VEC", rev)] +
                    ([("TEN", last_sv)] if b == 0 else []))
                last_ms = mm(ps_den[0:36, 0:512],
                             k_ind8m[:, 36 * b:36 * b + 36],
                             ret2_sb[:, 512 * b:512 * (b + 1)],
                             b == 0, b == 3,
                             waits=[("SCA", sq)] +
                             ([("VEC", dv)] if b == 0 else []),
                             inc="TEN" if b == 3 else None)
            mv = vop(lambda e: e.tensor_copy(B_, ps_den[32:36, 0:512]),
                     waits=[("TEN", last_ms)])
            pnote("den", "VEC", mv)
            vop(lambda e: e.tensor_scalar(A_, A_, 1.0, 50000.0, ALU.max,
                                          ALU.min))
            vop(lambda e: e.tensor_mul(Cs, A_, A_))
            vop(lambda e: e.tensor_scalar(B_, B_, 1.0 / VD, None, ALU.mult))
            VE2 = vop(lambda e: e.scalar_tensor_tensor(
                Cs, Cs, EPS_RMS, B_, ALU.mult, ALU.add))
            SQ3 = sop(lambda e: e.activation(A_, Cs, AF.Sqrt),
                      waits=[("VEC", VE2)])
            vop(lambda e: e.reciprocal(B_, A_), waits=[("SCA", SQ3)])
            vop(lambda e: e.tensor_mul(A_, B_, B_))
            vop(lambda e: e.tensor_mul(A_, A_, Cs))
            vop(lambda e: e.tensor_scalar(A_, A_, -0.5, 1.5, ALU.mult,
                                          ALU.add))
            PHI_V = vop(lambda e: e.tensor_mul(A_, A_, B_))
            if h == 0:
                for nm, ap_ in (("phi0", phi), ("ret0", ret_sb[:, :])):
                    if nm in tap_d:
                        dma("gpsimd", tap_d[nm][:, :], ap_,
                            waits=[("VEC", PHI_V)])
            g_hi, g_lo, g_rn, gsv = gspl[hl]
            GP_PS = [(ps_d, "ps_d"), (ps_a, "ps_a"), (ps_b, "ps_b"),
                     (ps_den, "den")]
            for tb in range(4):
                gpst, gpn = GP_PS[tb]
                first = True
                for hw, hx in ((0, 0), (0, 1), (1, 0)):
                    wt_ = g_hi if hw == 0 else g_lo
                    xs_ = xnh_p if hx == 0 else xnl_p
                    for kc in range(NCC):
                        mm(gpst[:, 0:512], wt_[:, 128 * kc:128 * (kc + 1)],
                           xs_[kc][:, 512 * tb:512 * (tb + 1)],
                           first, (hw, hx, kc) == (1, 0, NCC - 1),
                           waits=[("VEC", gsv)] +
                           (pw(gpn) if first else []),
                           inc="TEN" if (hw, hx, kc) == (1, 0, NCC - 1)
                           else None)
                        first = False
                gmm = cnt["TEN"]
                bmark(g_rn, "TEN", gmm)
                ev = vop(lambda e, tb=tb, h=h, gpst=gpst: e.tensor_scalar(
                    g_sb[:, 512 * tb:512 * (tb + 1)], gpst[:, 0:512],
                    k_bg2[:, h:h + 1], None, ALU.add),
                    waits=[("TEN", gmm)] +
                    (bw("g_sb") if hl == 0 and tb == 0 else []))
                if hl == 0 and tb == 0:
                    bclr("g_sb")
                pnote(gpn, "VEC", ev)
            for b in range(4):
                gpst, gpn = GP_PS[b]
                pb = mm(gpst[:, 0:512], k_indT4[0:4, 128 * b:128 * (b + 1)],
                        phi, True, True,
                        waits=pw(gpn) + [("VEC", PHI_V), *CONSTW],
                        inc="TEN")
                gv = vop(lambda e, b=b, gpst=gpst: e.tensor_mul(
                    ret_sb[:, 512 * b:512 * (b + 1)],
                    ret_sb[:, 512 * b:512 * (b + 1)], gpst[:, 0:512]),
                    waits=[("TEN", pb)])
                pnote(gpn, "VEC", gv)
                vop(lambda e, b=b: e.tensor_mul(
                    ret_sb[:, 512 * b:512 * (b + 1)],
                    ret_sb[:, 512 * b:512 * (b + 1)],
                    g_sb[:, 512 * b:512 * (b + 1)]))
                vop(lambda e, hl=hl, b=b: e.tensor_copy(
                    gath[:, T * hl + 512 * b:T * hl + 512 * (b + 1)],
                    ret_sb[:, 512 * b:512 * (b + 1)]),
                    waits=bw("gat" + str(hl)) if b == 0 else [])
                vg = vop(lambda e, hl=hl, b=b: e.scalar_tensor_tensor(
                    gatl[:, T * hl + 512 * b:T * hl + 512 * (b + 1)],
                    gath[:, T * hl + 512 * b:T * hl + 512 * (b + 1)],
                    -1.0, ret_sb[:, 512 * b:512 * (b + 1)],
                    ALU.mult, ALU.add))
                gate_done[hl][b] = vg
            bclr("gat" + str(hl))
            if h == 0 and "gat0" in tap_d:
                dma("gpsimd", tap_d["gat0"][:, :], gath[:, 0:T],
                    waits=[("VEC", vg)])

        for ts in range(NT):
            for cb in range(2):
                pst = ps_a if cb == 0 else ps_b
                pn = "ps_a" if cb == 0 else "ps_b"
                first = True
                wmm = None
                gw = max(osv, gate_done[0][ts // 4], gate_done[1][ts // 4])
                for hg, hw in ((0, 0), (0, 1), (1, 0)):
                    gsrc = gath if hg == 0 else gatl
                    wsrc = woh if hw == 0 else wol
                    for hl in range(2):
                        last = (hg, hw, hl) == (1, 0, 1) and grp > 0
                        wmm = mm(
                            pst[:, 0:512],
                            gsrc[:, T * hl + 128 * ts:T * hl + 128 * (ts + 1)],
                            wsrc[:, C * hl + 512 * cb:C * hl + 512 * (cb + 1)],
                            first, last,
                            waits=[("VEC", gw)] + (pw(pn) if first else []),
                            inc="TEN" if last else None)
                        first = False
                if grp == 0:
                    wmm = mm(pst[:, 0:512], k_onesr[0:1, :],
                             k_bo[0:1, 512 * cb:512 * (cb + 1)], False, True,
                             waits=[*CONSTW], inc="TEN")
                bmark("wvh", "TEN", wmm)
                bmark("gat0", "TEN", wmm)
                bmark("gat1", "TEN", wmm)
                sl4 = ts % 4
                wbuf = ret_sb if sl4 < 2 else g_sb
                wrn = "ret_sb" if sl4 < 2 else "g_sb"
                off = 1024 * (sl4 % 2)
                ev = vop(lambda e, cb=cb, pst=pst, wbuf=wbuf, off=off:
                         e.tensor_copy(
                             wbuf[:, off + 512 * cb:off + 512 * (cb + 1)],
                             pst[:, 0:512]),
                         waits=[("TEN", wmm)] + bw(f"wo_dma{sl4}"))
                pnote(pn, "VEC", ev)
            bclr(f"wo_dma{sl4}")
            wod = dma("gpsimd", scrATT[128 * ts:128 * (ts + 1), :],
                      wbuf[:, off:off + C],
                      waits=[("VEC", cnt["VEC"])] + att_prev_dgp,
                      accum=(grp > 0))
            bmark(f"wo_dma{sl4}", wod[0], wod[1])
            bmark(wrn, wod[0], wod[1])
        att_prev_dgp = qall("gpsimd")
    ATT_DGP = qall("gpsimd")
    if "attn" in tap_d:
        for ts in range(NT):
            ldw = dma("scalar", ret_sb[:, 0:C],
                      scrATT[128 * ts:128 * (ts + 1), :],
                      waits=qall("gpsimd") + [("VEC", cnt["VEC"])])
            dma("gpsimd", tap_d["attn"][128 * ts:128 * (ts + 1), :],
                ret_sb[:, 0:C], waits=[ldw])

    S.barrier()
    attn.close()
    if stop_after == "p2":
        return finish()

    ffn = contextlib.ExitStack()
    fsb = mk_sb(ffn)
    h3h = fsb("h3h", (128, NCH * 512), BF16)
    h3l = fsb("h3l", (128, NCH * 512), BF16)
    h3f = fsb("h3f", (128, 2 * 512))
    we_t = fsb("we_t", (128, 3 * 2048), BF16)
    wc_t = fsb("wc_t", (128, 3 * 1024), BF16)
    x_res = fsb("x_res", (128, 2 * C))
    a_res = fsb("a_res", (128, 2 * C))
    out_st = fsb("out_st", (128, 2 * C))

    cb1_ps = [ps_a, ps_b, ps_c, ps_d]
    cb1_pn = ["ps_a", "ps_b", "ps_c", "ps_d"]
    for nm in ("we0", "we1", "wc0", "wc1"):
        bclr(nm)
    for tile in range(4):
        toff = 512 * tile
        for oc in range(NCH):
            slot = 2048 * (oc % 3)
            wl = dma("sync", we_t[:, slot:slot + 2048],
                     scrWE[oc].rearrange("hl p a q -> p hl a q"),
                     waits=WE_DGP + bw(f"we{oc % 3}"))
            bclr(f"we{oc % 3}")
            pst = ps_a if oc % 2 == 0 else ps_b
            pn = "ps_a" if oc % 2 == 0 else "ps_b"
            first = True
            for hl_w, hl_x in ((0, 0), (0, 1), (1, 0)):
                xsrc = xnh_p if hl_x == 0 else xnl_p
                for kc in range(NCC):
                    mm(pst[:, 0:512],
                       we_t[:, slot + 1024 * hl_w + 128 * kc:
                            slot + 1024 * hl_w + 128 * (kc + 1)],
                       xsrc[kc][:, toff:toff + 512],
                       first, (hl_w, hl_x, kc) == (1, 0, NCC - 1),
                       waits=[wl] + (pw(pn) if first else []),
                       inc="TEN" if (hl_w, hl_x, kc) == (1, 0, NCC - 1)
                       else None)
                    first = False
            emm = cnt["TEN"]
            bmark(f"we{oc % 3}", "TEN", emm)
            gl = sop(lambda e, oc=oc, pst=pst: e.activation(
                h3f[:, 512 * (oc % 2):512 * (oc % 2 + 1)], pst[:, 0:512],
                AF.Gelu, bias=bprime[:, oc:oc + 1], scale=1.0),
                waits=[("TEN", emm), ("VEC", BPRIME_V)] + bw(f"h3f{oc % 2}"))
            bclr(f"h3f{oc % 2}")
            pnote(pn, "SCA", gl)
            vop(lambda e, oc=oc: e.tensor_copy(
                h3h[:, 512 * oc:512 * (oc + 1)],
                h3f[:, 512 * (oc % 2):512 * (oc % 2 + 1)]),
                waits=[("SCA", gl)])
            hv = vop(lambda e, oc=oc: e.scalar_tensor_tensor(
                h3l[:, 512 * oc:512 * (oc + 1)],
                h3h[:, 512 * oc:512 * (oc + 1)], -1.0,
                h3f[:, 512 * (oc % 2):512 * (oc % 2 + 1)],
                ALU.mult, ALU.add))
            bmark(f"h3f{oc % 2}", "VEC", hv)
        H3_S = cnt["VEC"]
        if tile == 0 and "h3" in tap_d:
            pass
        def load_res(ts):
            rs = C * (ts % 2)
            dma("scalar", x_res[:, rs:rs + C],
                d["x"][toff + 128 * ts:toff + 128 * (ts + 1), :],
                waits=bw(f"res{ts % 2}"))
            al = dma("scalar", a_res[:, rs:rs + C],
                     scrATT[toff + 128 * ts:toff + 128 * (ts + 1), :],
                     waits=ATT_DGP + bw(f"res{ts % 2}"))
            bclr(f"res{ts % 2}")
            return al

        al_l = [load_res(0), load_res(1)]
        bmms = {}
        for cb in range(2):
            for c2 in range(NCH):
                slot = 1024 * (c2 % 3)
                srcb = scrWC.rearrange("(hi gg) hl q -> hi gg hl q", gg=128)
                srcb = srcb[:, 4 * c2:4 * c2 + 4, :, 512 * cb:512 * (cb + 1)]
                srcb = srcb.rearrange("hi g hl q -> g hi hl q")
                wlw = WC_DGP + bw(f"wc{c2 % 3}")
                wl0 = dma("sync", wc_t[:, slot:slot + 512], srcb[:, :, 0, :],
                          waits=wlw)
                wl = dma("sync", wc_t[:, slot + 512:slot + 1024],
                         srcb[:, :, 1, :], waits=wlw)
                bclr(f"wc{c2 % 3}")
                wm = None
                for hl_h, hl_w in ((0, 0), (0, 1), (1, 0)):
                    hsrc = h3h if hl_h == 0 else h3l
                    for ts in range(4):
                        pst = ps_ret[ts] if cb == 0 else cb1_ps[ts]
                        pn = f"ret{ts}" if cb == 0 else cb1_pn[ts]
                        wm = mm(pst[:, 0:512],
                                hsrc[:, 512 * c2 + 128 * ts:
                                     512 * c2 + 128 * (ts + 1)],
                                wc_t[:, slot + 512 * hl_w:
                                     slot + 512 * (hl_w + 1)],
                                c2 == 0 and (hl_h, hl_w) == (0, 0), False,
                                waits=[wl0, wl, ("VEC", H3_S)] +
                                (pw(pn) if c2 == 0 and
                                 (hl_h, hl_w) == (0, 0) else []),
                                inc="TEN" if (hl_h, hl_w, ts) == (1, 0, 3)
                                else None)
                bmark(f"wc{c2 % 3}", "TEN", wm)
            for ts in range(4):
                pst = ps_ret[ts] if cb == 0 else cb1_ps[ts]
                bmms[(cb, ts)] = mm(
                    pst[:, 0:512], k_onesr[0:1, :],
                    k_bfin[0:1, 512 * cb:512 * (cb + 1)], False, True,
                    waits=[("VEC", BFIN_V)], inc="TEN")
        for ts in range(4):
            slot = C * (ts % 2)
            for cb in range(2):
                pst = ps_ret[ts] if cb == 0 else cb1_ps[ts]
                pn = f"ret{ts}" if cb == 0 else cb1_pn[ts]
                rs = C * (ts % 2)
                ev = vop(lambda e, pst=pst, rs=rs, cb=cb, slot=slot:
                         e.scalar_tensor_tensor(
                             out_st[:, slot + 512 * cb:slot + 512 * (cb + 1)],
                             pst[:, 0:512], 1.0,
                             x_res[:, rs + 512 * cb:rs + 512 * (cb + 1)],
                             ALU.mult, ALU.add),
                         waits=[("TEN", bmms[(cb, ts)]), al_l[ts]]
                         + (bw(f"y{ts % 2}") if cb == 0 else []))
                pnote(pn, "VEC", ev)
                ev2 = vop(lambda e, rs=rs, cb=cb, slot=slot: e.tensor_add(
                    out_st[:, slot + 512 * cb:slot + 512 * (cb + 1)],
                    out_st[:, slot + 512 * cb:slot + 512 * (cb + 1)],
                    a_res[:, rs + 512 * cb:rs + 512 * (cb + 1)]))
                bmark(f"res{ts % 2}", "VEC", ev2)
            bclr(f"y{ts % 2}")
            yd = dma("gpsimd",
                     y[toff + 128 * ts:toff + 128 * (ts + 1), :],
                     out_st[:, slot:slot + C],
                     waits=[("VEC", cnt["VEC"])])
            bmark(f"y{ts % 2}", yd[0], yd[1])
            if ts + 2 < 4:
                al_l.append(load_res(ts + 2))

    ffn.close()
    return finish()



def kernel(**inputs):
    x_full = np.ascontiguousarray(np.asarray(inputs["L0_attn_x"], np.float32))
    B = x_full.shape[0]
    consts, meta = build_host_constants(inputs)
    nc = build_program(meta)
    base = {
        "Wqk": np.ascontiguousarray(np.asarray(inputs["L0_attn_W_qk"], np.float32)),
        "Wv": np.ascontiguousarray(np.asarray(inputs["L0_attn_W_v"], np.float32)),
        "Wg": np.ascontiguousarray(np.asarray(inputs["L0_attn_W_g"], np.float32)),
        "Wo": np.ascontiguousarray(np.asarray(inputs["L0_attn_W_o"], np.float32)),
        "Wexp": np.ascontiguousarray(np.asarray(inputs["L0_ffn_W_exp"], np.float32)),
        "Wcon": np.ascontiguousarray(np.asarray(inputs["L0_ffn_W_con"], np.float32)),
    }
    base.update({k: np.ascontiguousarray(v) for k, v in consts.items()})
    in_maps = []
    for b in range(B):
        m = dict(base)
        m["x"] = x_full[b]
        in_maps.append(m)
    res = run_bass_kernel_spmd(nc, in_maps, core_ids=list(range(B)))
    out = np.stack([np.asarray(r["y"]) for r in res.results], axis=0)
    return out.astype(np.float32)


# version 27
# speedup vs baseline: 1.0980x; 1.0088x over previous
import contextlib
import sys
import ml_dtypes
import numpy as np
import concourse.bass as bass
import concourse.mybir as mybir
from concourse.bass_utils import run_bass_kernel_spmd

F32 = mybir.dt.float32
BF16 = mybir.dt.bfloat16
ALU = mybir.AluOpType
AF = mybir.ActivationFunctionType
AX = mybir.AxisListType

T, C, H, KD, VBD, VD, DH = 2048, 1024, 16, 64, 2048, 128, 4096
NCC = C // 128
NCH = DH // 128
NT = T // 128
EPS_LN = 1e-5
EPS_RMS = float(np.finfo(np.float32).eps)

ENGINES = ("tensor", "vector", "scalar", "gpsimd", "sync")
NQCLS = 8
_QUEUES = ("sync", "scalar", "gpsimd")
SEMS = ("TEN", "VEC", "SCA", "GPS") + tuple(
    f"{q[:2].upper()}{i}" for q in _QUEUES for i in range(NQCLS))


class Sched:

    def __init__(self):
        self.items = {e: [] for e in ENGINES}
        self.count = {s: 0 for s in SEMS}
        self.pending_bar = {e: None for e in ENGINES}

    def barrier(self):
        snap = [(s, v) for s, v in self.count.items() if v > 0]
        for e in ENGINES:
            self.pending_bar[e] = snap

    _SELF = {"vector": "VEC", "scalar": "SCA", "gpsimd": "GPS"}

    def emit(self, eng, fn, waits=(), inc=None, inc_by=1):
        w = dict()
        if self.pending_bar[eng]:
            for s, v in self.pending_bar[eng]:
                w[s] = max(w.get(s, 0), v)
            self.pending_bar[eng] = None
        for s, v in waits:
            w[s] = max(w.get(s, 0), v)
        selfsem = self._SELF.get(eng)
        if selfsem is not None and inc == selfsem and self.count[selfsem] > 0:
            w[selfsem] = max(w.get(selfsem, 0), self.count[selfsem])
        self.items[eng].append((fn, sorted(w.items()), inc, inc_by))
        if inc is not None:
            self.count[inc] += inc_by
            return self.count[inc]
        return None

    def install(self, block, sems):
        for engname in ENGINES:
            items = self.items[engname]
            if not items:
                continue

            def body(e, items=items):
                for fn, waits, inc, inc_by in items:
                    for s, v in waits:
                        e.wait_ge(sems[s], v)
                    ins = fn(e)
                    if inc is not None:
                        ins.then_inc(sems[inc], inc_by)

            getattr(block, engname)(body)



def _decay_tables():
    h = np.arange(H, dtype=np.float64)
    gamma = np.log(1.0 - 2.0 ** (-5.0 - h))
    t = np.arange(T, dtype=np.float64)
    eg = np.exp(gamma)[:, None]
    ssum = (1.0 - eg ** (t[None, :] + 1.0)) / (1.0 - eg)
    inv_scale = 1.0 / np.sqrt(ssum)
    et = np.exp(gamma[:, None] * t[None, :]) * inv_scale
    es = np.exp(-gamma[:, None] * t[None, :])
    et64 = np.repeat(et, KD, axis=0)
    es64 = np.repeat(es, KD, axis=0)
    etw = et64.astype(np.float32)
    esw = es64.astype(np.float32)
    etwl = (et64 - etw.astype(np.float64)).astype(ml_dtypes.bfloat16)
    eswl = (es64 - esw.astype(np.float64)).astype(ml_dtypes.bfloat16)
    return etw, esw, etwl, eswl


def _stage_coeffs(wstages):
    n_st = wstages.shape[0]
    f = np.arange(DH)
    a = np.empty((n_st, DH), np.float32)
    bP = np.empty((n_st, DH), np.float32)
    bT = np.empty((n_st, DH), np.float32)
    for k in range(n_st):
        s = 1 << k
        pr = (f // (2 * s)) * s + (f % s)
        bit = (f >> k) & 1
        a[k] = wstages[k, pr, bit, bit]
        bP[k] = wstages[k, pr, bit, 1 - bit]
        bT[k] = bP[k][f ^ s]
    return a, bP, bT


def _f_of_l2():
    c2 = np.arange(32)[:, None]
    p2 = np.arange(128)[None, :]
    return 128 * (p2 % 32) + 4 * c2 + (p2 // 32)


def _chain_specs(inputs):
    ua, ubP, ubT = _stage_coeffs(np.asarray(inputs["L0_ffn_up_weights"], np.float32))
    da, dbP, dbT = _stage_coeffs(np.asarray(inputs["L0_ffn_down_weights"], np.float32))
    l1 = lambda v: v.reshape(32, 128)
    l2 = lambda v: v[_f_of_l2()]
    spec = []
    spec.append((l1(ua[6]), l1(ubT[6]), 64))
    for k in range(5, -1, -1):
        spec.append((l1(ua[k]), l1(ubT[k]), 1 << k))
    spec.append((l2(ua[11]), l2(ubT[11]), 16))
    for k in range(10, 6, -1):
        spec.append((l2(ua[k]), l2(ubT[k]), 1 << (k - 7)))
    spec.append((l2(da[7]), l2(dbP[7]), 1))
    for k in range(8, 12):
        spec.append((l2(da[k]), l2(dbP[k]), 1 << (k - 7)))
    spec.append((l1(da[0]), l1(dbP[0]), 1))
    for k in range(1, 7):
        spec.append((l1(da[k]), l1(dbP[k]), 1 << k))
    alpha = np.stack([s[0] for s in spec]).astype(np.float32)
    beta = np.stack([s[1] for s in spec]).astype(np.float32)
    xors = [s[2] for s in spec]
    return alpha, beta, xors


CHAIN_LAYOUT = (("A_up", 7), ("B_up", 5), ("B_dnT", 5), ("A_dnT", 7))


def build_host_constants(inputs):
    import ml_dtypes
    alpha, beta, xors = _chain_specs(inputs)
    etw, esw, etwl, eswl = _decay_tables()
    xs_list = (1, 2, 4, 8, 16, 32, 64)
    xpat = np.zeros((7, 128, 128), np.float32)
    i = np.arange(128)
    for xi, s in enumerate(xs_list):
        xpat[xi, i, i ^ s] = 1.0
    causal = (np.arange(128)[None, :] >= np.arange(128)[:, None]).astype(np.float32)
    ind4 = np.zeros((128, 16), np.float32)
    for b in range(4):
        ind4[:, 4 * b + b] = 1.0
    ind8d = np.zeros((128, 32), np.float32)
    ind8m = np.zeros((128, 4 * 36), np.float32)
    for b in range(4):
        ind8d[:, 8 * b + b] = 1.0
        ind8m[:, 36 * b + 32 + b] = 1.0
    indT4 = np.zeros((4, 4 * 128), np.float32)
    for b in range(4):
        indT4[b, 128 * b:128 * (b + 1)] = 1.0
    b_qk = np.asarray(inputs["L0_attn_b_qk"], np.float32)
    consts = {
        "alpha": np.ascontiguousarray(alpha.transpose(2, 0, 1).reshape(128, -1)),
        "beta": np.ascontiguousarray(beta.transpose(2, 0, 1).reshape(128, -1)),
        "etw": etw, "esw": esw, "etwl": etwl, "eswl": eswl,
        "xpat": xpat, "causal": causal,
        "ident": np.eye(128, dtype=np.float32),
        "onesr": np.ones((1, 128), np.float32),
        "ind4f": ind4,
        "ind4b": ind4.astype(ml_dtypes.bfloat16),
        "ind8m": ind8m,
        "indT4": indT4.T.copy(),
        "bqkq": b_qk[:C].reshape(8, 128).T.copy(),
        "bqkk": b_qk[C:].reshape(8, 128).T.copy(),
        "bg2": np.asarray(inputs["L0_attn_b_g"], np.float32).reshape(16, 128).T.copy(),
        "upb_l2": np.asarray(inputs["L0_ffn_up_bias"], np.float32)[_f_of_l2()].T.copy(),
        "bv": np.asarray(inputs["L0_attn_b_v"], np.float32).reshape(1, VBD),
        "bo": np.asarray(inputs["L0_attn_b_o"], np.float32).reshape(1, C),
        "bexp": np.asarray(inputs["L0_ffn_b_exp"], np.float32).reshape(1, DH),
        "bcon": np.asarray(inputs["L0_ffn_b_con"], np.float32).reshape(1, C),
        "dbias": np.asarray(inputs["L0_ffn_down_bias"], np.float32).reshape(1, C),
    }
    ind_t = np.zeros((4, 512), np.float32)
    for b in range(4):
        ind_t[b, 128 * b:128 * (b + 1)] = 1.0
    consts["indT4"] = ind_t
    consts["ind8d"] = ind8d.astype(ml_dtypes.bfloat16)
    xpi = {s: i for i, s in enumerate(xs_list)}
    meta = {"xors": xors, "xpi": xpi}
    return consts, meta



MM_LINES = []


def build_program(meta, taps=(), stop_after=None, p2_groups=8, p2_heads=2, p2_wo=True, p2_parts="sadvrgp"):
    nc = bass.Bass("TRN2", target_bir_lowering=False,
                   detect_race_conditions=False)
    S = Sched()
    cnt = S.count

    def din(name, shape, dt=F32):
        return nc.dram_tensor(name, list(shape), dt, kind="ExternalInput").ap()

    d = {}
    d["x"] = din("x", (T, C))
    d["Wqk"] = din("Wqk", (C, 2 * C))
    d["Wv"] = din("Wv", (C, VBD))
    d["Wg"] = din("Wg", (C, VBD))
    d["Wo"] = din("Wo", (VBD, C))
    d["Wexp"] = din("Wexp", (C, DH))
    d["Wcon"] = din("Wcon", (DH, C))
    for name, shape in (
        ("alpha", (128, 24 * 32)), ("beta", (128, 24 * 32)),
        ("etw", (C, T)), ("esw", (C, T)),
        ("xpat", (7, 128, 128)), ("causal", (128, 128)), ("ident", (128, 128)),
        ("onesr", (1, 128)), ("ind4f", (128, 16)), ("indT4", (4, 512)),
        ("bqkq", (128, 8)), ("bqkk", (128, 8)), ("bg2", (128, 16)),
        ("upb_l2", (128, 32)), ("bv", (1, VBD)), ("bo", (1, C)),
        ("bexp", (1, DH)), ("bcon", (1, C)), ("dbias", (1, C)),
    ):
        d[name] = din(name, shape)
    d["etwl"] = din("etwl", (C, T), BF16)
    d["eswl"] = din("eswl", (C, T), BF16)
    d["ind4b"] = din("ind4b", (128, 16), BF16)
    d["ind8d"] = din("ind8d", (128, 32), BF16)
    d["ind8m"] = din("ind8m", (128, 4 * 36))
    y = nc.dram_tensor("y", [T, C], F32, kind="ExternalOutput").ap()

    scrA = nc.dram_tensor("scrA", [DH, C], F32, kind="Internal").ap()
    scrB = nc.dram_tensor("scrB", [DH, C + 1], F32, kind="Internal").ap()
    scrWE = nc.dram_tensor("scrWE", [NCH, 2, 128, NCC, 128], BF16,
                           kind="Internal").ap()
    scrV = nc.dram_tensor("scrV", [DH, C], F32, kind="Internal").ap()
    scrWC = nc.dram_tensor("scrWC", [DH, 2, C], BF16,
                           kind="Internal").ap()
    scrATT = nc.dram_tensor("scrATT", [T, C], F32, kind="Internal").ap()

    tap_d = {}
    for tname, tshape in dict(
        bprime=(128, 32),
        q0=(128, T), k0=(128, T), v0=(128, NT * 256), s0=(128, T),
        g0=(128, T), retn0=(128, T),
        den0=(4, 512), phi0=(4, 512), ret0=(128, T), gat0=(128, T),
        attn=(T, C), h3=(128, NCH * 512),
    ).items():
        if tname in taps:
            tap_d[tname] = nc.dram_tensor("tap_" + tname, list(tshape), F32,
                                          kind="ExternalOutput").ap()

    root = contextlib.ExitStack()

    def mk_sb(st):
        return (lambda name, shape, dt=F32:
                st.enter_context(nc.sbuf_tensor(name, list(shape), dt)))

    sb = mk_sb(root)
    psum = lambda name, shape: root.enter_context(
        nc.psum_tensor(name, list(shape), F32))

    xnh_p = [sb(f"xnh{c}", (128, T), BF16) for c in range(NCC)]
    xnl_p = [sb(f"xnl{c}", (128, T), BF16) for c in range(NCC)]
    k_ident = sb("k_ident", (128, 128))
    k_causal = sb("k_causal", (128, 128))
    k_onesr = sb("k_onesr", (1, 128))
    k_ind4f = sb("k_ind4f", (128, 16))
    k_ind4b = sb("k_ind4b", (128, 16), BF16)
    k_ind8d = sb("k_ind8d", (128, 32), BF16)
    k_ind8m = sb("k_ind8m", (128, 4 * 36))
    k_indT4 = sb("k_indT4", (4, 512))
    k_bqkq = sb("k_bqkq", (128, 8))
    k_bqkk = sb("k_bqkk", (128, 8))
    k_bg2 = sb("k_bg2", (128, 16))
    k_upb = sb("k_upb", (128, 32))
    k_bv = sb("k_bv", (1, VBD))
    k_bo = sb("k_bo", (1, C))
    k_bfin = sb("k_bfin", (1, C))
    bprime = sb("bprime", (128, 32))

    ps_a = psum("ps_a", (128, 512))
    ps_b = psum("ps_b", (128, 512))
    ps_ret = [psum(f"ps_ret{i}", (128, 512)) for i in range(4)]
    ps_c = psum("ps_c", (128, 512))
    ps_d = psum("ps_d", (128, 512))
    ps_den = ps_c
    CH_PS = [(ps_a, "cps_a"), (ps_b, "cps_b"), (ps_c, "cps_c"),
             (ps_d, "cps_d"), (ps_ret[0], "ret0"), (ps_ret[1], "ret1"),
             (ps_ret[2], "ret2"), (ps_ret[3], "ret3")]
    ps_ms = ps_d

    sems = {s: root.enter_context(nc.semaphore(s)) for s in SEMS}

    def mm(ps_ap, lhsT, rhs, start, stop, waits=(), inc=None):
        MM_LINES.append((sys._getframe(1).f_lineno,
                         lhsT.dtype == F32 and rhs.dtype == F32))
        return S.emit("tensor",
                      lambda e, o=ps_ap, a=lhsT, b=rhs, st=start, sp=stop:
                      e.matmul(o, a, b, start=st, stop=sp),
                      waits=waits, inc=inc)

    def tp(ps_ap, in_ap, id_ap, waits=(), inc="TEN"):
        MM_LINES.append((-sys._getframe(1).f_lineno, False))
        return S.emit("tensor",
                      lambda e, o=ps_ap, i=in_ap, idm=id_ap:
                      e.transpose(o, i, idm),
                      waits=waits, inc=inc)

    qseq = {q: 0 for q in _QUEUES}

    def dma(queue, out, in_, waits=(), accum=False):
        if accum:
            fn = lambda e, o=out, i=in_: e.dma_start(out=o, in_=i,
                                                     accum_op=ALU.add)
        else:
            fn = lambda e, o=out, i=in_: e.dma_start(out=o, in_=i)
        n = qseq[queue]
        qseq[queue] += 1
        sem = f"{queue[:2].upper()}{n % NQCLS}"
        prev = 16 * (n // NQCLS)
        S.emit(queue, fn, waits=list(waits) + [(sem, prev)],
               inc=sem, inc_by=16)
        return (sem, prev + 16)

    def qall(queue):
        return [(f"{queue[:2].upper()}{i}",
                 cnt[f"{queue[:2].upper()}{i}"]) for i in range(NQCLS)]

    def vop(fn, waits=()):
        return S.emit("vector", fn, waits=waits, inc="VEC")

    def sop(fn, waits=()):
        return S.emit("scalar", fn, waits=waits, inc="SCA")

    def allw():
        return [(s, v) for s, v in cnt.items() if v > 0]

    pfree = {}
    bnote = {}

    def _note(dic, name, sem, val):
        d_ = dic.setdefault(name, {})
        d_[sem] = max(d_.get(sem, 0), val)

    def _w(dic, name):
        return list(dic.get(name, {}).items())

    pnote = lambda name, sem, val: _note(pfree, name, sem, val)
    bmark = lambda name, sem, val: _note(bnote, name, sem, val)
    pw = lambda name: _w(pfree, name)
    bw = lambda name: _w(bnote, name)

    def bclr(name):
        bnote.pop(name, None)

    def finish():
        S.emit("gpsimd", lambda e: e.nop(),
               waits=qall("gpsimd") + qall("sync") + qall("scalar"))
        with nc.Block() as block:
            S.install(block, sems)
        root.close()
        return nc

    for tile_ap, src in (
        (k_ident[:, :], d["ident"][:, :]), (k_causal[:, :], d["causal"][:, :]),
        (k_ind8d[:, :], d["ind8d"][:, :]), (k_ind8m[:, :], d["ind8m"][:, :]),
        (k_onesr[:, :], d["onesr"][:, :]), (k_ind4f[:, :], d["ind4f"][:, :]),
        (k_ind4b[:, :], d["ind4b"][:, :]), (k_indT4[:, :], d["indT4"][:, :]),
        (k_bqkq[:, :], d["bqkq"][:, :]), (k_bqkk[:, :], d["bqkk"][:, :]),
        (k_bg2[:, :], d["bg2"][:, :]), (k_upb[:, :], d["upb_l2"][:, :]),
        (k_bv[:, :], d["bv"][:, :]), (k_bo[:, :], d["bo"][:, :]),
        (k_bfin[:, :], d["bcon"][:, :]),
    ):
        dma("sync", tile_ap, src)
    CONSTW = qall("sync")

    prep = contextlib.ExitStack()
    psb = mk_sb(prep)
    wrow = psb("wrow", (128, DH))
    wrow2 = psb("wrow2", (128, DH))
    wtile = psb("wtile", (128, C + 1))
    wtile2 = psb("wtile2", (128, C + 1))
    wtile3 = psb("wtile3", (128, C + 1))
    evt3 = psb("evt3", (128, 1025))
    chainL = psb("chainL", (128, 32 * 128))
    stile8 = psb("stile8", (128, 1024))
    xpat_t = [psb(f"xpat{i}", (128, 128)) for i in range(7)]
    alpha_t = psb("alpha_t", (128, 24 * 32))
    beta_t = psb("beta_t", (128, 24 * 32))
    evt = psb("evt", (128, 1025))
    evt2 = psb("evt2", (128, 1025))
    whl0 = psb("whl0", (128, 2048), BF16)
    whl1 = psb("whl1", (128, 2048), BF16)
    k_brow = psb("k_brow", (1, DH))
    x_tok = psb("x_tok", (128, C))
    x_tok2 = wtile3[:, 0:C]
    x_sq = evt3[:, 0:1024]
    xn_st = evt2[:, 0:1024]
    ln_s = psb("ln_s", (128, 16))

    dma("sync", k_brow[:, :], d["bexp"][:, :])
    for i in range(7):
        dma("sync", xpat_t[i][:, :], d["xpat"][i, :, :])
    dma("sync", alpha_t[:, :], d["alpha"][:, :])
    dma("sync", beta_t[:, :], d["beta"][:, :])
    dma("sync", evt[0:1, 0:C], d["dbias"][:, :])
    CONSTW2 = qall("sync")
    BFIN_V = vop(lambda e: e.tensor_add(k_bfin[0:1, :], k_bfin[0:1, :],
                                        evt[0:1, 0:C]),
                 waits=[*CONSTW2])

    p0st = {"tp": 0}
    xcons = {0: (0, 0), 1: (0, 0)}

    def emit_p0_tile(ti):
        xt = x_tok if ti % 2 == 0 else x_tok2
        o = 8 * (ti % 2)
        pv, psc = xcons[ti % 2]
        ld = dma("sync" if ti % 2 == 0 else "scalar", xt[:, :],
                 d["x"][128 * ti:128 * (ti + 1), :],
                 waits=[("VEC", pv), ("SCA", psc)])
        SQ = sop(lambda e: e.activation(x_sq[:, :], xt[:, :], AF.Square,
                                        accum_out=ln_s[:, o + 1:o + 2]),
                 waits=[ld])
        vop(lambda e: e.tensor_reduce(ln_s[:, o:o + 1], xt[:, :], AX.X,
                                      ALU.add),
            waits=[ld])
        vop(lambda e: e.tensor_scalar(ln_s[:, o + 2:o + 3], ln_s[:, o:o + 1],
                                      1.0 / C, None, ALU.mult))
        vop(lambda e: e.tensor_mul(ln_s[:, o + 3:o + 4], ln_s[:, o + 2:o + 3],
                                   ln_s[:, o + 2:o + 3]))
        vop(lambda e: e.scalar_tensor_tensor(ln_s[:, o + 4:o + 5],
                                             ln_s[:, o + 1:o + 2],
                                             1.0 / C, ln_s[:, o + 3:o + 4],
                                             ALU.mult, ALU.subtract),
            waits=[("SCA", SQ)])
        vop(lambda e: e.tensor_scalar(ln_s[:, o + 4:o + 5],
                                      ln_s[:, o + 4:o + 5], EPS_LN,
                                      None, ALU.add))
        VE = cnt["VEC"]
        SQ2 = sop(lambda e: e.activation(ln_s[:, o + 5:o + 6],
                                         ln_s[:, o + 4:o + 5], AF.Sqrt),
                  waits=[("VEC", VE)])
        vop(lambda e: e.reciprocal(ln_s[:, o + 6:o + 7],
                                   ln_s[:, o + 5:o + 6]),
            waits=[("SCA", SQ2)])
        vop(lambda e: e.tensor_mul(ln_s[:, o + 7:o + 8], ln_s[:, o + 6:o + 7],
                                   ln_s[:, o + 6:o + 7]))
        vop(lambda e: e.tensor_mul(ln_s[:, o + 7:o + 8], ln_s[:, o + 7:o + 8],
                                   ln_s[:, o + 4:o + 5]))
        vop(lambda e: e.tensor_scalar(ln_s[:, o + 7:o + 8],
                                      ln_s[:, o + 7:o + 8], -0.5, 1.5,
                                      ALU.mult, ALU.add))
        vop(lambda e: e.tensor_mul(ln_s[:, o + 7:o + 8], ln_s[:, o + 7:o + 8],
                                   ln_s[:, o + 6:o + 7]))
        XN_V = vop(lambda e: e.tensor_scalar(xn_st[:, 0:C], xt[:, :],
                                             ln_s[:, o + 2:o + 3],
                                             ln_s[:, o + 7:o + 8],
                                             ALU.subtract, ALU.mult),
                   waits=[("TEN", p0st["tp"])])
        for c in range(NCC):
            pst, pn = CH_PS[c % 8]
            t_mm = tp(pst[:, 0:128], xn_st[:, 128 * c:128 * (c + 1)],
                      k_ident[:, :],
                      waits=[("VEC", XN_V)] + pw(pn) +
                      ([*CONSTW] if ti == 0 and c == 0 else []))
            p0st["tp"] = t_mm
            vop(lambda e, c=c, ti=ti, pst=pst: e.tensor_copy(
                xnh_p[c][:, 128 * ti:128 * (ti + 1)], pst[:, 0:128]),
                waits=[("TEN", t_mm)])
            lo_ = vop(lambda e, c=c, ti=ti, pst=pst: e.scalar_tensor_tensor(
                xnl_p[c][:, 128 * ti:128 * (ti + 1)],
                xnh_p[c][:, 128 * ti:128 * (ti + 1)], -1.0,
                pst[:, 0:128], ALU.mult, ALU.add))
            pnote(pn, "VEC", lo_)
        xcons[ti % 2] = (XN_V, SQ)

    w1_last = {(bf, ch): ("VEC", 0) for bf in range(2) for ch in range(4)}

    def emit_w1_rc(rc):
        wr_ = wrow if rc % 2 == 0 else wrow2
        ld_ch = []
        for ch in range(4):
            ld_ch.append(dma("sync" if ch % 2 == 0 else "scalar",
                             wr_[:, 1024 * ch:1024 * (ch + 1)],
                             d["Wexp"][128 * rc:128 * (rc + 1),
                                       1024 * ch:1024 * (ch + 1)],
                             waits=[w1_last[(rc % 2, ch)]]))
        for cc in range(NCH):
            pst, pn = CH_PS[cc % 8]
            t_mm = tp(pst[:, 0:128], wr_[:, 128 * cc:128 * (cc + 1)],
                      k_ident[:, :],
                      waits=[ld_ch[cc // 8]] + pw(pn))
            w_extra = qall("gpsimd") if cc % 8 == 0 else []
            ev_w = vop(lambda e, cc=cc, pst=pst: e.tensor_copy(
                evt[:, 128 * (cc % 8):128 * (cc % 8 + 1)], pst[:, 0:128]),
                waits=[("TEN", t_mm)] + w_extra)
            pnote(pn, "VEC", ev_w)
            if cc % 8 == 7:
                base = cc - 7
                w1_last[(rc % 2, cc // 8)] = ("TEN", t_mm)
                dst = scrA[:, 128 * rc:128 * (rc + 1)] \
                    .rearrange("(a p) q -> p a q", p=128)[:, base:cc + 1, :]
                dma("gpsimd", dst,
                    evt[:, 0:1024].rearrange("p (a q) -> p a q", q=128),
                    waits=[("VEC", cnt["VEC"])])

    for ti in range(NT):
        emit_p0_tile(ti)
        if ti % 2 == 1:
            emit_w1_rc(ti // 2)
    P0_V = cnt["VEC"]
    bmark("wt2", "VEC", P0_V)
    bmark("wt2", "SCA", cnt["SCA"])

    if stop_after == "p0":
        prep.close()
        return finish()

    W1_DGP = qall("gpsimd")

    xors = meta["xors"]
    xpi = meta["xpi"]
    n_base = {}
    acc = 0
    for nm, st_n in CHAIN_LAYOUT:
        n_base[nm] = acc
        acc += st_n

    chst = {"upd": [0] * 32, "t": [0] * 8, "rd": [0] * 32}

    def build_chain(name):
        base, steps = n_base[name], dict(CHAIN_LAYOUT)[name]
        for si in range(steps):
            for c in range(32):
                n = base + si
                xp = xpat_t[xpi[xors[n]]]
                al = alpha_t[:, 32 * n + c:32 * n + c + 1]
                be = beta_t[:, 32 * n + c:32 * n + c + 1]
                cl = chainL[:, 128 * c:128 * (c + 1)]
                tsl = stile8[:, 128 * (c % 8):128 * (c % 8 + 1)]
                prev_upd = chst["upd"][c]
                if si == 0:
                    s1 = sop(lambda e, al=al, cl=cl: e.activation(
                        cl, k_ident[:, :], AF.Copy, bias=0.0, scale=al),
                        waits=[*CONSTW2, ("VEC", prev_upd),
                               ("TEN", chst["rd"][c])])
                    v1 = vop(lambda e, be=be, xp=xp, cl=cl:
                             e.scalar_tensor_tensor(cl, xp[:, :], be, cl,
                                                    ALU.mult, ALU.add),
                             waits=[("SCA", s1)])
                    chst["upd"][c] = v1
                else:
                    pst, pn = CH_PS[c % 8]
                    t_mm = mm(pst[:, 0:128], xp[:, :], cl, True, True,
                              waits=[("VEC", prev_upd)] + pw(pn), inc="TEN")
                    s1 = sop(lambda e, al=al, cl=cl, tsl=tsl: e.activation(
                        tsl, cl, AF.Copy, bias=0.0, scale=al),
                        waits=[("VEC", prev_upd),
                               ("VEC", chst["t"][c % 8])])
                    v1 = vop(lambda e, be=be, cl=cl, tsl=tsl, pst=pst:
                             e.scalar_tensor_tensor(cl, pst[:, 0:128], be,
                                                    tsl, ALU.mult,
                                                    ALU.add),
                             waits=[("TEN", t_mm), ("SCA", s1)])
                    chst["upd"][c] = v1
                    pnote(pn, "VEC", v1)
                    chst["t"][c % 8] = v1

    build_chain("A_up")
    for c in range(32):
        wt = [wtile, wtile2, wtile3][c % 3]
        ev_ = [evt, evt2, evt3][c % 3]
        wtn = f"wt{c % 3}"
        evn = f"ev{c % 3}"
        ldw = dma("scalar" if c % 2 == 0 else "sync",
                  wt[:, 0:C], scrA[128 * c:128 * (c + 1), :],
                  waits=W1_DGP + bw(wtn))
        bclr(wtn)
        t_bt = tp(ps_c[:, 0:1], k_brow[0:1, 128 * c:128 * (c + 1)],
                  k_onesr[0:1, 0:1],
                  waits=[*CONSTW] + pw("cps_c"))
        bv_ = vop(lambda e, wt=wt: e.tensor_copy(wt[:, C:C + 1], ps_c[:, 0:1]),
                  waits=[("TEN", t_bt), ldw])
        pnote("cps_c", "VEC", bv_)
        lhs = chainL[:, 128 * c:128 * (c + 1)]
        m1 = mm(ps_a[:, 0:512], lhs, wt[:, 0:512], True, True,
                waits=[("VEC", bv_)] + pw("cps_a"), inc="TEN")
        m2 = mm(ps_b[:, 0:512], lhs, wt[:, 512:1024], True, True,
                waits=[("VEC", bv_)] + pw("cps_b"), inc="TEN")
        m3 = mm(ps_d[:, 0:1], lhs, wt[:, C:C + 1], True, True,
                waits=[("VEC", bv_)] + pw("cps_d"), inc="TEN")
        chst["rd"][c] = m3
        bmark(wtn, "TEN", m3)
        e1 = vop(lambda e, ev_=ev_: e.tensor_copy(ev_[:, 0:512],
                                                  ps_a[:, 0:512]),
                 waits=[("TEN", m1)] + bw(evn))
        pnote("cps_a", "VEC", e1)
        e2 = vop(lambda e, ev_=ev_: e.tensor_copy(ev_[:, 512:1024],
                                                  ps_b[:, 0:512]),
                 waits=[("TEN", m2)])
        pnote("cps_b", "VEC", e2)
        e3 = vop(lambda e, ev_=ev_: e.tensor_copy(ev_[:, 1024:1025],
                                                  ps_d[:, 0:1]),
                 waits=[("TEN", m3)])
        pnote("cps_d", "VEC", e3)
        bclr(evn)
        od = dma("gpsimd", scrB[128 * c:128 * (c + 1), 0:C + 1],
                 ev_[:, 0:1025], waits=[("VEC", e3)])
        bmark(evn, od[0], od[1])
    F1_DGP = qall("gpsimd")

    build_chain("B_up")
    for c2 in range(32):
        wt = [wtile, wtile2, wtile3][c2 % 3]
        ev_ = [evt, evt2, evt3][c2 % 3]
        wtn = f"wt{c2 % 3}"
        evn = f"ev{c2 % 3}"
        src = scrB.rearrange("(hi gg) q -> hi gg q", gg=128)
        src = src[:, 4 * c2:4 * c2 + 4, :].rearrange("hi g q -> g hi q")
        ldw = dma("scalar" if c2 % 2 == 0 else "sync",
                  wt[:, 0:C + 1], src,
                  waits=F1_DGP + bw(wtn))
        bclr(wtn)
        lhs = chainL[:, 128 * c2:128 * (c2 + 1)]
        m1 = mm(ps_a[:, 0:512], lhs, wt[:, 0:512], True, True,
                waits=[ldw] + pw("cps_a"), inc="TEN")
        m2 = mm(ps_b[:, 0:512], lhs, wt[:, 512:1024], True, True,
                waits=[ldw] + pw("cps_b"), inc="TEN")
        m3 = mm(ps_d[:, 0:1], lhs, wt[:, C:C + 1], True, True,
                waits=[ldw] + pw("cps_d"), inc="TEN")
        chst["rd"][c2] = m3
        e1 = vop(lambda e, ev_=ev_: e.tensor_copy(ev_[:, 0:512],
                                                  ps_a[:, 0:512]),
                 waits=[("TEN", m1)] + bw(evn))
        bclr(evn)
        pnote("cps_a", "VEC", e1)
        e2 = vop(lambda e, ev_=ev_: e.tensor_copy(ev_[:, 512:1024],
                                                  ps_b[:, 0:512]),
                 waits=[("TEN", m2)])
        pnote("cps_b", "VEC", e2)
        e3 = vop(lambda e, c2=c2: e.tensor_copy(bprime[:, c2:c2 + 1],
                                                ps_d[:, 0:1]),
                 waits=[("TEN", m3)])
        pnote("cps_d", "VEC", e3)
        tv = None
        whl = whl0 if c2 % 2 == 0 else whl1
        for rc in range(NCC):
            pst, pn = CH_PS[(rc + 4 * (c2 % 2)) % 8]
            t_mm = tp(pst[:, 0:128], ev_[:, 128 * rc:128 * (rc + 1)],
                      k_ident[:, :],
                      waits=[("VEC", e2)] + pw(pn))
            bmark(evn, "TEN", t_mm)
            vop(lambda e, rc=rc, whl=whl, pst=pst: e.tensor_copy(
                whl[:, 128 * rc:128 * (rc + 1)], pst[:, 0:128]),
                waits=[("TEN", t_mm)] +
                (bw(f"whl{c2 % 2}") if rc == 0 else []))
            tv = vop(lambda e, rc=rc, whl=whl, pst=pst:
                     e.scalar_tensor_tensor(
                         whl[:, 1024 + 128 * rc:1024 + 128 * (rc + 1)],
                         whl[:, 128 * rc:128 * (rc + 1)], -1.0,
                         pst[:, 0:128], ALU.mult, ALU.add))
            pnote(pn, "VEC", tv)
        bclr(f"whl{c2 % 2}")
        od = dma("gpsimd",
                 scrWE[c2].rearrange("hl p a q -> p hl a q"),
                 whl[:, 0:2048].rearrange("p (hl a q) -> p hl a q",
                                          hl=2, a=NCC),
                 waits=[("VEC", tv)])
        bmark(f"whl{c2 % 2}", od[0], od[1])
    BPRIME_V = vop(lambda e: e.tensor_add(bprime[:, :], bprime[:, :],
                                          k_upb[:, :]),
                   waits=[*CONSTW])
    WE_DGP = qall("gpsimd")
    if "bprime" in tap_d:
        dma("gpsimd", tap_d["bprime"][:, :], bprime[:, :],
            waits=[("VEC", BPRIME_V)])

    build_chain("B_dnT")

    def down_fold(src_of, dst_of, split):
        for c in range(32):
            wt = [wtile, wtile2, wtile3][c % 3]
            ev_ = [evt, evt2, evt3][c % 3]
            whl = whl0 if c % 2 == 0 else whl1
            wtn = f"wt{c % 3}"
            evn = f"ev{c % 3}"
            ldw = dma("scalar" if c % 2 == 0 else "sync",
                      wt[:, 0:C], src_of(c), waits=bw(wtn))
            bclr(wtn)
            lhs = chainL[:, 128 * c:128 * (c + 1)]
            m1 = mm(ps_a[:, 0:512], lhs, wt[:, 0:512], True, True,
                    waits=[ldw] + pw("cps_a"), inc="TEN")
            m2 = mm(ps_b[:, 0:512], lhs, wt[:, 512:1024], True, True,
                    waits=[ldw] + pw("cps_b"), inc="TEN")
            chst["rd"][c] = m2
            bmark(wtn, "TEN", m2)
            if not split:
                e1 = vop(lambda e, ev_=ev_: e.tensor_copy(ev_[:, 0:512],
                                                          ps_a[:, 0:512]),
                         waits=[("TEN", m1)] + bw(evn))
                bclr(evn)
                pnote("cps_a", "VEC", e1)
                e2 = vop(lambda e, ev_=ev_: e.tensor_copy(ev_[:, 512:1024],
                                                          ps_b[:, 0:512]),
                         waits=[("TEN", m2)])
                pnote("cps_b", "VEC", e2)
                od = dma("gpsimd", dst_of(c), ev_[:, 0:1024],
                         waits=[("VEC", e2)])
                bmark(evn, od[0], od[1])
            else:
                vop(lambda e, whl=whl: e.tensor_copy(whl[:, 0:512],
                                                     ps_a[:, 0:512]),
                    waits=[("TEN", m1)] + bw(f"whl{c % 2}"))
                bclr(f"whl{c % 2}")
                v1 = vop(lambda e, whl=whl: e.scalar_tensor_tensor(
                    whl[:, 1024:1536], whl[:, 0:512], -1.0,
                    ps_a[:, 0:512], ALU.mult, ALU.add))
                pnote("cps_a", "VEC", v1)
                vop(lambda e, whl=whl: e.tensor_copy(whl[:, 512:1024],
                                                     ps_b[:, 0:512]),
                    waits=[("TEN", m2)])
                v2 = vop(lambda e, whl=whl: e.scalar_tensor_tensor(
                    whl[:, 1536:2048], whl[:, 512:1024], -1.0,
                    ps_b[:, 0:512], ALU.mult, ALU.add))
                pnote("cps_b", "VEC", v2)
                od = dma("gpsimd", dst_of(c),
                         whl[:, 0:2048].rearrange("p (hl q) -> p hl q", hl=2),
                         waits=[("VEC", v2)])
                bmark(f"whl{c % 2}", od[0], od[1])

    def l2_rows(t, c):
        s_ = t.rearrange("(hi gg) q -> hi gg q", gg=128)
        return s_[:, 4 * c:4 * c + 4, :].rearrange("hi g q -> g hi q")

    down_fold(lambda c: l2_rows(d["Wcon"], c), lambda c: l2_rows(scrV, c),
              False)
    V1_DGP = qall("gpsimd")

    build_chain("A_dnT")
    for s_, v_ in V1_DGP:
        _note(bnote, "wt0", s_, v_)
        _note(bnote, "wt1", s_, v_)
        _note(bnote, "wt2", s_, v_)
    down_fold(lambda c: scrV[128 * c:128 * (c + 1), :],
              lambda c: scrWC[128 * c:128 * (c + 1), :, :], True)
    WC_DGP = qall("gpsimd")
    S.barrier()
    prep.close()
    if stop_after == "p1":
        return finish()

    attn = contextlib.ExitStack()
    asb = mk_sb(attn)
    qh_sb = asb("qh_sb", (128, T), BF16)
    ql_sb = asb("ql_sb", (128, T), BF16)
    kh_sb = asb("kh_sb", (128, T), BF16)
    kl_sb = asb("kl_sb", (128, T), BF16)
    v_sb = asb("v_sb", (128, NT * 256))
    dec_hi = asb("dec_hi", (128, 4 * 512))
    dec_lo = asb("dec_lo", (128, 4 * 512), BF16)
    wst = asb("wst", (128, NCC * 128))
    wqh = asb("wqh", (128, NCC * 128), BF16)
    wql = asb("wql", (128, NCC * 128), BF16)
    wkh = asb("wkh", (128, NCC * 128), BF16)
    wkl = asb("wkl", (128, NCC * 128), BF16)
    wvh = asb("wvh", (128, NCC * 256), BF16)
    wvl = asb("wvl", (128, NCC * 256), BF16)
    woh, wol = wvh, wvl
    s_sb = [asb(f"s_sb{i}", (128, T)) for i in range(2)]
    sabs = asb("sabs", (128, T), BF16)
    ret_sb = asb("ret_sb", (128, T))
    g_sb = asb("g_sb", (128, T))
    gath = asb("gath", (128, 2 * T), BF16)
    gatl = asb("gatl", (128, 2 * T), BF16)

    P2_PS = [(ps_a, "ps_a"), (ps_b, "ps_b"), (ps_den, "den"),
             (ps_d, "ps_d"), (ps_ret[0], "ret0"), (ps_ret[1], "ret1"),
             (ps_ret[2], "ret2"), (ps_ret[3], "ret3")]
    att_prev_dgp = []
    for grp in range(p2_groups):
        h0 = 2 * grp

        dec_ld = {}

        def load_dec(which, tb):
            wi = 0 if which == "q" else 1
            s4 = 2 * wi + tb % 2
            hname, lname = ("etw", "etwl") if wi == 0 else ("esw", "eswl")
            w_ = bw(f"dec{s4}")
            bclr(f"dec{s4}")
            dec_ld[(which, tb)] = [
                dma("sync" if wi == 0 else "scalar",
                    dec_hi[:, 512 * s4:512 * (s4 + 1)],
                    d[hname][128 * grp:128 * (grp + 1),
                             512 * tb:512 * (tb + 1)], waits=w_),
                dma("scalar" if wi == 0 else "sync",
                    dec_lo[:, 512 * s4:512 * (s4 + 1)],
                    d[lname][128 * grp:128 * (grp + 1),
                             512 * tb:512 * (tb + 1)], waits=w_)]

        for tb0 in range(2):
            load_dec("q", tb0)
            load_dec("k", tb0)
        spl = {}
        for nm_, hib, lob, dsrc, cw in (
            ("wq", wqh, wql, d["Wqk"][:, 128 * grp:128 * (grp + 1)], 128),
            ("wk", wkh, wkl,
             d["Wqk"][:, C + 128 * grp:C + 128 * (grp + 1)], 128),
            ("wv", wvh, wvl, d["Wv"][:, 256 * grp:256 * (grp + 1)], 256),
        ):
            kc_half = 1024 // cw
            for half in range((cw * NCC) // 1024):
                off = 1024 * half
                wl_ = []
                lw = bw("wst")
                bclr("wst")
                for kc in range(kc_half):
                    kk = half * kc_half + kc
                    wl_.append(dma("sync" if kc % 2 == 0 else "scalar",
                                   wst[:, cw * kc:cw * (kc + 1)],
                                   dsrc[128 * kk:128 * (kk + 1), :],
                                   waits=lw))
                sh = sop(lambda e, h_=hib, off=off: e.copy(
                    h_[:, off:off + 1024], wst[:, 0:1024]),
                    waits=list(wl_) + bw(nm_ + "h"))
                sv = vop(lambda e, h_=hib, l_=lob, off=off:
                         e.scalar_tensor_tensor(
                             l_[:, off:off + 1024], h_[:, off:off + 1024],
                             -1.0, wst[:, 0:1024], ALU.mult, ALU.add),
                         waits=[("SCA", sh)])
                bmark("wst", "VEC", sv)
            bclr(nm_ + "h")
            spl[nm_] = sv

        qk_i = 0
        for tb in range(4):
            for which in ("q", "k"):
                whi = wqh if which == "q" else wkh
                wlo = wql if which == "q" else wkl
                pst, pn = P2_PS[qk_i % 8]
                qk_i += 1
                dh_ = qh_sb if which == "q" else kh_sb
                dl_ = ql_sb if which == "q" else kl_sb
                bias = k_bqkq[:, grp:grp + 1] if which == "q" \
                    else k_bqkk[:, grp:grp + 1]
                wi = 0 if which == "q" else 1
                s4 = 2 * wi + tb % 2
                first = True
                for hw, hx in ((0, 0), (0, 1), (1, 0)):
                    wt_ = whi if hw == 0 else wlo
                    xs_ = xnh_p if hx == 0 else xnl_p
                    for kc in range(NCC):
                        mm(pst[:, 0:512], wt_[:, 128 * kc:128 * (kc + 1)],
                           xs_[kc][:, 512 * tb:512 * (tb + 1)],
                           first, (hw, hx, kc) == (1, 0, NCC - 1),
                           waits=[("VEC", spl[
                               "wq" if which == "q" else "wk"])] +
                           (pw(pn) if first else []),
                           inc="TEN" if (hw, hx, kc) == (1, 0, NCC - 1)
                           else None)
                        first = False
                tmm = cnt["TEN"]
                
                bmark("wqh" if which == "q" else "wkh", "TEN", tmm)
                qf_st = s_sb[1][:, 1536:2048]
                sc2 = s_sb[1][:, 1024:1536]
                hi_ap = dec_hi[:, 512 * s4:512 * (s4 + 1)]
                lo_ap = dec_lo[:, 512 * s4:512 * (s4 + 1)]
                e0 = vop(lambda e, bias=bias, pst=pst, qf_st=qf_st:
                         e.tensor_scalar(qf_st[:, 0:512], pst[:, 0:512],
                                         bias, None, ALU.add),
                         waits=[("TEN", tmm)] + dec_ld[(which, tb)])
                pnote(pn, "VEC", e0)
                vop(lambda e, qf_st=qf_st, sc2=sc2, lo_ap=lo_ap:
                    e.tensor_mul(sc2[:, 0:512], qf_st[:, 0:512], lo_ap))
                vop(lambda e, qf_st=qf_st, hi_ap=hi_ap:
                    e.tensor_mul(qf_st[:, 0:512], qf_st[:, 0:512], hi_ap))
                e0d = vop(lambda e, qf_st=qf_st, sc2=sc2:
                          e.tensor_add(qf_st[:, 0:512], qf_st[:, 0:512],
                                       sc2[:, 0:512]))
                bmark(f"dec{s4}", "VEC", e0d)
                vop(lambda e, dh_=dh_, tb=tb, qf_st=qf_st: e.tensor_copy(
                    dh_[:, 512 * tb:512 * (tb + 1)], qf_st[:, 0:512]))
                vop(lambda e, dh_=dh_, dl_=dl_, tb=tb, qf_st=qf_st:
                    e.scalar_tensor_tensor(
                        dl_[:, 512 * tb:512 * (tb + 1)],
                        dh_[:, 512 * tb:512 * (tb + 1)], -1.0,
                        qf_st[:, 0:512], ALU.mult, ALU.add))
                if tb + 2 < 4:
                    load_dec(which, tb + 2)
        for sl in range(NT):
            pst, pn = P2_PS[sl % 8]
            first = True
            for hx, hw in ((0, 0), (0, 1), (1, 0)):
                xs_ = xnh_p if hx == 0 else xnl_p
                wt_ = wvh if hw == 0 else wvl
                for kc in range(NCC):
                    mm(pst[:, 0:256], xs_[kc][:, 128 * sl:128 * (sl + 1)],
                       wt_[:, 256 * kc:256 * (kc + 1)],
                       first, False,
                       waits=[("VEC", spl["wv"])] +
                       (pw(pn) if first else []))
                    first = False
            bmm = mm(pst[:, 0:256], k_onesr[0:1, :],
                     k_bv[0:1, 256 * grp:256 * (grp + 1)], False, True,
                     waits=[*CONSTW], inc="TEN")
            bmark("wvh", "TEN", bmm)
            ev = vop(lambda e, sl=sl, pst=pst: e.tensor_copy(
                v_sb[:, 256 * sl:256 * (sl + 1)], pst[:, 0:256]),
                waits=[("TEN", bmm)])
            pnote(pn, "VEC", ev)
        QKV_V = cnt["VEC"]

        gspl = []
        for hl2 in range(2):
            h2 = h0 + hl2
            g_hi = wqh if hl2 == 0 else wkh
            g_lo = wql if hl2 == 0 else wkl
            g_rn = "wqh" if hl2 == 0 else "wkh"
            wg_l = []
            wgw = bw("wst")
            bclr("wst")
            for kc in range(NCC):
                wg_l.append(dma("sync" if kc % 2 == 0 else "scalar",
                                wst[:, 128 * kc:128 * (kc + 1)],
                                d["Wg"][128 * kc:128 * (kc + 1),
                                        128 * h2:128 * (h2 + 1)],
                                waits=wgw))
            gsh = sop(lambda e, g_hi=g_hi: e.copy(g_hi[:, 0:1024],
                                                  wst[:, 0:1024]),
                      waits=list(wg_l) + bw(g_rn))
            gsv = vop(lambda e, g_hi=g_hi, g_lo=g_lo:
                      e.scalar_tensor_tensor(
                          g_lo[:, 0:1024], g_hi[:, 0:1024], -1.0,
                          wst[:, 0:1024], ALU.mult, ALU.add),
                      waits=[("SCA", gsh)])
            bclr(g_rn)
            bmark("wst", "VEC", gsv)
            gspl.append((g_hi, g_lo, g_rn, gsv))

        for hl2 in range(2):
            wow = bw("wst")
            bclr("wst")
            wo_ld = [dma("sync", wst[:, 0:512],
                         d["Wo"][128 * (h0 + hl2):128 * (h0 + hl2 + 1),
                                 0:512], waits=wow),
                     dma("scalar", wst[:, 512:1024],
                         d["Wo"][128 * (h0 + hl2):128 * (h0 + hl2 + 1),
                                 512:1024], waits=wow)]
            osh = sop(lambda e, hl2=hl2: e.copy(
                woh[:, C * hl2:C * (hl2 + 1)], wst[:, 0:1024]),
                waits=wo_ld + (bw("wvh") if hl2 == 0 else []))
            osv = vop(lambda e, hl2=hl2: e.scalar_tensor_tensor(
                wol[:, C * hl2:C * (hl2 + 1)], woh[:, C * hl2:C * (hl2 + 1)],
                -1.0, wst[:, 0:1024], ALU.mult, ALU.add),
                waits=[("SCA", osh)])
            bmark("wst", "VEC", osv)
        bclr("wvh")
        gate_done = [[None] * 4, [None] * 4]

        for hl in range(p2_heads):
            h = h0 + hl
            qof = 64 * hl
            den_q = []
            den_done = {}
            n_den = 0

            def flush_den(j_end):
                nonlocal n_den, last_den
                while den_q and den_q[0][0] < j_end:
                    jq, b, lo, hi, ab = den_q.pop(0)
                    last_den = mm(ps_den[0:8, lo - 512 * b:512],
                                  k_ind8d[:, 8 * b:8 * b + 8], sabs[:, lo:hi],
                                  n_den == 0, (jq, b) == (NT - 1, 3),
                                  waits=[("SCA", ab)] +
                                  (pw("den") if n_den == 0 else []),
                                  inc="TEN")
                    den_done[b] = last_den
                    n_den += 1

            sv_q = []
            sv_done = {}

            def flush_sv(j_end):
                nonlocal last_sv
                while sv_q and sv_q[0][0] < j_end:
                    jq, b, lo, hi, ev1, buf_ = sv_q.pop(0)
                    last_sv = mm(ps_ret[b][:, lo - 512 * b:512],
                                 v_sb[:, 256 * jq + 128 * hl:
                                      256 * jq + 128 * (hl + 1)],
                                 buf_[:, lo:hi], jq == 0, jq == 4 * b + 3,
                                 waits=[("VEC", ev1)] +
                                 (pw(f"ret{b}") if jq == 0 else []),
                                 inc="TEN")
                    sv_done[b] = last_sv

            last_den = None
            last_sv = None
            sc_i = 0
            for j in range(NT):
                flush_den(j)
                flush_sv(j)
                t0 = 128 * j
                buf = s_sb[j % 2]
                for b in range(j // 4, 4):
                    lo = max(t0, 512 * b)
                    hi = 512 * (b + 1)
                    pst, pn = [(ps_a, "ps_a"), (ps_b, "ps_b"),
                               (ps_d, "ps_d")][sc_i % 3]
                    sc_i += 1
                    smm = None
                    for sp, (kb, qb) in enumerate(
                            ((kh_sb, qh_sb), (kh_sb, ql_sb), (kl_sb, qh_sb))):
                        smm = mm(pst[:, 0:hi - lo],
                                 kb[qof:qof + 64, t0:t0 + 128],
                                 qb[qof:qof + 64, lo:hi], sp == 0, sp == 2,
                                 waits=[("VEC", QKV_V)] +
                                 (pw(pn) if sp == 0 else []),
                                 inc="TEN" if sp == 2 else None)
                    if lo == t0:
                        ev1 = vop(lambda e, buf=buf, lo=lo, pst=pst:
                                  e.tensor_mul(buf[:, lo:lo + 128],
                                               pst[:, 0:128], k_causal[:, :]),
                                  waits=[("TEN", smm)])
                        if hi > lo + 128:
                            ev1 = vop(lambda e, buf=buf, lo=lo, hi=hi, pst=pst:
                                      e.tensor_copy(buf[:, lo + 128:hi],
                                                    pst[:, 128:hi - lo]))
                    else:
                        ev1 = vop(lambda e, buf=buf, lo=lo, hi=hi, pst=pst:
                                  e.tensor_copy(buf[:, lo:hi],
                                                pst[:, 0:hi - lo]),
                                  waits=[("TEN", smm)])
                    pnote(pn, "VEC", ev1)
                    ab = sop(lambda e, buf=buf, lo=lo, hi=hi: e.activation(
                        sabs[:, lo:hi], buf[:, lo:hi], AF.Abs),
                        waits=[("VEC", ev1),
                               ("TEN", den_done.get(b, 0))])
                    den_q.append((j, b, lo, hi, ab))
                    sv_q.append((j, b, lo, hi, ev1, buf))
            flush_sv(NT)
            flush_den(NT)
            A_ = s_sb[1][0:4, 0:512]
            B_ = s_sb[1][0:4, 512:1024]
            Cs = s_sb[1][0:4, 1024:1536]
            phi = A_
            dv = vop(lambda e: e.tensor_copy(A_, ps_den[0:4, 0:512]),
                     waits=[("TEN", last_den)])
            pnote("den", "VEC", dv)
            ret2_sb = s_sb[0]
            for b in range(4):
                rev = vop(lambda e, b=b: e.tensor_copy(
                    ret_sb[:, 512 * b:512 * (b + 1)], ps_ret[b][:, 0:512]),
                    waits=[("TEN", sv_done[b])] +
                    (bw("ret_sb") if hl == 0 and b == 0 else []))
                if hl == 0 and b == 0:
                    bclr("ret_sb")
                pnote(f"ret{b}", "VEC", rev)
                sq = sop(lambda e, b=b, ret2_sb=ret2_sb: e.activation(
                    ret2_sb[:, 512 * b:512 * (b + 1)],
                    ret_sb[:, 512 * b:512 * (b + 1)], AF.Square),
                    waits=[("VEC", rev)] +
                    ([("TEN", last_sv)] if b == 0 else []))
                last_ms = mm(ps_den[0:36, 0:512],
                             k_ind8m[:, 36 * b:36 * b + 36],
                             ret2_sb[:, 512 * b:512 * (b + 1)],
                             b == 0, b == 3,
                             waits=[("SCA", sq)] +
                             ([("VEC", dv)] if b == 0 else []),
                             inc="TEN" if b == 3 else None)
            mv = vop(lambda e: e.tensor_copy(B_, ps_den[32:36, 0:512]),
                     waits=[("TEN", last_ms)])
            pnote("den", "VEC", mv)
            vop(lambda e: e.tensor_scalar(A_, A_, 1.0, 50000.0, ALU.max,
                                          ALU.min))
            vop(lambda e: e.tensor_mul(Cs, A_, A_))
            vop(lambda e: e.tensor_scalar(B_, B_, 1.0 / VD, None, ALU.mult))
            VE2 = vop(lambda e: e.scalar_tensor_tensor(
                Cs, Cs, EPS_RMS, B_, ALU.mult, ALU.add))
            SQ3 = sop(lambda e: e.activation(A_, Cs, AF.Sqrt),
                      waits=[("VEC", VE2)])
            vop(lambda e: e.reciprocal(B_, A_), waits=[("SCA", SQ3)])
            vop(lambda e: e.tensor_mul(A_, B_, B_))
            vop(lambda e: e.tensor_mul(A_, A_, Cs))
            vop(lambda e: e.tensor_scalar(A_, A_, -0.5, 1.5, ALU.mult,
                                          ALU.add))
            PHI_V = vop(lambda e: e.tensor_mul(A_, A_, B_))
            if h == 0:
                for nm, ap_ in (("phi0", phi), ("ret0", ret_sb[:, :])):
                    if nm in tap_d:
                        dma("gpsimd", tap_d[nm][:, :], ap_,
                            waits=[("VEC", PHI_V)])
            g_hi, g_lo, g_rn, gsv = gspl[hl]
            GP_PS = [(ps_d, "ps_d"), (ps_a, "ps_a"), (ps_b, "ps_b"),
                     (ps_den, "den")]
            for tb in range(4):
                gpst, gpn = GP_PS[tb]
                first = True
                for hw, hx in ((0, 0), (0, 1), (1, 0)):
                    wt_ = g_hi if hw == 0 else g_lo
                    xs_ = xnh_p if hx == 0 else xnl_p
                    for kc in range(NCC):
                        mm(gpst[:, 0:512], wt_[:, 128 * kc:128 * (kc + 1)],
                           xs_[kc][:, 512 * tb:512 * (tb + 1)],
                           first, (hw, hx, kc) == (1, 0, NCC - 1),
                           waits=[("VEC", gsv)] +
                           (pw(gpn) if first else []),
                           inc="TEN" if (hw, hx, kc) == (1, 0, NCC - 1)
                           else None)
                        first = False
                gmm = cnt["TEN"]
                bmark(g_rn, "TEN", gmm)
                ev = vop(lambda e, tb=tb, h=h, gpst=gpst: e.tensor_scalar(
                    g_sb[:, 512 * tb:512 * (tb + 1)], gpst[:, 0:512],
                    k_bg2[:, h:h + 1], None, ALU.add),
                    waits=[("TEN", gmm)] +
                    (bw("g_sb") if hl == 0 and tb == 0 else []))
                if hl == 0 and tb == 0:
                    bclr("g_sb")
                pnote(gpn, "VEC", ev)
            for b in range(4):
                gpst, gpn = GP_PS[b]
                pb = mm(gpst[:, 0:512], k_indT4[0:4, 128 * b:128 * (b + 1)],
                        phi, True, True,
                        waits=pw(gpn) + [("VEC", PHI_V), *CONSTW],
                        inc="TEN")
                gv = vop(lambda e, b=b, gpst=gpst: e.tensor_mul(
                    ret_sb[:, 512 * b:512 * (b + 1)],
                    ret_sb[:, 512 * b:512 * (b + 1)], gpst[:, 0:512]),
                    waits=[("TEN", pb)])
                pnote(gpn, "VEC", gv)
                vop(lambda e, b=b: e.tensor_mul(
                    ret_sb[:, 512 * b:512 * (b + 1)],
                    ret_sb[:, 512 * b:512 * (b + 1)],
                    g_sb[:, 512 * b:512 * (b + 1)]))
                vop(lambda e, hl=hl, b=b: e.tensor_copy(
                    gath[:, T * hl + 512 * b:T * hl + 512 * (b + 1)],
                    ret_sb[:, 512 * b:512 * (b + 1)]),
                    waits=bw("gat" + str(hl)) if b == 0 else [])
                vg = vop(lambda e, hl=hl, b=b: e.scalar_tensor_tensor(
                    gatl[:, T * hl + 512 * b:T * hl + 512 * (b + 1)],
                    gath[:, T * hl + 512 * b:T * hl + 512 * (b + 1)],
                    -1.0, ret_sb[:, 512 * b:512 * (b + 1)],
                    ALU.mult, ALU.add))
                gate_done[hl][b] = vg
            bclr("gat" + str(hl))
            if h == 0 and "gat0" in tap_d:
                dma("gpsimd", tap_d["gat0"][:, :], gath[:, 0:T],
                    waits=[("VEC", vg)])

        for ts in range(NT):
            for cb in range(2):
                pst = ps_a if cb == 0 else ps_b
                pn = "ps_a" if cb == 0 else "ps_b"
                first = True
                wmm = None
                gw = max(osv, gate_done[0][ts // 4], gate_done[1][ts // 4])
                for hg, hw in ((0, 0), (0, 1), (1, 0)):
                    gsrc = gath if hg == 0 else gatl
                    wsrc = woh if hw == 0 else wol
                    for hl in range(2):
                        last = (hg, hw, hl) == (1, 0, 1) and grp > 0
                        wmm = mm(
                            pst[:, 0:512],
                            gsrc[:, T * hl + 128 * ts:T * hl + 128 * (ts + 1)],
                            wsrc[:, C * hl + 512 * cb:C * hl + 512 * (cb + 1)],
                            first, last,
                            waits=[("VEC", gw)] + (pw(pn) if first else []),
                            inc="TEN" if last else None)
                        first = False
                if grp == 0:
                    wmm = mm(pst[:, 0:512], k_onesr[0:1, :],
                             k_bo[0:1, 512 * cb:512 * (cb + 1)], False, True,
                             waits=[*CONSTW], inc="TEN")
                bmark("wvh", "TEN", wmm)
                bmark("gat0", "TEN", wmm)
                bmark("gat1", "TEN", wmm)
                sl4 = ts % 4
                wbuf = ret_sb if sl4 < 2 else g_sb
                wrn = "ret_sb" if sl4 < 2 else "g_sb"
                off = 1024 * (sl4 % 2)
                ev = vop(lambda e, cb=cb, pst=pst, wbuf=wbuf, off=off:
                         e.tensor_copy(
                             wbuf[:, off + 512 * cb:off + 512 * (cb + 1)],
                             pst[:, 0:512]),
                         waits=[("TEN", wmm)] + bw(f"wo_dma{sl4}"))
                pnote(pn, "VEC", ev)
            bclr(f"wo_dma{sl4}")
            wod = dma("gpsimd", scrATT[128 * ts:128 * (ts + 1), :],
                      wbuf[:, off:off + C],
                      waits=[("VEC", cnt["VEC"])] + att_prev_dgp,
                      accum=(grp > 0))
            bmark(f"wo_dma{sl4}", wod[0], wod[1])
            bmark(wrn, wod[0], wod[1])
        att_prev_dgp = qall("gpsimd")
    ATT_DGP = qall("gpsimd")
    if "attn" in tap_d:
        for ts in range(NT):
            ldw = dma("scalar", ret_sb[:, 0:C],
                      scrATT[128 * ts:128 * (ts + 1), :],
                      waits=qall("gpsimd") + [("VEC", cnt["VEC"])])
            dma("gpsimd", tap_d["attn"][128 * ts:128 * (ts + 1), :],
                ret_sb[:, 0:C], waits=[ldw])

    S.barrier()
    attn.close()
    if stop_after == "p2":
        return finish()

    ffn = contextlib.ExitStack()
    fsb = mk_sb(ffn)
    h3h = fsb("h3h", (128, NCH * 512), BF16)
    h3l = fsb("h3l", (128, NCH * 512), BF16)
    h3f = fsb("h3f", (128, 3 * 512))
    we_t = fsb("we_t", (128, 4 * 2048), BF16)
    wc_t = fsb("wc_t", (128, 4 * 1024), BF16)
    x_res = fsb("x_res", (128, 2 * C))
    a_res = fsb("a_res", (128, 2 * C))
    out_st = fsb("out_st", (128, 2 * C))

    cb1_ps = [ps_a, ps_b, ps_c, ps_d]
    cb1_pn = ["ps_a", "ps_b", "ps_c", "ps_d"]
    for nm in ("we0", "we1", "wc0", "wc1"):
        bclr(nm)
    for tile in range(4):
        toff = 512 * tile
        for oc in range(NCH):
            slot = 2048 * (oc % 4)
            wl = dma("sync", we_t[:, slot:slot + 2048],
                     scrWE[oc].rearrange("hl p a q -> p hl a q"),
                     waits=WE_DGP + bw(f"we{oc % 4}"))
            bclr(f"we{oc % 4}")
            pst = ps_a if oc % 2 == 0 else ps_b
            pn = "ps_a" if oc % 2 == 0 else "ps_b"
            first = True
            for hl_w, hl_x in ((0, 0), (0, 1), (1, 0)):
                xsrc = xnh_p if hl_x == 0 else xnl_p
                for kc in range(NCC):
                    mm(pst[:, 0:512],
                       we_t[:, slot + 1024 * hl_w + 128 * kc:
                            slot + 1024 * hl_w + 128 * (kc + 1)],
                       xsrc[kc][:, toff:toff + 512],
                       first, (hl_w, hl_x, kc) == (1, 0, NCC - 1),
                       waits=[wl] + (pw(pn) if first else []),
                       inc="TEN" if (hl_w, hl_x, kc) == (1, 0, NCC - 1)
                       else None)
                    first = False
            emm = cnt["TEN"]
            bmark(f"we{oc % 4}", "TEN", emm)
            gl = sop(lambda e, oc=oc, pst=pst: e.activation(
                h3f[:, 512 * (oc % 3):512 * (oc % 3 + 1)], pst[:, 0:512],
                AF.Gelu, bias=bprime[:, oc:oc + 1], scale=1.0),
                waits=[("TEN", emm), ("VEC", BPRIME_V)] + bw(f"h3f{oc % 3}"))
            bclr(f"h3f{oc % 3}")
            pnote(pn, "SCA", gl)
            vop(lambda e, oc=oc: e.tensor_copy(
                h3h[:, 512 * oc:512 * (oc + 1)],
                h3f[:, 512 * (oc % 3):512 * (oc % 3 + 1)]),
                waits=[("SCA", gl)])
            hv = vop(lambda e, oc=oc: e.scalar_tensor_tensor(
                h3l[:, 512 * oc:512 * (oc + 1)],
                h3h[:, 512 * oc:512 * (oc + 1)], -1.0,
                h3f[:, 512 * (oc % 3):512 * (oc % 3 + 1)],
                ALU.mult, ALU.add))
            bmark(f"h3f{oc % 3}", "VEC", hv)
        H3_S = cnt["VEC"]
        if tile == 0 and "h3" in tap_d:
            pass
        def load_res(ts):
            rs = C * (ts % 2)
            dma("scalar", x_res[:, rs:rs + C],
                d["x"][toff + 128 * ts:toff + 128 * (ts + 1), :],
                waits=bw(f"res{ts % 2}"))
            al = dma("scalar", a_res[:, rs:rs + C],
                     scrATT[toff + 128 * ts:toff + 128 * (ts + 1), :],
                     waits=ATT_DGP + bw(f"res{ts % 2}"))
            bclr(f"res{ts % 2}")
            return al

        al_l = [load_res(0), load_res(1)]
        bmms = {}
        for cb in range(2):
            for c2 in range(NCH):
                slot = 1024 * (c2 % 4)
                srcb = scrWC.rearrange("(hi gg) hl q -> hi gg hl q", gg=128)
                srcb = srcb[:, 4 * c2:4 * c2 + 4, :, 512 * cb:512 * (cb + 1)]
                srcb = srcb.rearrange("hi g hl q -> g hi hl q")
                wlw = WC_DGP + bw(f"wc{c2 % 4}")
                wl0 = dma("sync", wc_t[:, slot:slot + 512], srcb[:, :, 0, :],
                          waits=wlw)
                wl = dma("sync", wc_t[:, slot + 512:slot + 1024],
                         srcb[:, :, 1, :], waits=wlw)
                bclr(f"wc{c2 % 4}")
                wm = None
                for hl_h, hl_w in ((0, 0), (0, 1), (1, 0)):
                    hsrc = h3h if hl_h == 0 else h3l
                    for ts in range(4):
                        pst = ps_ret[ts] if cb == 0 else cb1_ps[ts]
                        pn = f"ret{ts}" if cb == 0 else cb1_pn[ts]
                        wm = mm(pst[:, 0:512],
                                hsrc[:, 512 * c2 + 128 * ts:
                                     512 * c2 + 128 * (ts + 1)],
                                wc_t[:, slot + 512 * hl_w:
                                     slot + 512 * (hl_w + 1)],
                                c2 == 0 and (hl_h, hl_w) == (0, 0), False,
                                waits=[wl0, wl, ("VEC", H3_S)] +
                                (pw(pn) if c2 == 0 and
                                 (hl_h, hl_w) == (0, 0) else []),
                                inc="TEN" if (hl_h, hl_w, ts) == (1, 0, 3)
                                else None)
                bmark(f"wc{c2 % 4}", "TEN", wm)
            for ts in range(4):
                pst = ps_ret[ts] if cb == 0 else cb1_ps[ts]
                bmms[(cb, ts)] = mm(
                    pst[:, 0:512], k_onesr[0:1, :],
                    k_bfin[0:1, 512 * cb:512 * (cb + 1)], False, True,
                    waits=[("VEC", BFIN_V)], inc="TEN")
        for ts in range(4):
            slot = C * (ts % 2)
            for cb in range(2):
                pst = ps_ret[ts] if cb == 0 else cb1_ps[ts]
                pn = f"ret{ts}" if cb == 0 else cb1_pn[ts]
                rs = C * (ts % 2)
                ev = vop(lambda e, pst=pst, rs=rs, cb=cb, slot=slot:
                         e.scalar_tensor_tensor(
                             out_st[:, slot + 512 * cb:slot + 512 * (cb + 1)],
                             pst[:, 0:512], 1.0,
                             x_res[:, rs + 512 * cb:rs + 512 * (cb + 1)],
                             ALU.mult, ALU.add),
                         waits=[("TEN", bmms[(cb, ts)]), al_l[ts]]
                         + (bw(f"y{ts % 2}") if cb == 0 else []))
                pnote(pn, "VEC", ev)
                ev2 = vop(lambda e, rs=rs, cb=cb, slot=slot: e.tensor_add(
                    out_st[:, slot + 512 * cb:slot + 512 * (cb + 1)],
                    out_st[:, slot + 512 * cb:slot + 512 * (cb + 1)],
                    a_res[:, rs + 512 * cb:rs + 512 * (cb + 1)]))
                bmark(f"res{ts % 2}", "VEC", ev2)
            bclr(f"y{ts % 2}")
            yd = dma("gpsimd",
                     y[toff + 128 * ts:toff + 128 * (ts + 1), :],
                     out_st[:, slot:slot + C],
                     waits=[("VEC", cnt["VEC"])])
            bmark(f"y{ts % 2}", yd[0], yd[1])
            if ts + 2 < 4:
                al_l.append(load_res(ts + 2))

    ffn.close()
    return finish()



def kernel(**inputs):
    x_full = np.ascontiguousarray(np.asarray(inputs["L0_attn_x"], np.float32))
    B = x_full.shape[0]
    consts, meta = build_host_constants(inputs)
    nc = build_program(meta)
    base = {
        "Wqk": np.ascontiguousarray(np.asarray(inputs["L0_attn_W_qk"], np.float32)),
        "Wv": np.ascontiguousarray(np.asarray(inputs["L0_attn_W_v"], np.float32)),
        "Wg": np.ascontiguousarray(np.asarray(inputs["L0_attn_W_g"], np.float32)),
        "Wo": np.ascontiguousarray(np.asarray(inputs["L0_attn_W_o"], np.float32)),
        "Wexp": np.ascontiguousarray(np.asarray(inputs["L0_ffn_W_exp"], np.float32)),
        "Wcon": np.ascontiguousarray(np.asarray(inputs["L0_ffn_W_con"], np.float32)),
    }
    base.update({k: np.ascontiguousarray(v) for k, v in consts.items()})
    in_maps = []
    for b in range(B):
        m = dict(base)
        m["x"] = x_full[b]
        in_maps.append(m)
    res = run_bass_kernel_spmd(nc, in_maps, core_ids=list(range(B)))
    out = np.stack([np.asarray(r["y"]) for r in res.results], axis=0)
    return out.astype(np.float32)


# version 28 (final)
# speedup vs baseline: 1.1169x; 1.0173x over previous
import contextlib
import sys
import ml_dtypes
import numpy as np
import concourse.bass as bass
import concourse.mybir as mybir
from concourse.bass_utils import run_bass_kernel_spmd

F32 = mybir.dt.float32
BF16 = mybir.dt.bfloat16
ALU = mybir.AluOpType
AF = mybir.ActivationFunctionType
AX = mybir.AxisListType

T, C, H, KD, VBD, VD, DH = 2048, 1024, 16, 64, 2048, 128, 4096
NCC = C // 128
NCH = DH // 128
NT = T // 128
EPS_LN = 1e-5
EPS_RMS = float(np.finfo(np.float32).eps)

ENGINES = ("tensor", "vector", "scalar", "gpsimd", "sync")
NQCLS = 8
_QUEUES = ("sync", "scalar", "gpsimd")
SEMS = ("TEN", "VEC", "SCA", "GPS") + tuple(
    f"{q[:2].upper()}{i}" for q in _QUEUES for i in range(NQCLS))


class Sched:

    def __init__(self):
        self.items = {e: [] for e in ENGINES}
        self.count = {s: 0 for s in SEMS}
        self.pending_bar = {e: None for e in ENGINES}

    def barrier(self):
        snap = [(s, v) for s, v in self.count.items() if v > 0]
        for e in ENGINES:
            self.pending_bar[e] = snap

    _SELF = {"vector": "VEC", "scalar": "SCA", "gpsimd": "GPS"}

    def emit(self, eng, fn, waits=(), inc=None, inc_by=1):
        w = dict()
        if self.pending_bar[eng]:
            for s, v in self.pending_bar[eng]:
                w[s] = max(w.get(s, 0), v)
            self.pending_bar[eng] = None
        for s, v in waits:
            w[s] = max(w.get(s, 0), v)
        selfsem = self._SELF.get(eng)
        if selfsem is not None and inc == selfsem and self.count[selfsem] > 0:
            w[selfsem] = max(w.get(selfsem, 0), self.count[selfsem])
        self.items[eng].append((fn, sorted(w.items()), inc, inc_by))
        if inc is not None:
            self.count[inc] += inc_by
            return self.count[inc]
        return None

    def install(self, block, sems):
        for engname in ENGINES:
            items = self.items[engname]
            if not items:
                continue

            def body(e, items=items):
                for fn, waits, inc, inc_by in items:
                    for s, v in waits:
                        e.wait_ge(sems[s], v)
                    ins = fn(e)
                    if inc is not None:
                        ins.then_inc(sems[inc], inc_by)

            getattr(block, engname)(body)



def _decay_tables():
    h = np.arange(H, dtype=np.float64)
    gamma = np.log(1.0 - 2.0 ** (-5.0 - h))
    t = np.arange(T, dtype=np.float64)
    eg = np.exp(gamma)[:, None]
    ssum = (1.0 - eg ** (t[None, :] + 1.0)) / (1.0 - eg)
    inv_scale = 1.0 / np.sqrt(ssum)
    et = np.exp(gamma[:, None] * t[None, :]) * inv_scale
    es = np.exp(-gamma[:, None] * t[None, :])
    et64 = np.repeat(et, KD, axis=0)
    es64 = np.repeat(es, KD, axis=0)
    etw = et64.astype(np.float32)
    esw = es64.astype(np.float32)
    etwl = (et64 - etw.astype(np.float64)).astype(ml_dtypes.bfloat16)
    eswl = (es64 - esw.astype(np.float64)).astype(ml_dtypes.bfloat16)
    return etw, esw, etwl, eswl


def _stage_coeffs(wstages):
    n_st = wstages.shape[0]
    f = np.arange(DH)
    a = np.empty((n_st, DH), np.float32)
    bP = np.empty((n_st, DH), np.float32)
    bT = np.empty((n_st, DH), np.float32)
    for k in range(n_st):
        s = 1 << k
        pr = (f // (2 * s)) * s + (f % s)
        bit = (f >> k) & 1
        a[k] = wstages[k, pr, bit, bit]
        bP[k] = wstages[k, pr, bit, 1 - bit]
        bT[k] = bP[k][f ^ s]
    return a, bP, bT


def _f_of_l2():
    c2 = np.arange(32)[:, None]
    p2 = np.arange(128)[None, :]
    return 128 * (p2 % 32) + 4 * c2 + (p2 // 32)


def _chain_specs(inputs):
    ua, ubP, ubT = _stage_coeffs(np.asarray(inputs["L0_ffn_up_weights"], np.float32))
    da, dbP, dbT = _stage_coeffs(np.asarray(inputs["L0_ffn_down_weights"], np.float32))
    l1 = lambda v: v.reshape(32, 128)
    l2 = lambda v: v[_f_of_l2()]
    spec = []
    spec.append((l1(ua[6]), l1(ubT[6]), 64))
    for k in range(5, -1, -1):
        spec.append((l1(ua[k]), l1(ubT[k]), 1 << k))
    spec.append((l2(ua[11]), l2(ubT[11]), 16))
    for k in range(10, 6, -1):
        spec.append((l2(ua[k]), l2(ubT[k]), 1 << (k - 7)))
    spec.append((l2(da[7]), l2(dbP[7]), 1))
    for k in range(8, 12):
        spec.append((l2(da[k]), l2(dbP[k]), 1 << (k - 7)))
    spec.append((l1(da[0]), l1(dbP[0]), 1))
    for k in range(1, 7):
        spec.append((l1(da[k]), l1(dbP[k]), 1 << k))
    alpha = np.stack([s[0] for s in spec]).astype(np.float32)
    beta = np.stack([s[1] for s in spec]).astype(np.float32)
    xors = [s[2] for s in spec]
    return alpha, beta, xors


CHAIN_LAYOUT = (("A_up", 7), ("B_up", 5), ("B_dnT", 5), ("A_dnT", 7))


def build_host_constants(inputs):
    import ml_dtypes
    alpha, beta, xors = _chain_specs(inputs)
    etw, esw, etwl, eswl = _decay_tables()
    xs_list = (1, 2, 4, 8, 16, 32, 64)
    xpat = np.zeros((7, 128, 128), np.float32)
    i = np.arange(128)
    for xi, s in enumerate(xs_list):
        xpat[xi, i, i ^ s] = 1.0
    causal = (np.arange(128)[None, :] >= np.arange(128)[:, None]).astype(np.float32)
    ind4 = np.zeros((128, 16), np.float32)
    for b in range(4):
        ind4[:, 4 * b + b] = 1.0
    ind8d = np.zeros((128, 32), np.float32)
    ind8m = np.zeros((128, 4 * 36), np.float32)
    for b in range(4):
        ind8d[:, 8 * b + b] = 1.0
        ind8m[:, 36 * b + 32 + b] = 1.0
    indT4 = np.zeros((4, 4 * 128), np.float32)
    for b in range(4):
        indT4[b, 128 * b:128 * (b + 1)] = 1.0
    b_qk = np.asarray(inputs["L0_attn_b_qk"], np.float32)
    consts = {
        "alpha": np.ascontiguousarray(alpha.transpose(2, 0, 1).reshape(128, -1)),
        "beta": np.ascontiguousarray(beta.transpose(2, 0, 1).reshape(128, -1)),
        "etw": etw, "esw": esw, "etwl": etwl, "eswl": eswl,
        "xpat": xpat, "causal": causal,
        "ident": np.eye(128, dtype=np.float32),
        "onesr": np.ones((1, 128), np.float32),
        "ind4f": ind4,
        "ind4b": ind4.astype(ml_dtypes.bfloat16),
        "ind8m": ind8m,
        "indT4": indT4.T.copy(),
        "bqkq": b_qk[:C].reshape(8, 128).T.copy(),
        "bqkk": b_qk[C:].reshape(8, 128).T.copy(),
        "bg2": np.asarray(inputs["L0_attn_b_g"], np.float32).reshape(16, 128).T.copy(),
        "upb_l2": np.asarray(inputs["L0_ffn_up_bias"], np.float32)[_f_of_l2()].T.copy(),
        "bv": np.asarray(inputs["L0_attn_b_v"], np.float32).reshape(1, VBD),
        "bo": np.asarray(inputs["L0_attn_b_o"], np.float32).reshape(1, C),
        "bexp": np.asarray(inputs["L0_ffn_b_exp"], np.float32).reshape(1, DH),
        "bcon": np.asarray(inputs["L0_ffn_b_con"], np.float32).reshape(1, C),
        "dbias": np.asarray(inputs["L0_ffn_down_bias"], np.float32).reshape(1, C),
    }
    ind_t = np.zeros((4, 512), np.float32)
    for b in range(4):
        ind_t[b, 128 * b:128 * (b + 1)] = 1.0
    consts["indT4"] = ind_t
    consts["ind8d"] = ind8d.astype(ml_dtypes.bfloat16)
    xpi = {s: i for i, s in enumerate(xs_list)}
    meta = {"xors": xors, "xpi": xpi}
    return consts, meta



MM_LINES = []


def build_program(meta, taps=(), stop_after=None, p2_groups=8, p2_heads=2, p2_wo=True, p2_parts="sadvrgp"):
    nc = bass.Bass("TRN2", target_bir_lowering=False,
                   detect_race_conditions=False)
    S = Sched()
    cnt = S.count

    def din(name, shape, dt=F32):
        return nc.dram_tensor(name, list(shape), dt, kind="ExternalInput").ap()

    d = {}
    d["x"] = din("x", (T, C))
    d["Wqk"] = din("Wqk", (C, 2 * C))
    d["Wv"] = din("Wv", (C, VBD))
    d["Wg"] = din("Wg", (C, VBD))
    d["Wo"] = din("Wo", (VBD, C))
    d["Wexp"] = din("Wexp", (C, DH))
    d["Wcon"] = din("Wcon", (DH, C))
    for name, shape in (
        ("alpha", (128, 24 * 32)), ("beta", (128, 24 * 32)),
        ("etw", (C, T)), ("esw", (C, T)),
        ("xpat", (7, 128, 128)), ("causal", (128, 128)), ("ident", (128, 128)),
        ("onesr", (1, 128)), ("ind4f", (128, 16)), ("indT4", (4, 512)),
        ("bqkq", (128, 8)), ("bqkk", (128, 8)), ("bg2", (128, 16)),
        ("upb_l2", (128, 32)), ("bv", (1, VBD)), ("bo", (1, C)),
        ("bexp", (1, DH)), ("bcon", (1, C)), ("dbias", (1, C)),
    ):
        d[name] = din(name, shape)
    d["etwl"] = din("etwl", (C, T), BF16)
    d["eswl"] = din("eswl", (C, T), BF16)
    d["ind4b"] = din("ind4b", (128, 16), BF16)
    d["ind8d"] = din("ind8d", (128, 32), BF16)
    d["ind8m"] = din("ind8m", (128, 4 * 36))
    y = nc.dram_tensor("y", [T, C], F32, kind="ExternalOutput").ap()

    scrA = nc.dram_tensor("scrA", [DH, C], F32, kind="Internal").ap()
    scrB = nc.dram_tensor("scrB", [DH, C + 1], F32, kind="Internal").ap()
    scrWE = nc.dram_tensor("scrWE", [NCH, 2, 128, NCC, 128], BF16,
                           kind="Internal").ap()
    scrV = nc.dram_tensor("scrV", [DH, C], F32, kind="Internal").ap()
    scrWC = nc.dram_tensor("scrWC", [DH, 2, C], BF16,
                           kind="Internal").ap()
    scrATT = nc.dram_tensor("scrATT", [T, C], F32, kind="Internal").ap()

    tap_d = {}
    for tname, tshape in dict(
        bprime=(128, 32),
        q0=(128, T), k0=(128, T), v0=(128, NT * 256), s0=(128, T),
        g0=(128, T), retn0=(128, T),
        den0=(4, 512), phi0=(4, 512), ret0=(128, T), gat0=(128, T),
        attn=(T, C), h3=(128, NCH * 512),
    ).items():
        if tname in taps:
            tap_d[tname] = nc.dram_tensor("tap_" + tname, list(tshape), F32,
                                          kind="ExternalOutput").ap()

    root = contextlib.ExitStack()

    def mk_sb(st):
        return (lambda name, shape, dt=F32:
                st.enter_context(nc.sbuf_tensor(name, list(shape), dt)))

    sb = mk_sb(root)
    psum = lambda name, shape: root.enter_context(
        nc.psum_tensor(name, list(shape), F32))

    xnh_p = [sb(f"xnh{c}", (128, T), BF16) for c in range(NCC)]
    xnl_p = [sb(f"xnl{c}", (128, T), BF16) for c in range(NCC)]
    k_ident = sb("k_ident", (128, 128))
    k_causal = sb("k_causal", (128, 128))
    k_onesr = sb("k_onesr", (1, 128))
    k_ind4f = sb("k_ind4f", (128, 16))
    k_ind4b = sb("k_ind4b", (128, 16), BF16)
    k_ind8d = sb("k_ind8d", (128, 32), BF16)
    k_ind8m = sb("k_ind8m", (128, 4 * 36))
    k_indT4 = sb("k_indT4", (4, 512))
    k_bqkq = sb("k_bqkq", (128, 8))
    k_bqkk = sb("k_bqkk", (128, 8))
    k_bg2 = sb("k_bg2", (128, 16))
    k_upb = sb("k_upb", (128, 32))
    k_bv = sb("k_bv", (1, VBD))
    k_bo = sb("k_bo", (1, C))
    k_bfin = sb("k_bfin", (1, C))
    bprime = sb("bprime", (128, 32))

    ps_a = psum("ps_a", (128, 512))
    ps_b = psum("ps_b", (128, 512))
    ps_ret = [psum(f"ps_ret{i}", (128, 512)) for i in range(4)]
    ps_c = psum("ps_c", (128, 512))
    ps_d = psum("ps_d", (128, 512))
    ps_den = ps_c
    CH_PS = [(ps_a, "cps_a"), (ps_b, "cps_b"), (ps_c, "cps_c"),
             (ps_d, "cps_d"), (ps_ret[0], "ret0"), (ps_ret[1], "ret1"),
             (ps_ret[2], "ret2"), (ps_ret[3], "ret3")]
    ps_ms = ps_d

    sems = {s: root.enter_context(nc.semaphore(s)) for s in SEMS}

    def mm(ps_ap, lhsT, rhs, start, stop, waits=(), inc=None):
        MM_LINES.append((sys._getframe(1).f_lineno,
                         lhsT.dtype == F32 and rhs.dtype == F32))
        return S.emit("tensor",
                      lambda e, o=ps_ap, a=lhsT, b=rhs, st=start, sp=stop:
                      e.matmul(o, a, b, start=st, stop=sp),
                      waits=waits, inc=inc)

    def tp(ps_ap, in_ap, id_ap, waits=(), inc="TEN"):
        MM_LINES.append((-sys._getframe(1).f_lineno, False))
        return S.emit("tensor",
                      lambda e, o=ps_ap, i=in_ap, idm=id_ap:
                      e.transpose(o, i, idm),
                      waits=waits, inc=inc)

    qseq = {q: 0 for q in _QUEUES}

    def dma(queue, out, in_, waits=(), accum=False):
        if accum:
            fn = lambda e, o=out, i=in_: e.dma_start(out=o, in_=i,
                                                     accum_op=ALU.add)
        else:
            fn = lambda e, o=out, i=in_: e.dma_start(out=o, in_=i)
        n = qseq[queue]
        qseq[queue] += 1
        sem = f"{queue[:2].upper()}{n % NQCLS}"
        prev = 16 * (n // NQCLS)
        S.emit(queue, fn, waits=list(waits) + [(sem, prev)],
               inc=sem, inc_by=16)
        return (sem, prev + 16)

    def qall(queue):
        return [(f"{queue[:2].upper()}{i}",
                 cnt[f"{queue[:2].upper()}{i}"]) for i in range(NQCLS)]

    def vop(fn, waits=()):
        return S.emit("vector", fn, waits=waits, inc="VEC")

    def sop(fn, waits=()):
        return S.emit("scalar", fn, waits=waits, inc="SCA")

    def allw():
        return [(s, v) for s, v in cnt.items() if v > 0]

    pfree = {}
    bnote = {}

    def _note(dic, name, sem, val):
        d_ = dic.setdefault(name, {})
        d_[sem] = max(d_.get(sem, 0), val)

    def _w(dic, name):
        return list(dic.get(name, {}).items())

    pnote = lambda name, sem, val: _note(pfree, name, sem, val)
    bmark = lambda name, sem, val: _note(bnote, name, sem, val)
    pw = lambda name: _w(pfree, name)
    bw = lambda name: _w(bnote, name)

    def bclr(name):
        bnote.pop(name, None)

    def finish():
        S.emit("gpsimd", lambda e: e.nop(),
               waits=qall("gpsimd") + qall("sync") + qall("scalar"))
        with nc.Block() as block:
            S.install(block, sems)
        root.close()
        return nc

    for tile_ap, src in (
        (k_ident[:, :], d["ident"][:, :]), (k_causal[:, :], d["causal"][:, :]),
        (k_ind8d[:, :], d["ind8d"][:, :]), (k_ind8m[:, :], d["ind8m"][:, :]),
        (k_onesr[:, :], d["onesr"][:, :]), (k_ind4f[:, :], d["ind4f"][:, :]),
        (k_ind4b[:, :], d["ind4b"][:, :]), (k_indT4[:, :], d["indT4"][:, :]),
        (k_bqkq[:, :], d["bqkq"][:, :]), (k_bqkk[:, :], d["bqkk"][:, :]),
        (k_bg2[:, :], d["bg2"][:, :]), (k_upb[:, :], d["upb_l2"][:, :]),
        (k_bv[:, :], d["bv"][:, :]), (k_bo[:, :], d["bo"][:, :]),
        (k_bfin[:, :], d["bcon"][:, :]),
    ):
        dma("sync", tile_ap, src)
    CONSTW = qall("sync")

    prep = contextlib.ExitStack()
    psb = mk_sb(prep)
    wrow = psb("wrow", (128, DH))
    wrow2 = psb("wrow2", (128, DH))
    wtile = psb("wtile", (128, C + 1))
    wtile2 = psb("wtile2", (128, C + 1))
    wtile3 = psb("wtile3", (128, C + 1))
    evt3 = psb("evt3", (128, 1025))
    chainL = psb("chainL", (128, 32 * 128))
    stile8 = psb("stile8", (128, 1024))
    xpat_t = [psb(f"xpat{i}", (128, 128)) for i in range(7)]
    alpha_t = psb("alpha_t", (128, 24 * 32))
    beta_t = psb("beta_t", (128, 24 * 32))
    evt = psb("evt", (128, 1025))
    evt2 = psb("evt2", (128, 1025))
    whl0 = psb("whl0", (128, 2048), BF16)
    whl1 = psb("whl1", (128, 2048), BF16)
    k_brow = psb("k_brow", (1, DH))
    x_tok = psb("x_tok", (128, C))
    x_tok2 = wtile3[:, 0:C]
    x_sq = evt3[:, 0:1024]
    xn_st = evt2[:, 0:1024]
    ln_s = psb("ln_s", (128, 16))

    dma("sync", k_brow[:, :], d["bexp"][:, :])
    for i in range(7):
        dma("sync", xpat_t[i][:, :], d["xpat"][i, :, :])
    dma("sync", alpha_t[:, :], d["alpha"][:, :])
    dma("sync", beta_t[:, :], d["beta"][:, :])
    dma("sync", evt[0:1, 0:C], d["dbias"][:, :])
    CONSTW2 = qall("sync")
    BFIN_V = vop(lambda e: e.tensor_add(k_bfin[0:1, :], k_bfin[0:1, :],
                                        evt[0:1, 0:C]),
                 waits=[*CONSTW2])

    p0st = {"tp": 0}
    xcons = {0: (0, 0), 1: (0, 0)}

    def emit_p0_tile(ti):
        xt = x_tok if ti % 2 == 0 else x_tok2
        o = 8 * (ti % 2)
        pv, psc = xcons[ti % 2]
        ld = dma("sync" if ti % 2 == 0 else "scalar", xt[:, :],
                 d["x"][128 * ti:128 * (ti + 1), :],
                 waits=[("VEC", pv), ("SCA", psc)])
        SQ = sop(lambda e: e.activation(x_sq[:, :], xt[:, :], AF.Square,
                                        accum_out=ln_s[:, o + 1:o + 2]),
                 waits=[ld])
        vop(lambda e: e.tensor_reduce(ln_s[:, o:o + 1], xt[:, :], AX.X,
                                      ALU.add),
            waits=[ld])
        vop(lambda e: e.tensor_scalar(ln_s[:, o + 2:o + 3], ln_s[:, o:o + 1],
                                      1.0 / C, None, ALU.mult))
        vop(lambda e: e.tensor_mul(ln_s[:, o + 3:o + 4], ln_s[:, o + 2:o + 3],
                                   ln_s[:, o + 2:o + 3]))
        vop(lambda e: e.scalar_tensor_tensor(ln_s[:, o + 4:o + 5],
                                             ln_s[:, o + 1:o + 2],
                                             1.0 / C, ln_s[:, o + 3:o + 4],
                                             ALU.mult, ALU.subtract),
            waits=[("SCA", SQ)])
        vop(lambda e: e.tensor_scalar(ln_s[:, o + 4:o + 5],
                                      ln_s[:, o + 4:o + 5], EPS_LN,
                                      None, ALU.add))
        VE = cnt["VEC"]
        SQ2 = sop(lambda e: e.activation(ln_s[:, o + 5:o + 6],
                                         ln_s[:, o + 4:o + 5], AF.Sqrt),
                  waits=[("VEC", VE)])
        vop(lambda e: e.reciprocal(ln_s[:, o + 6:o + 7],
                                   ln_s[:, o + 5:o + 6]),
            waits=[("SCA", SQ2)])
        vop(lambda e: e.tensor_mul(ln_s[:, o + 7:o + 8], ln_s[:, o + 6:o + 7],
                                   ln_s[:, o + 6:o + 7]))
        vop(lambda e: e.tensor_mul(ln_s[:, o + 7:o + 8], ln_s[:, o + 7:o + 8],
                                   ln_s[:, o + 4:o + 5]))
        vop(lambda e: e.tensor_scalar(ln_s[:, o + 7:o + 8],
                                      ln_s[:, o + 7:o + 8], -0.5, 1.5,
                                      ALU.mult, ALU.add))
        vop(lambda e: e.tensor_mul(ln_s[:, o + 7:o + 8], ln_s[:, o + 7:o + 8],
                                   ln_s[:, o + 6:o + 7]))
        XN_V = vop(lambda e: e.tensor_scalar(xn_st[:, 0:C], xt[:, :],
                                             ln_s[:, o + 2:o + 3],
                                             ln_s[:, o + 7:o + 8],
                                             ALU.subtract, ALU.mult),
                   waits=[("TEN", p0st["tp"])])
        for c in range(NCC):
            pst, pn = CH_PS[c % 8]
            t_mm = tp(pst[:, 0:128], xn_st[:, 128 * c:128 * (c + 1)],
                      k_ident[:, :],
                      waits=[("VEC", XN_V)] + pw(pn) +
                      ([*CONSTW] if ti == 0 and c == 0 else []))
            p0st["tp"] = t_mm
            vop(lambda e, c=c, ti=ti, pst=pst: e.tensor_copy(
                xnh_p[c][:, 128 * ti:128 * (ti + 1)], pst[:, 0:128]),
                waits=[("TEN", t_mm)])
            lo_ = vop(lambda e, c=c, ti=ti, pst=pst: e.scalar_tensor_tensor(
                xnl_p[c][:, 128 * ti:128 * (ti + 1)],
                xnh_p[c][:, 128 * ti:128 * (ti + 1)], -1.0,
                pst[:, 0:128], ALU.mult, ALU.add))
            pnote(pn, "VEC", lo_)
        xcons[ti % 2] = (XN_V, SQ)

    w1_last = {(bf, ch): ("VEC", 0) for bf in range(2) for ch in range(4)}

    def emit_w1_rc(rc):
        wr_ = wrow if rc % 2 == 0 else wrow2
        ld_ch = []
        for ch in range(4):
            ld_ch.append(dma("sync" if ch % 2 == 0 else "scalar",
                             wr_[:, 1024 * ch:1024 * (ch + 1)],
                             d["Wexp"][128 * rc:128 * (rc + 1),
                                       1024 * ch:1024 * (ch + 1)],
                             waits=[w1_last[(rc % 2, ch)]]))
        for cc in range(NCH):
            pst, pn = CH_PS[cc % 8]
            t_mm = tp(pst[:, 0:128], wr_[:, 128 * cc:128 * (cc + 1)],
                      k_ident[:, :],
                      waits=[ld_ch[cc // 8]] + pw(pn))
            w_extra = qall("gpsimd") if cc % 8 == 0 else []
            ev_w = vop(lambda e, cc=cc, pst=pst: e.tensor_copy(
                evt[:, 128 * (cc % 8):128 * (cc % 8 + 1)], pst[:, 0:128]),
                waits=[("TEN", t_mm)] + w_extra)
            pnote(pn, "VEC", ev_w)
            if cc % 8 == 7:
                base = cc - 7
                w1_last[(rc % 2, cc // 8)] = ("TEN", t_mm)
                dst = scrA[:, 128 * rc:128 * (rc + 1)] \
                    .rearrange("(a p) q -> p a q", p=128)[:, base:cc + 1, :]
                dma("gpsimd", dst,
                    evt[:, 0:1024].rearrange("p (a q) -> p a q", q=128),
                    waits=[("VEC", cnt["VEC"])])

    for ti in range(NT):
        emit_p0_tile(ti)
        if ti % 2 == 1:
            emit_w1_rc(ti // 2)
    P0_V = cnt["VEC"]
    bmark("wt2", "VEC", P0_V)
    bmark("wt2", "SCA", cnt["SCA"])

    if stop_after == "p0":
        prep.close()
        return finish()

    W1_DGP = qall("gpsimd")

    xors = meta["xors"]
    xpi = meta["xpi"]
    n_base = {}
    acc = 0
    for nm, st_n in CHAIN_LAYOUT:
        n_base[nm] = acc
        acc += st_n

    chst = {"upd": [0] * 32, "t": [0] * 8, "rd": [0] * 32}

    def build_chain(name):
        base, steps = n_base[name], dict(CHAIN_LAYOUT)[name]
        for si in range(steps):
            for c in range(32):
                n = base + si
                xp = xpat_t[xpi[xors[n]]]
                al = alpha_t[:, 32 * n + c:32 * n + c + 1]
                be = beta_t[:, 32 * n + c:32 * n + c + 1]
                cl = chainL[:, 128 * c:128 * (c + 1)]
                tsl = stile8[:, 128 * (c % 8):128 * (c % 8 + 1)]
                prev_upd = chst["upd"][c]
                if si == 0:
                    s1 = sop(lambda e, al=al, cl=cl: e.activation(
                        cl, k_ident[:, :], AF.Copy, bias=0.0, scale=al),
                        waits=[*CONSTW2, ("VEC", prev_upd),
                               ("TEN", chst["rd"][c])])
                    v1 = vop(lambda e, be=be, xp=xp, cl=cl:
                             e.scalar_tensor_tensor(cl, xp[:, :], be, cl,
                                                    ALU.mult, ALU.add),
                             waits=[("SCA", s1)])
                    chst["upd"][c] = v1
                else:
                    pst, pn = CH_PS[c % 8]
                    t_mm = mm(pst[:, 0:128], xp[:, :], cl, True, True,
                              waits=[("VEC", prev_upd)] + pw(pn), inc="TEN")
                    s1 = sop(lambda e, al=al, cl=cl, tsl=tsl: e.activation(
                        tsl, cl, AF.Copy, bias=0.0, scale=al),
                        waits=[("VEC", prev_upd),
                               ("VEC", chst["t"][c % 8])])
                    v1 = vop(lambda e, be=be, cl=cl, tsl=tsl, pst=pst:
                             e.scalar_tensor_tensor(cl, pst[:, 0:128], be,
                                                    tsl, ALU.mult,
                                                    ALU.add),
                             waits=[("TEN", t_mm), ("SCA", s1)])
                    chst["upd"][c] = v1
                    pnote(pn, "VEC", v1)
                    chst["t"][c % 8] = v1

    build_chain("A_up")
    for c in range(32):
        wt = [wtile, wtile2, wtile3][c % 3]
        ev_ = [evt, evt2, evt3][c % 3]
        wtn = f"wt{c % 3}"
        evn = f"ev{c % 3}"
        ldw = dma("scalar" if c % 2 == 0 else "sync",
                  wt[:, 0:C], scrA[128 * c:128 * (c + 1), :],
                  waits=W1_DGP + bw(wtn))
        bclr(wtn)
        t_bt = tp(ps_c[:, 0:1], k_brow[0:1, 128 * c:128 * (c + 1)],
                  k_onesr[0:1, 0:1],
                  waits=[*CONSTW] + pw("cps_c"))
        bv_ = vop(lambda e, wt=wt: e.tensor_copy(wt[:, C:C + 1], ps_c[:, 0:1]),
                  waits=[("TEN", t_bt), ldw])
        pnote("cps_c", "VEC", bv_)
        lhs = chainL[:, 128 * c:128 * (c + 1)]
        m1 = mm(ps_a[:, 0:512], lhs, wt[:, 0:512], True, True,
                waits=[("VEC", bv_)] + pw("cps_a"), inc="TEN")
        m2 = mm(ps_b[:, 0:512], lhs, wt[:, 512:1024], True, True,
                waits=[("VEC", bv_)] + pw("cps_b"), inc="TEN")
        m3 = mm(ps_d[:, 0:1], lhs, wt[:, C:C + 1], True, True,
                waits=[("VEC", bv_)] + pw("cps_d"), inc="TEN")
        chst["rd"][c] = m3
        bmark(wtn, "TEN", m3)
        e1 = vop(lambda e, ev_=ev_: e.tensor_copy(ev_[:, 0:512],
                                                  ps_a[:, 0:512]),
                 waits=[("TEN", m1)] + bw(evn))
        pnote("cps_a", "VEC", e1)
        e2 = vop(lambda e, ev_=ev_: e.tensor_copy(ev_[:, 512:1024],
                                                  ps_b[:, 0:512]),
                 waits=[("TEN", m2)])
        pnote("cps_b", "VEC", e2)
        e3 = vop(lambda e, ev_=ev_: e.tensor_copy(ev_[:, 1024:1025],
                                                  ps_d[:, 0:1]),
                 waits=[("TEN", m3)])
        pnote("cps_d", "VEC", e3)
        bclr(evn)
        od = dma("gpsimd", scrB[128 * c:128 * (c + 1), 0:C + 1],
                 ev_[:, 0:1025], waits=[("VEC", e3)])
        bmark(evn, od[0], od[1])
    F1_DGP = qall("gpsimd")

    build_chain("B_up")
    for c2 in range(32):
        wt = [wtile, wtile2, wtile3][c2 % 3]
        ev_ = [evt, evt2, evt3][c2 % 3]
        wtn = f"wt{c2 % 3}"
        evn = f"ev{c2 % 3}"
        src = scrB.rearrange("(hi gg) q -> hi gg q", gg=128)
        src = src[:, 4 * c2:4 * c2 + 4, :].rearrange("hi g q -> g hi q")
        ldw = dma("scalar" if c2 % 2 == 0 else "sync",
                  wt[:, 0:C + 1], src,
                  waits=F1_DGP + bw(wtn))
        bclr(wtn)
        lhs = chainL[:, 128 * c2:128 * (c2 + 1)]
        m1 = mm(ps_a[:, 0:512], lhs, wt[:, 0:512], True, True,
                waits=[ldw] + pw("cps_a"), inc="TEN")
        m2 = mm(ps_b[:, 0:512], lhs, wt[:, 512:1024], True, True,
                waits=[ldw] + pw("cps_b"), inc="TEN")
        m3 = mm(ps_d[:, 0:1], lhs, wt[:, C:C + 1], True, True,
                waits=[ldw] + pw("cps_d"), inc="TEN")
        chst["rd"][c2] = m3
        e1 = vop(lambda e, ev_=ev_: e.tensor_copy(ev_[:, 0:512],
                                                  ps_a[:, 0:512]),
                 waits=[("TEN", m1)] + bw(evn))
        bclr(evn)
        pnote("cps_a", "VEC", e1)
        e2 = vop(lambda e, ev_=ev_: e.tensor_copy(ev_[:, 512:1024],
                                                  ps_b[:, 0:512]),
                 waits=[("TEN", m2)])
        pnote("cps_b", "VEC", e2)
        e3 = vop(lambda e, c2=c2: e.tensor_copy(bprime[:, c2:c2 + 1],
                                                ps_d[:, 0:1]),
                 waits=[("TEN", m3)])
        pnote("cps_d", "VEC", e3)
        tv = None
        whl = whl0 if c2 % 2 == 0 else whl1
        for rc in range(NCC):
            pst, pn = CH_PS[(rc + 4 * (c2 % 2)) % 8]
            t_mm = tp(pst[:, 0:128], ev_[:, 128 * rc:128 * (rc + 1)],
                      k_ident[:, :],
                      waits=[("VEC", e2)] + pw(pn))
            bmark(evn, "TEN", t_mm)
            vop(lambda e, rc=rc, whl=whl, pst=pst: e.tensor_copy(
                whl[:, 128 * rc:128 * (rc + 1)], pst[:, 0:128]),
                waits=[("TEN", t_mm)] +
                (bw(f"whl{c2 % 2}") if rc == 0 else []))
            tv = vop(lambda e, rc=rc, whl=whl, pst=pst:
                     e.scalar_tensor_tensor(
                         whl[:, 1024 + 128 * rc:1024 + 128 * (rc + 1)],
                         whl[:, 128 * rc:128 * (rc + 1)], -1.0,
                         pst[:, 0:128], ALU.mult, ALU.add))
            pnote(pn, "VEC", tv)
        bclr(f"whl{c2 % 2}")
        od = dma("gpsimd",
                 scrWE[c2].rearrange("hl p a q -> p hl a q"),
                 whl[:, 0:2048].rearrange("p (hl a q) -> p hl a q",
                                          hl=2, a=NCC),
                 waits=[("VEC", tv)])
        bmark(f"whl{c2 % 2}", od[0], od[1])
    BPRIME_V = vop(lambda e: e.tensor_add(bprime[:, :], bprime[:, :],
                                          k_upb[:, :]),
                   waits=[*CONSTW])
    WE_DGP = qall("gpsimd")
    if "bprime" in tap_d:
        dma("gpsimd", tap_d["bprime"][:, :], bprime[:, :],
            waits=[("VEC", BPRIME_V)])

    build_chain("B_dnT")

    def down_fold(src_of, dst_of, split):
        for c in range(32):
            wt = [wtile, wtile2, wtile3][c % 3]
            ev_ = [evt, evt2, evt3][c % 3]
            whl = whl0 if c % 2 == 0 else whl1
            wtn = f"wt{c % 3}"
            evn = f"ev{c % 3}"
            ldw = dma("scalar" if c % 2 == 0 else "sync",
                      wt[:, 0:C], src_of(c), waits=bw(wtn))
            bclr(wtn)
            lhs = chainL[:, 128 * c:128 * (c + 1)]
            m1 = mm(ps_a[:, 0:512], lhs, wt[:, 0:512], True, True,
                    waits=[ldw] + pw("cps_a"), inc="TEN")
            m2 = mm(ps_b[:, 0:512], lhs, wt[:, 512:1024], True, True,
                    waits=[ldw] + pw("cps_b"), inc="TEN")
            chst["rd"][c] = m2
            bmark(wtn, "TEN", m2)
            if not split:
                e1 = vop(lambda e, ev_=ev_: e.tensor_copy(ev_[:, 0:512],
                                                          ps_a[:, 0:512]),
                         waits=[("TEN", m1)] + bw(evn))
                bclr(evn)
                pnote("cps_a", "VEC", e1)
                e2 = vop(lambda e, ev_=ev_: e.tensor_copy(ev_[:, 512:1024],
                                                          ps_b[:, 0:512]),
                         waits=[("TEN", m2)])
                pnote("cps_b", "VEC", e2)
                od = dma("gpsimd", dst_of(c), ev_[:, 0:1024],
                         waits=[("VEC", e2)])
                bmark(evn, od[0], od[1])
            else:
                vop(lambda e, whl=whl: e.tensor_copy(whl[:, 0:512],
                                                     ps_a[:, 0:512]),
                    waits=[("TEN", m1)] + bw(f"whl{c % 2}"))
                bclr(f"whl{c % 2}")
                v1 = vop(lambda e, whl=whl: e.scalar_tensor_tensor(
                    whl[:, 1024:1536], whl[:, 0:512], -1.0,
                    ps_a[:, 0:512], ALU.mult, ALU.add))
                pnote("cps_a", "VEC", v1)
                vop(lambda e, whl=whl: e.tensor_copy(whl[:, 512:1024],
                                                     ps_b[:, 0:512]),
                    waits=[("TEN", m2)])
                v2 = vop(lambda e, whl=whl: e.scalar_tensor_tensor(
                    whl[:, 1536:2048], whl[:, 512:1024], -1.0,
                    ps_b[:, 0:512], ALU.mult, ALU.add))
                pnote("cps_b", "VEC", v2)
                od = dma("gpsimd", dst_of(c),
                         whl[:, 0:2048].rearrange("p (hl q) -> p hl q", hl=2),
                         waits=[("VEC", v2)])
                bmark(f"whl{c % 2}", od[0], od[1])

    def l2_rows(t, c):
        s_ = t.rearrange("(hi gg) q -> hi gg q", gg=128)
        return s_[:, 4 * c:4 * c + 4, :].rearrange("hi g q -> g hi q")

    down_fold(lambda c: l2_rows(d["Wcon"], c), lambda c: l2_rows(scrV, c),
              False)
    V1_DGP = qall("gpsimd")

    build_chain("A_dnT")
    for s_, v_ in V1_DGP:
        _note(bnote, "wt0", s_, v_)
        _note(bnote, "wt1", s_, v_)
        _note(bnote, "wt2", s_, v_)
    down_fold(lambda c: scrV[128 * c:128 * (c + 1), :],
              lambda c: scrWC[128 * c:128 * (c + 1), :, :], True)
    WC_DGP = qall("gpsimd")
    S.barrier()
    prep.close()
    if stop_after == "p1":
        return finish()

    attn = contextlib.ExitStack()
    asb = mk_sb(attn)
    qh_sb = asb("qh_sb", (128, T), BF16)
    ql_sb = asb("ql_sb", (128, T), BF16)
    kh_sb = asb("kh_sb", (128, T), BF16)
    kl_sb = asb("kl_sb", (128, T), BF16)
    v_sb = asb("v_sb", (128, NT * 256))
    dec_hi = asb("dec_hi", (128, 4 * 512))
    dec_lo = asb("dec_lo", (128, 4 * 512), BF16)
    wst = asb("wst", (128, NCC * 128))
    wst2 = asb("wst2", (128, NCC * 128))
    WSTS = [(wst, "wst"), (wst2, "wst2")]
    wsti = {"i": 0}
    wqh = asb("wqh", (128, NCC * 128), BF16)
    wql = asb("wql", (128, NCC * 128), BF16)
    wkh = asb("wkh", (128, NCC * 128), BF16)
    wkl = asb("wkl", (128, NCC * 128), BF16)
    wvh = asb("wvh", (128, NCC * 256), BF16)
    wvl = asb("wvl", (128, NCC * 256), BF16)
    woh, wol = wvh, wvl
    s_sb = [asb(f"s_sb{i}", (128, T)) for i in range(2)]
    sabs = asb("sabs", (128, T), BF16)
    ret_sb = asb("ret_sb", (128, T))
    g_sb = asb("g_sb", (128, T))
    gath = asb("gath", (128, 2 * T), BF16)
    gatl = asb("gatl", (128, 2 * T), BF16)

    P2_PS = [(ps_a, "ps_a"), (ps_b, "ps_b"), (ps_den, "den"),
             (ps_d, "ps_d"), (ps_ret[0], "ret0"), (ps_ret[1], "ret1"),
             (ps_ret[2], "ret2"), (ps_ret[3], "ret3")]
    att_prev_dgp = []
    for grp in range(p2_groups):
        h0 = 2 * grp

        dec_ld = {}

        def load_dec(which, tb):
            wi = 0 if which == "q" else 1
            s4 = 2 * wi + tb % 2
            hname, lname = ("etw", "etwl") if wi == 0 else ("esw", "eswl")
            w_ = bw(f"dec{s4}")
            bclr(f"dec{s4}")
            dec_ld[(which, tb)] = [
                dma("sync" if wi == 0 else "scalar",
                    dec_hi[:, 512 * s4:512 * (s4 + 1)],
                    d[hname][128 * grp:128 * (grp + 1),
                             512 * tb:512 * (tb + 1)], waits=w_),
                dma("scalar" if wi == 0 else "sync",
                    dec_lo[:, 512 * s4:512 * (s4 + 1)],
                    d[lname][128 * grp:128 * (grp + 1),
                             512 * tb:512 * (tb + 1)], waits=w_)]

        for tb0 in range(2):
            load_dec("q", tb0)
            load_dec("k", tb0)
        spl = {}
        for nm_, hib, lob, dsrc, cw in (
            ("wq", wqh, wql, d["Wqk"][:, 128 * grp:128 * (grp + 1)], 128),
            ("wk", wkh, wkl,
             d["Wqk"][:, C + 128 * grp:C + 128 * (grp + 1)], 128),
            ("wv", wvh, wvl, d["Wv"][:, 256 * grp:256 * (grp + 1)], 256),
        ):
            kc_half = 1024 // cw
            for half in range((cw * NCC) // 1024):
                off = 1024 * half
                wl_ = []
                ws_, wn_ = WSTS[wsti["i"] % 2]
                wsti["i"] += 1
                lw = bw(wn_)
                bclr(wn_)
                for kc in range(kc_half):
                    kk = half * kc_half + kc
                    wl_.append(dma("sync" if kc % 2 == 0 else "scalar",
                                   ws_[:, cw * kc:cw * (kc + 1)],
                                   dsrc[128 * kk:128 * (kk + 1), :],
                                   waits=lw))
                sh = sop(lambda e, h_=hib, off=off, ws_=ws_: e.copy(
                    h_[:, off:off + 1024], ws_[:, 0:1024]),
                    waits=list(wl_) + bw(nm_ + "h"))
                sv = vop(lambda e, h_=hib, l_=lob, off=off, ws_=ws_:
                         e.scalar_tensor_tensor(
                             l_[:, off:off + 1024], h_[:, off:off + 1024],
                             -1.0, ws_[:, 0:1024], ALU.mult, ALU.add),
                         waits=[("SCA", sh)])
                bmark(wn_, "VEC", sv)
            bclr(nm_ + "h")
            spl[nm_] = sv

        qk_i = 0
        for tb in range(4):
            for which in ("q", "k"):
                whi = wqh if which == "q" else wkh
                wlo = wql if which == "q" else wkl
                pst, pn = P2_PS[qk_i % 8]
                qk_i += 1
                dh_ = qh_sb if which == "q" else kh_sb
                dl_ = ql_sb if which == "q" else kl_sb
                bias = k_bqkq[:, grp:grp + 1] if which == "q" \
                    else k_bqkk[:, grp:grp + 1]
                wi = 0 if which == "q" else 1
                s4 = 2 * wi + tb % 2
                first = True
                for hw, hx in ((0, 0), (0, 1), (1, 0)):
                    wt_ = whi if hw == 0 else wlo
                    xs_ = xnh_p if hx == 0 else xnl_p
                    for kc in range(NCC):
                        mm(pst[:, 0:512], wt_[:, 128 * kc:128 * (kc + 1)],
                           xs_[kc][:, 512 * tb:512 * (tb + 1)],
                           first, (hw, hx, kc) == (1, 0, NCC - 1),
                           waits=[("VEC", spl[
                               "wq" if which == "q" else "wk"])] +
                           (pw(pn) if first else []),
                           inc="TEN" if (hw, hx, kc) == (1, 0, NCC - 1)
                           else None)
                        first = False
                tmm = cnt["TEN"]
                
                bmark("wqh" if which == "q" else "wkh", "TEN", tmm)
                qf_st = s_sb[1][:, 1536:2048]
                sc2 = s_sb[1][:, 1024:1536]
                hi_ap = dec_hi[:, 512 * s4:512 * (s4 + 1)]
                lo_ap = dec_lo[:, 512 * s4:512 * (s4 + 1)]
                e0 = vop(lambda e, bias=bias, pst=pst, qf_st=qf_st:
                         e.tensor_scalar(qf_st[:, 0:512], pst[:, 0:512],
                                         bias, None, ALU.add),
                         waits=[("TEN", tmm)] + dec_ld[(which, tb)])
                pnote(pn, "VEC", e0)
                vop(lambda e, qf_st=qf_st, sc2=sc2, lo_ap=lo_ap:
                    e.tensor_mul(sc2[:, 0:512], qf_st[:, 0:512], lo_ap))
                vop(lambda e, qf_st=qf_st, hi_ap=hi_ap:
                    e.tensor_mul(qf_st[:, 0:512], qf_st[:, 0:512], hi_ap))
                e0d = vop(lambda e, qf_st=qf_st, sc2=sc2:
                          e.tensor_add(qf_st[:, 0:512], qf_st[:, 0:512],
                                       sc2[:, 0:512]))
                bmark(f"dec{s4}", "VEC", e0d)
                vop(lambda e, dh_=dh_, tb=tb, qf_st=qf_st: e.tensor_copy(
                    dh_[:, 512 * tb:512 * (tb + 1)], qf_st[:, 0:512]))
                vop(lambda e, dh_=dh_, dl_=dl_, tb=tb, qf_st=qf_st:
                    e.scalar_tensor_tensor(
                        dl_[:, 512 * tb:512 * (tb + 1)],
                        dh_[:, 512 * tb:512 * (tb + 1)], -1.0,
                        qf_st[:, 0:512], ALU.mult, ALU.add))
                if tb + 2 < 4:
                    load_dec(which, tb + 2)
        for sl in range(NT):
            pst, pn = P2_PS[sl % 8]
            first = True
            for hx, hw in ((0, 0), (0, 1), (1, 0)):
                xs_ = xnh_p if hx == 0 else xnl_p
                wt_ = wvh if hw == 0 else wvl
                for kc in range(NCC):
                    mm(pst[:, 0:256], xs_[kc][:, 128 * sl:128 * (sl + 1)],
                       wt_[:, 256 * kc:256 * (kc + 1)],
                       first, False,
                       waits=[("VEC", spl["wv"])] +
                       (pw(pn) if first else []))
                    first = False
            bmm = mm(pst[:, 0:256], k_onesr[0:1, :],
                     k_bv[0:1, 256 * grp:256 * (grp + 1)], False, True,
                     waits=[*CONSTW], inc="TEN")
            bmark("wvh", "TEN", bmm)
            ev = vop(lambda e, sl=sl, pst=pst: e.tensor_copy(
                v_sb[:, 256 * sl:256 * (sl + 1)], pst[:, 0:256]),
                waits=[("TEN", bmm)])
            pnote(pn, "VEC", ev)
        QKV_V = cnt["VEC"]

        gspl = []
        for hl2 in range(2):
            h2 = h0 + hl2
            g_hi = wqh if hl2 == 0 else wkh
            g_lo = wql if hl2 == 0 else wkl
            g_rn = "wqh" if hl2 == 0 else "wkh"
            wg_l = []
            ws_, wn_ = WSTS[wsti["i"] % 2]
            wsti["i"] += 1
            wgw = bw(wn_)
            bclr(wn_)
            for kc in range(NCC):
                wg_l.append(dma("sync" if kc % 2 == 0 else "scalar",
                                ws_[:, 128 * kc:128 * (kc + 1)],
                                d["Wg"][128 * kc:128 * (kc + 1),
                                        128 * h2:128 * (h2 + 1)],
                                waits=wgw))
            gsh = sop(lambda e, g_hi=g_hi, ws_=ws_: e.copy(
                g_hi[:, 0:1024], ws_[:, 0:1024]),
                waits=list(wg_l) + bw(g_rn))
            gsv = vop(lambda e, g_hi=g_hi, g_lo=g_lo, ws_=ws_:
                      e.scalar_tensor_tensor(
                          g_lo[:, 0:1024], g_hi[:, 0:1024], -1.0,
                          ws_[:, 0:1024], ALU.mult, ALU.add),
                      waits=[("SCA", gsh)])
            bclr(g_rn)
            bmark(wn_, "VEC", gsv)
            gspl.append((g_hi, g_lo, g_rn, gsv))

        for hl2 in range(2):
            ws_, wn_ = WSTS[wsti["i"] % 2]
            wsti["i"] += 1
            wow = bw(wn_)
            bclr(wn_)
            wo_ld = [dma("sync", ws_[:, 0:512],
                         d["Wo"][128 * (h0 + hl2):128 * (h0 + hl2 + 1),
                                 0:512], waits=wow),
                     dma("scalar", ws_[:, 512:1024],
                         d["Wo"][128 * (h0 + hl2):128 * (h0 + hl2 + 1),
                                 512:1024], waits=wow)]
            osh = sop(lambda e, hl2=hl2, ws_=ws_: e.copy(
                woh[:, C * hl2:C * (hl2 + 1)], ws_[:, 0:1024]),
                waits=wo_ld + (bw("wvh") if hl2 == 0 else []))
            osv = vop(lambda e, hl2=hl2, ws_=ws_: e.scalar_tensor_tensor(
                wol[:, C * hl2:C * (hl2 + 1)], woh[:, C * hl2:C * (hl2 + 1)],
                -1.0, ws_[:, 0:1024], ALU.mult, ALU.add),
                waits=[("SCA", osh)])
            bmark(wn_, "VEC", osv)
        bclr("wvh")
        gate_done = [[None] * 4, [None] * 4]

        for hl in range(p2_heads):
            h = h0 + hl
            qof = 64 * hl
            den_q = []
            den_done = {}
            n_den = 0

            def flush_den(j_end):
                nonlocal n_den, last_den
                while den_q and den_q[0][0] < j_end:
                    jq, b, lo, hi, ab = den_q.pop(0)
                    last_den = mm(ps_den[0:8, lo - 512 * b:512],
                                  k_ind8d[:, 8 * b:8 * b + 8], sabs[:, lo:hi],
                                  n_den == 0, (jq, b) == (NT - 1, 3),
                                  waits=[("SCA", ab)] +
                                  (pw("den") if n_den == 0 else []),
                                  inc="TEN")
                    den_done[b] = last_den
                    n_den += 1

            sv_q = []
            sv_done = {}

            def flush_sv(j_end):
                nonlocal last_sv
                while sv_q and sv_q[0][0] < j_end:
                    jq, b, lo, hi, ev1, buf_ = sv_q.pop(0)
                    last_sv = mm(ps_ret[b][:, lo - 512 * b:512],
                                 v_sb[:, 256 * jq + 128 * hl:
                                      256 * jq + 128 * (hl + 1)],
                                 buf_[:, lo:hi], jq == 0, jq == 4 * b + 3,
                                 waits=[("VEC", ev1)] +
                                 (pw(f"ret{b}") if jq == 0 else []),
                                 inc="TEN")
                    sv_done[b] = last_sv

            last_den = None
            last_sv = None
            sc_i = 0
            for j in range(NT):
                flush_den(j)
                flush_sv(j)
                t0 = 128 * j
                buf = s_sb[j % 2]
                for b in range(j // 4, 4):
                    lo = max(t0, 512 * b)
                    hi = 512 * (b + 1)
                    pst, pn = [(ps_a, "ps_a"), (ps_b, "ps_b"),
                               (ps_d, "ps_d")][sc_i % 3]
                    sc_i += 1
                    smm = None
                    for sp, (kb, qb) in enumerate(
                            ((kh_sb, qh_sb), (kh_sb, ql_sb), (kl_sb, qh_sb))):
                        smm = mm(pst[:, 0:hi - lo],
                                 kb[qof:qof + 64, t0:t0 + 128],
                                 qb[qof:qof + 64, lo:hi], sp == 0, sp == 2,
                                 waits=[("VEC", QKV_V)] +
                                 (pw(pn) if sp == 0 else []),
                                 inc="TEN" if sp == 2 else None)
                    if lo == t0:
                        ev1 = vop(lambda e, buf=buf, lo=lo, pst=pst:
                                  e.tensor_mul(buf[:, lo:lo + 128],
                                               pst[:, 0:128], k_causal[:, :]),
                                  waits=[("TEN", smm)])
                        if hi > lo + 128:
                            ev1 = vop(lambda e, buf=buf, lo=lo, hi=hi, pst=pst:
                                      e.tensor_copy(buf[:, lo + 128:hi],
                                                    pst[:, 128:hi - lo]))
                    else:
                        ev1 = vop(lambda e, buf=buf, lo=lo, hi=hi, pst=pst:
                                  e.tensor_copy(buf[:, lo:hi],
                                                pst[:, 0:hi - lo]),
                                  waits=[("TEN", smm)])
                    pnote(pn, "VEC", ev1)
                    ab = sop(lambda e, buf=buf, lo=lo, hi=hi: e.activation(
                        sabs[:, lo:hi], buf[:, lo:hi], AF.Abs),
                        waits=[("VEC", ev1),
                               ("TEN", den_done.get(b, 0))])
                    den_q.append((j, b, lo, hi, ab))
                    sv_q.append((j, b, lo, hi, ev1, buf))
            flush_sv(NT)
            flush_den(NT)
            A_ = s_sb[1][0:4, 0:512]
            B_ = s_sb[1][0:4, 512:1024]
            Cs = s_sb[1][0:4, 1024:1536]
            phi = A_
            dv = vop(lambda e: e.tensor_copy(A_, ps_den[0:4, 0:512]),
                     waits=[("TEN", last_den)])
            pnote("den", "VEC", dv)
            ret2_sb = s_sb[0]
            for b in range(4):
                rev = vop(lambda e, b=b: e.tensor_copy(
                    ret_sb[:, 512 * b:512 * (b + 1)], ps_ret[b][:, 0:512]),
                    waits=[("TEN", sv_done[b])] +
                    (bw("ret_sb") if hl == 0 and b == 0 else []))
                if hl == 0 and b == 0:
                    bclr("ret_sb")
                pnote(f"ret{b}", "VEC", rev)
                sq = sop(lambda e, b=b, ret2_sb=ret2_sb: e.activation(
                    ret2_sb[:, 512 * b:512 * (b + 1)],
                    ret_sb[:, 512 * b:512 * (b + 1)], AF.Square),
                    waits=[("VEC", rev)] +
                    ([("TEN", last_sv)] if b == 0 else []))
                last_ms = mm(ps_den[0:36, 0:512],
                             k_ind8m[:, 36 * b:36 * b + 36],
                             ret2_sb[:, 512 * b:512 * (b + 1)],
                             b == 0, b == 3,
                             waits=[("SCA", sq)] +
                             ([("VEC", dv)] if b == 0 else []),
                             inc="TEN" if b == 3 else None)
            mv = vop(lambda e: e.tensor_copy(B_, ps_den[32:36, 0:512]),
                     waits=[("TEN", last_ms)])
            pnote("den", "VEC", mv)
            vop(lambda e: e.tensor_scalar(A_, A_, 1.0, 50000.0, ALU.max,
                                          ALU.min))
            vop(lambda e: e.tensor_mul(Cs, A_, A_))
            vop(lambda e: e.tensor_scalar(B_, B_, 1.0 / VD, None, ALU.mult))
            VE2 = vop(lambda e: e.scalar_tensor_tensor(
                Cs, Cs, EPS_RMS, B_, ALU.mult, ALU.add))
            SQ3 = sop(lambda e: e.activation(A_, Cs, AF.Sqrt),
                      waits=[("VEC", VE2)])
            vop(lambda e: e.reciprocal(B_, A_), waits=[("SCA", SQ3)])
            vop(lambda e: e.tensor_mul(A_, B_, B_))
            vop(lambda e: e.tensor_mul(A_, A_, Cs))
            vop(lambda e: e.tensor_scalar(A_, A_, -0.5, 1.5, ALU.mult,
                                          ALU.add))
            PHI_V = vop(lambda e: e.tensor_mul(A_, A_, B_))
            if h == 0:
                for nm, ap_ in (("phi0", phi), ("ret0", ret_sb[:, :])):
                    if nm in tap_d:
                        dma("gpsimd", tap_d[nm][:, :], ap_,
                            waits=[("VEC", PHI_V)])
            g_hi, g_lo, g_rn, gsv = gspl[hl]
            GP_PS = [(ps_d, "ps_d"), (ps_a, "ps_a"), (ps_b, "ps_b"),
                     (ps_den, "den")]
            for tb in range(4):
                gpst, gpn = GP_PS[tb]
                first = True
                for hw, hx in ((0, 0), (0, 1), (1, 0)):
                    wt_ = g_hi if hw == 0 else g_lo
                    xs_ = xnh_p if hx == 0 else xnl_p
                    for kc in range(NCC):
                        mm(gpst[:, 0:512], wt_[:, 128 * kc:128 * (kc + 1)],
                           xs_[kc][:, 512 * tb:512 * (tb + 1)],
                           first, (hw, hx, kc) == (1, 0, NCC - 1),
                           waits=[("VEC", gsv)] +
                           (pw(gpn) if first else []),
                           inc="TEN" if (hw, hx, kc) == (1, 0, NCC - 1)
                           else None)
                        first = False
                gmm = cnt["TEN"]
                bmark(g_rn, "TEN", gmm)
                ev = vop(lambda e, tb=tb, h=h, gpst=gpst: e.tensor_scalar(
                    g_sb[:, 512 * tb:512 * (tb + 1)], gpst[:, 0:512],
                    k_bg2[:, h:h + 1], None, ALU.add),
                    waits=[("TEN", gmm)] +
                    (bw("g_sb") if hl == 0 and tb == 0 else []))
                if hl == 0 and tb == 0:
                    bclr("g_sb")
                pnote(gpn, "VEC", ev)
            for b in range(4):
                gpst, gpn = GP_PS[b]
                pb = mm(gpst[:, 0:512], k_indT4[0:4, 128 * b:128 * (b + 1)],
                        phi, True, True,
                        waits=pw(gpn) + [("VEC", PHI_V), *CONSTW],
                        inc="TEN")
                gv = vop(lambda e, b=b, gpst=gpst: e.tensor_mul(
                    ret_sb[:, 512 * b:512 * (b + 1)],
                    ret_sb[:, 512 * b:512 * (b + 1)], gpst[:, 0:512]),
                    waits=[("TEN", pb)])
                pnote(gpn, "VEC", gv)
                vop(lambda e, b=b: e.tensor_mul(
                    ret_sb[:, 512 * b:512 * (b + 1)],
                    ret_sb[:, 512 * b:512 * (b + 1)],
                    g_sb[:, 512 * b:512 * (b + 1)]))
                vop(lambda e, hl=hl, b=b: e.tensor_copy(
                    gath[:, T * hl + 512 * b:T * hl + 512 * (b + 1)],
                    ret_sb[:, 512 * b:512 * (b + 1)]),
                    waits=bw("gat" + str(hl)) if b == 0 else [])
                vg = vop(lambda e, hl=hl, b=b: e.scalar_tensor_tensor(
                    gatl[:, T * hl + 512 * b:T * hl + 512 * (b + 1)],
                    gath[:, T * hl + 512 * b:T * hl + 512 * (b + 1)],
                    -1.0, ret_sb[:, 512 * b:512 * (b + 1)],
                    ALU.mult, ALU.add))
                gate_done[hl][b] = vg
            bclr("gat" + str(hl))
            if h == 0 and "gat0" in tap_d:
                dma("gpsimd", tap_d["gat0"][:, :], gath[:, 0:T],
                    waits=[("VEC", vg)])

        for ts in range(NT):
            for cb in range(2):
                pst = ps_a if cb == 0 else ps_b
                pn = "ps_a" if cb == 0 else "ps_b"
                first = True
                wmm = None
                gw = max(osv, gate_done[0][ts // 4], gate_done[1][ts // 4])
                for hg, hw in ((0, 0), (0, 1), (1, 0)):
                    gsrc = gath if hg == 0 else gatl
                    wsrc = woh if hw == 0 else wol
                    for hl in range(2):
                        last = (hg, hw, hl) == (1, 0, 1) and grp > 0
                        wmm = mm(
                            pst[:, 0:512],
                            gsrc[:, T * hl + 128 * ts:T * hl + 128 * (ts + 1)],
                            wsrc[:, C * hl + 512 * cb:C * hl + 512 * (cb + 1)],
                            first, last,
                            waits=[("VEC", gw)] + (pw(pn) if first else []),
                            inc="TEN" if last else None)
                        first = False
                if grp == 0:
                    wmm = mm(pst[:, 0:512], k_onesr[0:1, :],
                             k_bo[0:1, 512 * cb:512 * (cb + 1)], False, True,
                             waits=[*CONSTW], inc="TEN")
                bmark("wvh", "TEN", wmm)
                bmark("gat0", "TEN", wmm)
                bmark("gat1", "TEN", wmm)
                sl4 = ts % 4
                wbuf = ret_sb if sl4 < 2 else g_sb
                wrn = "ret_sb" if sl4 < 2 else "g_sb"
                off = 1024 * (sl4 % 2)
                ev = vop(lambda e, cb=cb, pst=pst, wbuf=wbuf, off=off:
                         e.tensor_copy(
                             wbuf[:, off + 512 * cb:off + 512 * (cb + 1)],
                             pst[:, 0:512]),
                         waits=[("TEN", wmm)] + bw(f"wo_dma{sl4}"))
                pnote(pn, "VEC", ev)
            bclr(f"wo_dma{sl4}")
            wod = dma("gpsimd", scrATT[128 * ts:128 * (ts + 1), :],
                      wbuf[:, off:off + C],
                      waits=[("VEC", cnt["VEC"])] + att_prev_dgp,
                      accum=(grp > 0))
            bmark(f"wo_dma{sl4}", wod[0], wod[1])
            bmark(wrn, wod[0], wod[1])
        att_prev_dgp = qall("gpsimd")
    ATT_DGP = qall("gpsimd")
    if "attn" in tap_d:
        for ts in range(NT):
            ldw = dma("scalar", ret_sb[:, 0:C],
                      scrATT[128 * ts:128 * (ts + 1), :],
                      waits=qall("gpsimd") + [("VEC", cnt["VEC"])])
            dma("gpsimd", tap_d["attn"][128 * ts:128 * (ts + 1), :],
                ret_sb[:, 0:C], waits=[ldw])

    S.barrier()
    attn.close()
    if stop_after == "p2":
        return finish()

    ffn = contextlib.ExitStack()
    fsb = mk_sb(ffn)
    h3h = fsb("h3h", (128, NCH * 512), BF16)
    h3l = fsb("h3l", (128, NCH * 512), BF16)
    h3f = fsb("h3f", (128, 3 * 512))
    we_t = fsb("we_t", (128, 4 * 2048), BF16)
    wc_t = fsb("wc_t", (128, 4 * 1024), BF16)
    x_res = fsb("x_res", (128, 2 * C))
    a_res = fsb("a_res", (128, 2 * C))
    out_st = fsb("out_st", (128, 2 * C))

    cb1_ps = [ps_a, ps_b, ps_c, ps_d]
    cb1_pn = ["ps_a", "ps_b", "ps_c", "ps_d"]
    for nm in ("we0", "we1", "wc0", "wc1"):
        bclr(nm)
    for tile in range(4):
        toff = 512 * tile
        for oc in range(NCH):
            slot = 2048 * (oc % 4)
            wl = dma("sync", we_t[:, slot:slot + 2048],
                     scrWE[oc].rearrange("hl p a q -> p hl a q"),
                     waits=WE_DGP + bw(f"we{oc % 4}"))
            bclr(f"we{oc % 4}")
            pst = ps_a if oc % 2 == 0 else ps_b
            pn = "ps_a" if oc % 2 == 0 else "ps_b"
            first = True
            for hl_w, hl_x in ((0, 0), (0, 1), (1, 0)):
                xsrc = xnh_p if hl_x == 0 else xnl_p
                for kc in range(NCC):
                    mm(pst[:, 0:512],
                       we_t[:, slot + 1024 * hl_w + 128 * kc:
                            slot + 1024 * hl_w + 128 * (kc + 1)],
                       xsrc[kc][:, toff:toff + 512],
                       first, (hl_w, hl_x, kc) == (1, 0, NCC - 1),
                       waits=[wl] + (pw(pn) if first else []),
                       inc="TEN" if (hl_w, hl_x, kc) == (1, 0, NCC - 1)
                       else None)
                    first = False
            emm = cnt["TEN"]
            bmark(f"we{oc % 4}", "TEN", emm)
            gl = sop(lambda e, oc=oc, pst=pst: e.activation(
                h3f[:, 512 * (oc % 3):512 * (oc % 3 + 1)], pst[:, 0:512],
                AF.Gelu, bias=bprime[:, oc:oc + 1], scale=1.0),
                waits=[("TEN", emm), ("VEC", BPRIME_V)] + bw(f"h3f{oc % 3}"))
            bclr(f"h3f{oc % 3}")
            pnote(pn, "SCA", gl)
            vop(lambda e, oc=oc: e.tensor_copy(
                h3h[:, 512 * oc:512 * (oc + 1)],
                h3f[:, 512 * (oc % 3):512 * (oc % 3 + 1)]),
                waits=[("SCA", gl)])
            hv = vop(lambda e, oc=oc: e.scalar_tensor_tensor(
                h3l[:, 512 * oc:512 * (oc + 1)],
                h3h[:, 512 * oc:512 * (oc + 1)], -1.0,
                h3f[:, 512 * (oc % 3):512 * (oc % 3 + 1)],
                ALU.mult, ALU.add))
            bmark(f"h3f{oc % 3}", "VEC", hv)
        H3_S = cnt["VEC"]
        if tile == 0 and "h3" in tap_d:
            pass
        def load_res(ts):
            rs = C * (ts % 2)
            dma("scalar", x_res[:, rs:rs + C],
                d["x"][toff + 128 * ts:toff + 128 * (ts + 1), :],
                waits=bw(f"res{ts % 2}"))
            al = dma("scalar", a_res[:, rs:rs + C],
                     scrATT[toff + 128 * ts:toff + 128 * (ts + 1), :],
                     waits=ATT_DGP + bw(f"res{ts % 2}"))
            bclr(f"res{ts % 2}")
            return al

        al_l = [load_res(0), load_res(1)]
        bmms = {}
        for cb in range(2):
            for c2 in range(NCH):
                slot = 1024 * (c2 % 4)
                srcb = scrWC.rearrange("(hi gg) hl q -> hi gg hl q", gg=128)
                srcb = srcb[:, 4 * c2:4 * c2 + 4, :, 512 * cb:512 * (cb + 1)]
                srcb = srcb.rearrange("hi g hl q -> g hi hl q")
                wlw = WC_DGP + bw(f"wc{c2 % 4}")
                wl0 = dma("sync", wc_t[:, slot:slot + 512], srcb[:, :, 0, :],
                          waits=wlw)
                wl = dma("sync", wc_t[:, slot + 512:slot + 1024],
                         srcb[:, :, 1, :], waits=wlw)
                bclr(f"wc{c2 % 4}")
                wm = None
                for hl_h, hl_w in ((0, 0), (0, 1), (1, 0)):
                    hsrc = h3h if hl_h == 0 else h3l
                    for ts in range(4):
                        pst = ps_ret[ts] if cb == 0 else cb1_ps[ts]
                        pn = f"ret{ts}" if cb == 0 else cb1_pn[ts]
                        wm = mm(pst[:, 0:512],
                                hsrc[:, 512 * c2 + 128 * ts:
                                     512 * c2 + 128 * (ts + 1)],
                                wc_t[:, slot + 512 * hl_w:
                                     slot + 512 * (hl_w + 1)],
                                c2 == 0 and (hl_h, hl_w) == (0, 0), False,
                                waits=[wl0, wl, ("VEC", H3_S)] +
                                (pw(pn) if c2 == 0 and
                                 (hl_h, hl_w) == (0, 0) else []),
                                inc="TEN" if (hl_h, hl_w, ts) == (1, 0, 3)
                                else None)
                bmark(f"wc{c2 % 4}", "TEN", wm)
            for ts in range(4):
                pst = ps_ret[ts] if cb == 0 else cb1_ps[ts]
                bmms[(cb, ts)] = mm(
                    pst[:, 0:512], k_onesr[0:1, :],
                    k_bfin[0:1, 512 * cb:512 * (cb + 1)], False, True,
                    waits=[("VEC", BFIN_V)], inc="TEN")
        for ts in range(4):
            slot = C * (ts % 2)
            for cb in range(2):
                pst = ps_ret[ts] if cb == 0 else cb1_ps[ts]
                pn = f"ret{ts}" if cb == 0 else cb1_pn[ts]
                rs = C * (ts % 2)
                ev = vop(lambda e, pst=pst, rs=rs, cb=cb, slot=slot:
                         e.scalar_tensor_tensor(
                             out_st[:, slot + 512 * cb:slot + 512 * (cb + 1)],
                             pst[:, 0:512], 1.0,
                             x_res[:, rs + 512 * cb:rs + 512 * (cb + 1)],
                             ALU.mult, ALU.add),
                         waits=[("TEN", bmms[(cb, ts)]), al_l[ts]]
                         + (bw(f"y{ts % 2}") if cb == 0 else []))
                pnote(pn, "VEC", ev)
                ev2 = vop(lambda e, rs=rs, cb=cb, slot=slot: e.tensor_add(
                    out_st[:, slot + 512 * cb:slot + 512 * (cb + 1)],
                    out_st[:, slot + 512 * cb:slot + 512 * (cb + 1)],
                    a_res[:, rs + 512 * cb:rs + 512 * (cb + 1)]))
                bmark(f"res{ts % 2}", "VEC", ev2)
            bclr(f"y{ts % 2}")
            yd = dma("gpsimd",
                     y[toff + 128 * ts:toff + 128 * (ts + 1), :],
                     out_st[:, slot:slot + C],
                     waits=[("VEC", cnt["VEC"])])
            bmark(f"y{ts % 2}", yd[0], yd[1])
            if ts + 2 < 4:
                al_l.append(load_res(ts + 2))

    ffn.close()
    return finish()



def kernel(**inputs):
    x_full = np.ascontiguousarray(np.asarray(inputs["L0_attn_x"], np.float32))
    B = x_full.shape[0]
    consts, meta = build_host_constants(inputs)
    nc = build_program(meta)
    base = {
        "Wqk": np.ascontiguousarray(np.asarray(inputs["L0_attn_W_qk"], np.float32)),
        "Wv": np.ascontiguousarray(np.asarray(inputs["L0_attn_W_v"], np.float32)),
        "Wg": np.ascontiguousarray(np.asarray(inputs["L0_attn_W_g"], np.float32)),
        "Wo": np.ascontiguousarray(np.asarray(inputs["L0_attn_W_o"], np.float32)),
        "Wexp": np.ascontiguousarray(np.asarray(inputs["L0_ffn_W_exp"], np.float32)),
        "Wcon": np.ascontiguousarray(np.asarray(inputs["L0_ffn_W_con"], np.float32)),
    }
    base.update({k: np.ascontiguousarray(v) for k, v in consts.items()})
    in_maps = []
    for b in range(B):
        m = dict(base)
        m["x"] = x_full[b]
        in_maps.append(m)
    res = run_bass_kernel_spmd(nc, in_maps, core_ids=list(range(B)))
    out = np.stack([np.asarray(r["y"]) for r in res.results], axis=0)
    return out.astype(np.float32)
